# version 03
# baseline (speedup 1.0000x reference)
import math
from contextlib import ExitStack

import numpy as np

import concourse.bass as bass
import concourse.mybir as mybir
from concourse.bass_utils import run_bass_kernel_spmd

F32 = mybir.dt.float32
BF16 = mybir.dt.bfloat16
I32 = mybir.dt.int32
AF = mybir.ActivationFunctionType
ALU = mybir.AluOpType
AXX = mybir.AxisListType.X

B, S, D = 4, 4096, 2048
TOWN = 2048
H, QLR, KVLR, DN, DR, DV = 16, 1536, 512, 128, 64, 128
DQ = DN + DR
SCALE = DQ ** (-0.5)
RMS_EPS = 1e-6
LN_EPS = 1e-5
E_EXP, I_FF, SH_FF = 8, 1408, 2816
KVH, HD = 4, 128
L1_EPS = 1e-6
NC_I = D // 128
NC_Q = QLR // 128
NC_KV = KVLR // 128
NJ = I_FF // 128
NJS = SH_FF // 128
SUMC = 256

LAUNCH_PLAN = [["A"], ["B"], ["C"], ["D"]]
A_DO_AG = True
A_DYN_OFF = True
A_STAGES = 3
A_A1B = True
A_CQS = True
A_NT = 8
A_RMS = True
A_SQ = True
A_TS = True
A_MM = True
A_COL = True


class Trk:
    __slots__ = ("w", "r", "excl")

    def __init__(self, excl=False):
        self.w = []
        self.r = []
        self.excl = excl


class Emit:
    ENGINES = ("sync", "gpsimd", "tensor", "vector", "scalar")
    QOF = {"sync": "qSY", "gpsimd": "qGP", "tensor": "eT", "vector": "eV",
           "scalar": "eS"}

    def __init__(self, nc, stack):
        self.nc = nc
        names = ("qSY", "qGP", "eT", "eV", "eS", "cc")
        self.sems = {q: stack.enter_context(nc.semaphore(q)) for q in names}
        self.count = {q: 0 for q in names}
        self.ops = {e: [] for e in self.ENGINES}
        self.satisfied = {e: {} for e in self.ENGINES}
        self.joins = {}

    def join(self):
        toks = self.all_tokens()
        for e in self.ENGINES:
            self.joins[e] = toks

    def _wait(self, engine, deps):
        deps = list(deps) + self.joins.pop(engine, [])
        need = {}
        for d in deps:
            if d is None:
                continue
            q, v = d
            if v > need.get(q, 0):
                need[q] = v
        for q, v in sorted(need.items()):
            if self.satisfied[engine].get(q, 0) >= v:
                continue
            self.satisfied[engine][q] = v
            sem = self.sems[q]
            self.ops[engine].append(lambda eng, sem=sem, v=v: eng.wait_ge(sem, v))

    def raw_op(self, engine, fn, deps=(), inc=None, q=None):
        q = q or self.QOF[engine]
        if inc is None:
            inc = 16 if q in ("qSY", "qGP") else 1
        if q in ("qSY", "qGP", "cc") and self.count[q] > 0:
            deps = list(deps) + [(q, self.count[q])]
        self._wait(engine, deps)
        self.count[q] += inc
        tok = (q, self.count[q])
        sem = self.sems[q]
        self.ops[engine].append(
            lambda eng, fn=fn, sem=sem, inc=inc: fn(eng).then_inc(sem, inc))
        return tok

    def raw_noinc(self, engine, fn, deps=()):
        self._wait(engine, deps)
        self.ops[engine].append(fn)

    def all_tokens(self):
        return [(q, v) for q, v in self.count.items() if v > 0]

    def flush(self, block):
        for e in self.ENGINES:
            lst = self.ops[e]
            if not lst:
                continue

            def body(eng, lst=lst):
                for f in lst:
                    f(eng)
            getattr(block, e)(body)


def _deps(reads, writes, extra):
    deps = list(extra)
    for t in reads:
        deps += t.w
        if t.excl:
            deps += t.r
    for t in writes:
        deps += t.w + t.r
    return deps


def _commit(tok, reads, writes):
    for t in reads:
        t.r.append(tok)
    for t in writes:
        t.w = [tok]
        t.r = []
    return tok


def op(E, engine, fn, reads=(), writes=(), extra=(), inc=None, q=None):
    tok = E.raw_op(engine, fn, deps=_deps(reads, writes, extra), inc=inc, q=q)
    return _commit(tok, reads, writes)


def mmg(E, steps, reads=(), writes=(), extra=()):
    deps = _deps(reads, writes, extra)
    for fn in steps[:-1]:
        E.raw_noinc("tensor", fn, deps=deps)
        deps = ()
    tok = E.raw_op("tensor", steps[-1], deps=deps)
    return _commit(tok, reads, writes)


def mk_ln(nc, pstack, tag):
    s = {nm: pstack.enter_context(nc.sbuf_tensor(f"ln{tag}_{nm}", [128, 1], F32))
         for nm in ("sum", "ssq", "mu", "var", "rstd", "nbias", "t1")}
    s["dump"] = pstack.enter_context(
        nc.sbuf_tensor(f"ln{tag}_dump", [128, D], BF16))
    s["trk"] = Trk()
    return s


def ln_apply(E, sc, x_ap, x_trk, out_ap, out_trk, P=None):
    t = sc["trk"]
    op(E, "scalar", lambda s_: s_.activation(sc["dump"][:], x_ap, AF.Copy,
                                             bias=0.0, scale=1.0,
                                             accum_out=sc["sum"][:]),
       reads=[x_trk], writes=[t])
    op(E, "scalar", lambda s_: s_.activation(sc["dump"][:], x_ap, AF.Square,
                                             bias=0.0, scale=1.0,
                                             accum_out=sc["ssq"][:]),
       reads=[x_trk], writes=[t])
    op(E, "vector", lambda v: v.tensor_scalar_mul(sc["mu"][:], sc["sum"][:],
                                                  1.0 / D), reads=[t], writes=[t])
    op(E, "vector", lambda v: v.tensor_mul(sc["t1"][:], sc["mu"][:], sc["mu"][:]),
       reads=[t], writes=[t])
    op(E, "vector", lambda v: v.scalar_tensor_tensor(
        sc["var"][:], sc["ssq"][:], 1.0 / D, sc["t1"][:], ALU.mult, ALU.subtract),
        reads=[t], writes=[t])
    op(E, "scalar", lambda s_: s_.activation(sc["t1"][:], sc["var"][:], AF.Sqrt,
                                             bias=P["eps_ln"][:], scale=1.0),
       reads=[t, P["t_eps"]], writes=[t])
    op(E, "vector", lambda v: v.reciprocal(sc["rstd"][:], sc["t1"][:]),
       reads=[t], writes=[t])
    op(E, "vector", lambda v: v.scalar_tensor_tensor(
        sc["nbias"][:], sc["mu"][:], -1.0, sc["rstd"][:], ALU.mult, ALU.mult),
        reads=[t], writes=[t])
    return op(E, "vector", lambda v: v.tensor_scalar(
        out_ap, x_ap, sc["rstd"][:], sc["nbias"][:], ALU.mult, ALU.add),
        reads=[x_trk, t], writes=[out_trk])


def build_program(phases):
    nc = bass.Bass("TRN2", target_bir_lowering=False)
    io = {}

    def dram(name, shape, dtype, kind):
        io[name] = nc.dram_tensor(name, shape, dtype, kind=kind)
        return io[name]

    def stream_kind(prod, cons):
        if prod in phases and cons in phases:
            return "Internal"
        if prod in phases:
            return "ExternalOutput"
        return "ExternalInput"

    if "A" in phases:
        dram("xo_tm", [TOWN, D], F32, "ExternalInput")
        dram("xa_fm", [D, S], F32, "ExternalInput")
        dram("wqaT", [D, QLR], F32, "ExternalInput")
        dram("wqbT_own", [QLR, 8 * DQ], F32, "ExternalInput")
        dram("wkvaT", [D, KVLR + DR], F32, "ExternalInput")
        dram("wkvbT_own", [KVLR, 8 * (DN + DV)], F32, "ExternalInput")
        dram("woT", [D, D], F32, "ExternalInput")
        dram("qaln", [128, NC_Q], F32, "ExternalInput")
        dram("kvln", [128, NC_KV], F32, "ExternalInput")
        dram("cos0", [DR, S], F32, "ExternalInput")
        dram("sin0", [DR, S], F32, "ExternalInput")
        dram("ownoff", [1, 1], I32, "ExternalInput")
        dram("cqs", [NC_Q, 128, S], BF16, "Internal")
        dram("attnin", [8 * 128, S], BF16, "Internal")
        dram("attnout", [4, 512, S], BF16, "Internal")
    if "B" in phases:
        dram("gwT", [D, E_EXP], F32, "ExternalInput")
        dram("w1", [E_EXP, D, I_FF], F32, "ExternalInput")
        dram("w3", [E_EXP, D, I_FF], F32, "ExternalInput")
        dram("w2", [E_EXP, I_FF, D], F32, "ExternalInput")
        dram("sw1", [D, SH_FF], F32, "ExternalInput")
        dram("sw3", [D, SH_FF], F32, "ExternalInput")
        dram("sw2", [SH_FF, D], F32, "ExternalInput")
    if "C" in phases:
        dram("wkT1", [D, KVH * HD], F32, "ExternalInput")
        dram("wvT1", [D, KVH * HD], F32, "ExternalInput")
        dram("wqT1", [D, D], F32, "ExternalInput")
        dram("woT1", [D, D], F32, "ExternalInput")
        dram("cos1", [HD, TOWN], F32, "ExternalInput")
        dram("sin1", [HD, TOWN], F32, "ExternalInput")
        dram("smask", [128, 2, 16], F32, "ExternalInput")
        dram("summin", [2 * KVH * 128, SUMC], F32, "Internal")
        dram("summall", [8 * 2 * KVH * 128, SUMC], F32, "Internal")
    if "D" in phases:
        dram("l1wT", [D, D], F32, "ExternalInput")
        dram("l1scale", [1, D], F32, "ExternalInput")
        dram("l1bias", [1, D], F32, "ExternalInput")
    dram("ltri", [128, 128], F32, "ExternalInput")
    dram("ident", [128, 128], F32, "ExternalInput")

    if "A" in phases or "B" in phases:
        dram("x1s", [TOWN, D], F32, stream_kind("A", "B"))
    if "B" in phases or "C" in phases:
        dram("x2s", [TOWN, D], F32, stream_kind("B", "C"))
    if "C" in phases or "D" in phases:
        dram("x3s", [TOWN, D], F32, stream_kind("C", "D"))
    if "D" in phases:
        dram("y", [TOWN, D], F32, "ExternalOutput")

    tk = {name: Trk() for name in io}

    with ExitStack() as stack:
        E = Emit(nc, stack)
        ltri_bf = stack.enter_context(nc.sbuf_tensor("ltri_bf", [128, 128], BF16))
        ident_bf = stack.enter_context(nc.sbuf_tensor("ident_bf", [128, 128], BF16))
        ones_col = stack.enter_context(nc.sbuf_tensor("ones_col", [128, 1], BF16))
        ones_r32 = stack.enter_context(nc.sbuf_tensor("ones_r32", [1, 128], F32))
        ones_r65 = stack.enter_context(nc.sbuf_tensor("ones_r65", [65, 128], F32))
        eps_ln = stack.enter_context(nc.sbuf_tensor("eps_ln", [128, 1], F32))
        eps_rms = stack.enter_context(nc.sbuf_tensor("eps_rms", [128, 1], F32))
        P = {"ltri": ltri_bf, "ident": ident_bf, "ones_col": ones_col,
             "ones_r32": ones_r32, "ones_r65": ones_r65,
             "eps_ln": eps_ln, "eps_rms": eps_rms,
             "t_ltri": Trk(), "t_ident": Trk(),
             "t_oc": Trk(), "t_or": Trk(), "t_eps": Trk()}
        op(E, "gpsimd", lambda g: g.dma_start(out=ltri_bf[:], in_=io["ltri"][:, :]),
           writes=[P["t_ltri"]])
        op(E, "gpsimd", lambda g: g.dma_start(out=ident_bf[:], in_=io["ident"][:, :]),
           writes=[P["t_ident"]])
        op(E, "vector", lambda v: v.memset(ones_col[:], 1.0), writes=[P["t_oc"]])
        op(E, "vector", lambda v: v.memset(ones_r32[:], 1.0), writes=[P["t_or"]])
        op(E, "vector", lambda v: v.memset(ones_r65[:], 1.0), writes=[P["t_or"]])
        op(E, "vector", lambda v: v.memset(eps_ln[:], LN_EPS), writes=[P["t_eps"]])
        op(E, "vector", lambda v: v.memset(eps_rms[:], RMS_EPS), writes=[P["t_eps"]])

        for ph in phases:
            with ExitStack() as pstack:
                if ph == "A":
                    phase_A(E, nc, io, tk, P, pstack)
                elif ph == "B":
                    phase_B(E, nc, io, tk, P, pstack)
                elif ph == "C":
                    phase_C(E, nc, io, tk, P, pstack)
                elif ph == "D":
                    phase_D(E, nc, io, tk, P, pstack)
            E.join()

        E._wait("sync", E.all_tokens())
        with nc.Block() as block:
            E.flush(block)
    return nc


def phase_A(E, nc, io, tk, P, ap):
    ecm = ap.enter_context
    pb = [ecm(nc.psum_tensor(f"psA{i}", [128, 512], F32)) for i in range(8)]
    tpb = [Trk(True) for _ in range(8)]

    with ExitStack() as a12:
        e1 = a12.enter_context
        trig0 = e1(nc.sbuf_tensor("trig0", [128, S], F32))
        ckvg = e1(nc.sbuf_tensor("ckvg", [128, NC_KV, S], BF16))
        kpe = e1(nc.sbuf_tensor("kpe", [DR + 1, S], BF16))
        rstdq = e1(nc.sbuf_tensor("rstdq", [1, S], F32))
        rstdkv = e1(nc.sbuf_tensor("rstdkv", [1, S], F32))
        kpen2 = e1(nc.sbuf_tensor("kpen2", [1, S], F32))
        rstdkvT = e1(nc.sbuf_tensor("rstdkvT", [128, S // 128], F32))
        t_cos = Trk(); t_sin = Trk(); t_ckvg = Trk(); t_kpe = Trk()
        t_kpen2 = Trk(); t_rq = Trk(); t_rkv = Trk(); t_rkvT = Trk()

        op(E, "sync", lambda sy: sy.dma_start(out=trig0[0:DR, :],
                                              in_=io["cos0"][:, :]),
           writes=[t_cos])
        op(E, "sync", lambda sy: sy.dma_start(out=trig0[DR:2 * DR, :],
                                              in_=io["sin0"][:, :]),
           writes=[t_sin])
        op(E, "vector", lambda v: v.memset(kpe[DR:DR + 1, :], 1.0),
           writes=[t_kpe])
        t_cqg = Trk()
        with ExitStack() as a1:
            ee = a1.enter_context
            wqa_sb = ee(nc.sbuf_tensor("wqa_sb", [128, NC_I, QLR], BF16))
            qaln_sb = ee(nc.sbuf_tensor("qaln_sb", [128, NC_Q], F32))
            xa = ee(nc.sbuf_tensor("xaf", [128, NC_I, 512], BF16))
            sqb = [ee(nc.sbuf_tensor(f"sqb{i}", [128, 512], BF16))
                   for i in range(2)]
            cqstage = [ee(nc.sbuf_tensor(f"cqstage{i}", [128, 512], BF16))
                       for i in range(2)]
            rowt = ee(nc.sbuf_tensor("rowt", [1, 512], F32))
            t_wqa = Trk(); t_qaln = Trk(); t_xa = Trk()
            t_sqb = [Trk(), Trk()]; t_cqst = [Trk(), Trk()]; t_rowt = Trk()

            op(E, "gpsimd", lambda g: g.dma_start(
                out=wqa_sb[:],
                in_=io["wqaT"].ap().rearrange("(c p) f -> p c f", p=128)),
               writes=[t_wqa])
            op(E, "sync", lambda sy: sy.dma_start(out=qaln_sb[:],
                                                  in_=io["qaln"][:, :]),
               writes=[t_qaln])
            for it in range(A_NT):
                cols = slice(it * 512, (it + 1) * 512)
                op(E, "gpsimd", lambda g, cols=cols: g.dma_start(
                    out=xa[:],
                    in_=io["xa_fm"].ap()[:, cols]
                        .rearrange("(c p) t -> p c t", p=128)),
                   writes=[t_xa])
                for cc in range(A_MM if isinstance(A_MM, int) and not isinstance(A_MM, bool) else (NC_Q if A_MM else 0)):
                    bk, tb = pb[cc % 4], tpb[cc % 4]
                    mmg(E, [
                        (lambda t, ic=ic, cc=cc, bk=bk:
                         t.matmul(bk[:], wqa_sb[:, ic, cc * 128:(cc + 1) * 128],
                                  xa[:, ic, :], start=(ic == 0),
                                  stop=(ic == NC_I - 1)))
                        for ic in range(NC_I)],
                        reads=[t_wqa, t_xa], writes=[tb])
                    sb, tsb = sqb[cc % 2], t_sqb[cc % 2]
                    if A_SQ:
                        op(E, "scalar", lambda s_, bk=bk, sb=sb: s_.activation(
                            sb[:], bk[:], AF.Square, bias=0.0, scale=1.0),
                           reads=[tb], writes=[tsb])
                    if A_RMS:
                        mmg(E, [(lambda t, sb=sb, cc=cc: t.matmul(
                            pb[6][0:1, :], P["ones_col"][:], sb[:],
                            start=(cc == 0), stop=(cc == NC_Q - 1),
                            skip_group_check=True))],
                            reads=[tsb, P["t_oc"]], writes=[tpb[6]])
                    cst, t_cst = cqstage[cc % 2], t_cqst[cc % 2]
                    if A_TS:
                        op(E, "vector", lambda v, bk=bk, cst=cst, cc=cc:
                           v.tensor_scalar(cst[:], bk[:],
                                           qaln_sb[:, cc:cc + 1], None,
                                           ALU.mult),
                           reads=[tb, t_qaln], writes=[t_cst])
                    if A_CQS:
                        op(E, "sync", lambda sy, cc=cc, cols=cols, cst=cst:
                           sy.dma_start(out=io["cqs"].ap()[cc, :, cols],
                                        in_=cst[:]),
                           reads=[t_cst], writes=[tk["cqs"]])
                if A_RMS:
                    op(E, "scalar", lambda s_: s_.activation(
                        rowt[:], pb[6][0:1, :], AF.Sqrt,
                        bias=P["eps_rms"][0:1, :], scale=1.0 / QLR),
                       reads=[tpb[6], P["t_eps"]], writes=[t_rowt])
                    op(E, "vector", lambda v, cols=cols: v.reciprocal(
                        rstdq[0:1, cols], rowt[:]), reads=[t_rowt], writes=[t_rq])

        E.join()
        with ExitStack() as a1:
            ee = a1.enter_context
            wkva_sb = ee(nc.sbuf_tensor("wkva_sb", [128, NC_I, KVLR + DR], BF16))
            kvln_sb = ee(nc.sbuf_tensor("kvln_sb", [128, NC_KV], F32))
            xb = ee(nc.sbuf_tensor("xaf2", [128, NC_I, 512], BF16))
            sqb = [ee(nc.sbuf_tensor(f"sqd{i}", [128, 512], BF16))
                   for i in range(2)]
            kro1 = ee(nc.sbuf_tensor("kro1", [DR, 512], F32))
            kro2 = ee(nc.sbuf_tensor("kro2", [DR, 512], F32))
            rowtv = ee(nc.sbuf_tensor("rowtb", [1, 512], F32))
            t_wkva = Trk(); t_kvln = Trk(); t_xb = Trk()
            t_sqb = [Trk(), Trk()]; t_kro1 = Trk(); t_kro2 = Trk(); t_rowtv = Trk()

            op(E, "gpsimd", lambda g: g.dma_start(
                out=wkva_sb[:],
                in_=io["wkvaT"].ap().rearrange("(c p) f -> p c f", p=128)),
               writes=[t_wkva])
            op(E, "sync", lambda sy: sy.dma_start(out=kvln_sb[:],
                                                  in_=io["kvln"][:, :]),
               writes=[t_kvln])
            for it in range(S // 512 if A_A1B else 0):
                cols = slice(it * 512, (it + 1) * 512)
                op(E, "gpsimd", lambda g, cols=cols: g.dma_start(
                    out=xb[:],
                    in_=io["xa_fm"].ap()[:, cols]
                        .rearrange("(c p) t -> p c t", p=128)),
                   writes=[t_xb])
                for cc in range(NC_KV):
                    bk, tb = pb[cc], tpb[cc]
                    mmg(E, [
                        (lambda t, ic=ic, cc=cc, bk=bk:
                         t.matmul(bk[:], wkva_sb[:, ic, cc * 128:(cc + 1) * 128],
                                  xb[:, ic, :], start=(ic == 0),
                                  stop=(ic == NC_I - 1)))
                        for ic in range(NC_I)],
                        reads=[t_wkva, t_xb], writes=[tb])
                    sb, tsb = sqb[cc % 2], t_sqb[cc % 2]
                    op(E, "scalar", lambda s_, bk=bk, sb=sb: s_.activation(
                        sb[:], bk[:], AF.Square, bias=0.0, scale=1.0),
                       reads=[tb], writes=[tsb])
                    mmg(E, [(lambda t, sb=sb, cc=cc: t.matmul(
                        pb[6][0:1, :], P["ones_col"][:], sb[:],
                        start=(cc == 0), stop=(cc == NC_KV - 1),
                        skip_group_check=True))],
                        reads=[tsb, P["t_oc"]], writes=[tpb[6]])
                    op(E, "vector", lambda v, bk=bk, cc=cc, cols=cols:
                       v.tensor_scalar(ckvg[:, cc, cols], bk[:],
                                       kvln_sb[:, cc:cc + 1], None, ALU.mult),
                       reads=[tb, t_kvln], writes=[t_ckvg])
                op(E, "scalar", lambda s_: s_.activation(
                    rowtv[:], pb[6][0:1, :], AF.Sqrt, bias=P["eps_rms"][0:1, :],
                    scale=1.0 / KVLR),
                   reads=[tpb[6], P["t_eps"]], writes=[t_rowtv])
                op(E, "vector", lambda v, cols=cols: v.reciprocal(
                    rstdkv[0:1, cols], rowtv[:]), reads=[t_rowtv], writes=[t_rkv])
                mmg(E, [
                    (lambda t, ic=ic:
                     t.matmul(pb[7][0:DR, :], wkva_sb[:, ic, KVLR:KVLR + DR],
                              xb[:, ic, :], start=(ic == 0),
                              stop=(ic == NC_I - 1)))
                    for ic in range(NC_I)],
                    reads=[t_wkva, t_xb], writes=[tpb[7]])
                op(E, "vector", lambda v, cols=cols: v.tensor_mul(
                    kro1[:], pb[7][0:DR, :], trig0[0:DR, cols]),
                   reads=[tpb[7], t_cos], writes=[t_kro1])
                op(E, "vector", lambda v, cols=cols: v.scalar_tensor_tensor(
                    kro2[0:32, :], pb[7][32:64, :], -1.0, trig0[DR:DR + 32, cols],
                    ALU.mult, ALU.mult), reads=[tpb[7], t_sin], writes=[t_kro2])
                op(E, "vector", lambda v, cols=cols: v.scalar_tensor_tensor(
                    kro2[32:64, :], pb[7][0:32, :], 1.0, trig0[DR + 32:2 * DR, cols],
                    ALU.mult, ALU.mult), reads=[tpb[7], t_sin], writes=[t_kro2])
                op(E, "vector", lambda v: v.tensor_add(kro1[:], kro1[:],
                                                       kro2[:]),
                   reads=[t_kro2], writes=[t_kro1])
                op(E, "vector", lambda v, cols=cols: v.tensor_copy(
                    kpe[0:DR, cols], kro1[:]), reads=[t_kro1], writes=[t_kpe])
                sb, tsb = sqb[0], t_sqb[0]
                op(E, "scalar", lambda s_, sb=sb: s_.activation(
                    sb[0:DR, :], kro1[:], AF.Square, bias=0.0, scale=1.0),
                   reads=[t_kro1], writes=[tsb])
                mmg(E, [(lambda t, sb=sb: t.matmul(
                    pb[6][0:1, :], P["ones_col"][0:DR, :], sb[0:DR, :],
                    start=True, stop=True))],
                    reads=[tsb, P["t_oc"]], writes=[tpb[6]])
                op(E, "vector", lambda v, cols=cols: v.tensor_copy(
                    kpen2[0:1, cols], pb[6][0:1, :]), reads=[tpb[6]],
                   writes=[t_kpen2])

            for tc in range(S // 128 if A_COL else 0):
                mmg(E, [(lambda t, tc=tc: t.matmul(
                    pb[5][:, 0:1], rstdkv[0:1, tc * 128:(tc + 1) * 128],
                    P["ones_r32"][0:1, 0:1], start=True, stop=True))],
                    reads=[t_rkv, P["t_or"]], writes=[tpb[5]])
                op(E, "vector", lambda v, tc=tc: v.tensor_copy(
                    rstdkvT[:, tc:tc + 1], pb[5][:, 0:1]),
                   reads=[tpb[5]], writes=[t_rkvT])

        E.join()
        with ExitStack() as a2:
            e2 = a2.enter_context
            knope = e2(nc.sbuf_tensor("knope", [128, S], BF16))
            vaug = e2(nc.sbuf_tensor("vaug", [128, S // 128, DV + 1], BF16))
            q1 = e2(nc.sbuf_tensor("q1", [128, 512], BF16))
            q2 = e2(nc.sbuf_tensor("q2", [DR + 1, 512], BF16))
            maxk = e2(nc.sbuf_tensor("maxk", [1, 2], F32))
            kn2t = e2(nc.sbuf_tensor("kn2t", [1, 512], F32))
            qnt = e2(nc.sbuf_tensor("qnt", [1, 512], F32))
            mk1 = e2(nc.sbuf_tensor("mk1", [1, 1], F32))
            cqt = [e2(nc.sbuf_tensor("cqt0", [128, NC_Q, 512], BF16))]
            t_cqt = [Trk()]
            wqb_sl = e2(nc.sbuf_tensor("wqb_sl", [128, NC_Q, DQ], BF16))
            wkvb_sl = e2(nc.sbuf_tensor("wkvb_sl", [128, NC_KV, DN + DV], BF16))
            pt = [e2(nc.sbuf_tensor(f"pt{i}", [128, 512], BF16)) for i in range(3)]
            sqb = [e2(nc.sbuf_tensor(f"sqc{i}", [128, 512], BF16)) for i in range(2)]
            rb = e2(nc.sbuf_tensor("rb", [128, 512], F32))
            qro1 = e2(nc.sbuf_tensor("qro1", [DR, 512], F32))
            qro2 = e2(nc.sbuf_tensor("qro2", [DR, 512], F32))
            oacc = e2(nc.sbuf_tensor("oacc", [128, 512], F32))
            lacc = e2(nc.sbuf_tensor("lacc", [1, 512], F32))
            m_old = e2(nc.sbuf_tensor("m_old", [1, 512], F32))
            m_new = e2(nc.sbuf_tensor("m_new", [1, 512], F32))
            arow = e2(nc.sbuf_tensor("arow", [1, 512], F32))
            rlrow = e2(nc.sbuf_tensor("rlrow", [1, 512], F32))
            rlb = e2(nc.sbuf_tensor("rlb", [128, 512], F32))
            attn_t = e2(nc.sbuf_tensor("attn_t", [128, 512], BF16))
            rowt2 = e2(nc.sbuf_tensor("rowt2", [1, 512], F32))
            t_knope = Trk(); t_vaug = Trk(); t_q1 = Trk(); t_q2 = Trk()
            t_kn2t = Trk(); t_qnt = Trk(); t_mk1 = Trk()
            t_maxk = Trk()
            t_wqb = Trk(); t_wkvb = Trk()
            t_pt = [Trk() for _ in range(3)]
            t_sqb = [Trk(), Trk()]
            t_rb = Trk(); t_qro1 = Trk(); t_qro2 = Trk()
            t_oacc = Trk(); t_lacc = Trk(); t_mo = Trk(); t_mn = Trk()
            t_ar = Trk(); t_rl = Trk(); t_rlb = Trk(); t_at = Trk()
            t_rowt2 = Trk()

            op(E, "vector", lambda v: v.memset(vaug.ap()[:, :, DV:DV + 1], 1.0),
               writes=[t_vaug])

            for hl in range(8 if A_STAGES >= 2 else 0):
                op(E, "gpsimd", lambda g, hl=hl: g.dma_start(
                    out=wqb_sl[:],
                    in_=io["wqbT_own"].ap()[:, hl * DQ:(hl + 1) * DQ]
                        .rearrange("(c p) f -> p c f", p=128)),
                   writes=[t_wqb])
                op(E, "gpsimd", lambda g, hl=hl: g.dma_start(
                    out=wkvb_sl[:],
                    in_=io["wkvbT_own"].ap()[:, hl * 256:(hl + 1) * 256]
                        .rearrange("(c p) f -> p c f", p=128)),
                   writes=[t_wkvb])
                for it in range(S // 512):
                    cols = slice(it * 512, (it + 1) * 512)
                    bk, tb = pb[it % 2], tpb[it % 2]
                    mmg(E, [
                        (lambda t, cc=cc, bk=bk, cols=cols:
                         t.matmul(bk[:], wkvb_sl[:, cc, 0:DN], ckvg[:, cc, cols],
                                  start=(cc == 0), stop=(cc == NC_KV - 1)))
                        for cc in range(NC_KV)],
                        reads=[t_wkvb, t_ckvg], writes=[tb])
                    sb, tsb = sqb[it % 2], t_sqb[it % 2]
                    op(E, "scalar", lambda s_, bk=bk, sb=sb: s_.activation(
                        sb[:], bk[:], AF.Square, bias=0.0, scale=1.0),
                       reads=[tb], writes=[tsb])
                    mmg(E, [(lambda t, sb=sb: t.matmul(
                        pb[6][0:1, :], P["ones_col"][:], sb[:],
                        start=True, stop=True))],
                        reads=[tsb, P["t_oc"]], writes=[tpb[6]])
                    op(E, "vector", lambda v, cols=cols: v.tensor_mul(
                        rowt2[:], rstdkv[0:1, cols], rstdkv[0:1, cols]),
                       reads=[t_rkv], writes=[t_rowt2])
                    op(E, "vector", lambda v: v.tensor_mul(
                        rowt2[:], rowt2[:], pb[6][0:1, :]),
                       reads=[tpb[6]], writes=[t_rowt2])
                    op(E, "vector", lambda v, cols=cols: v.tensor_add(
                        kn2t[:], rowt2[:], kpen2[0:1, cols]),
                       reads=[t_rowt2, t_kpen2], writes=[t_kn2t])
                    op(E, "vector", lambda v: v.reduce_max(
                        mk1[:], kn2t[:], axis=AXX),
                       reads=[t_kn2t], writes=[t_mk1])
                    half = it // 4
                    if it % 4 == 0:
                        op(E, "vector", lambda v, half=half: v.tensor_copy(
                            maxk[0:1, half:half + 1], mk1[:]),
                           reads=[t_mk1], writes=[t_maxk])
                    else:
                        op(E, "vector", lambda v, half=half: v.tensor_max(
                            maxk[0:1, half:half + 1], maxk[0:1, half:half + 1],
                            mk1[:]),
                           reads=[t_mk1], writes=[t_maxk])
                    mmg(E, [(lambda t, cols=cols: t.matmul(
                        pb[7][:, :], P["ones_r32"][:], rstdkv[0:1, cols],
                        start=True, stop=True))],
                        reads=[t_rkv, P["t_or"]], writes=[tpb[7]])
                    op(E, "vector", lambda v: v.tensor_copy(rb[:], pb[7][:, :]),
                       reads=[tpb[7]], writes=[t_rb])
                    op(E, "vector", lambda v, bk=bk, cols=cols: v.tensor_mul(
                        knope[:, cols], bk[:], rb[:]),
                       reads=[tb, t_rb], writes=[t_knope])
                op(E, "scalar", lambda s_: s_.activation(
                    maxk[:], maxk[:], AF.Sqrt, bias=0.0, scale=1.0),
                   reads=[t_maxk], writes=[t_maxk])
                for tc in range(S // 128):
                    bk, tb = pb[2 + (tc % 2)], tpb[2 + (tc % 2)]
                    mmg(E, [
                        (lambda t, cc=cc, bk=bk, tc=tc:
                         t.matmul(bk[:, 0:DV],
                                  ckvg[:, cc, tc * 128:(tc + 1) * 128],
                                  wkvb_sl[:, cc, DN:DN + DV],
                                  start=(cc == 0), stop=(cc == NC_KV - 1)))
                        for cc in range(NC_KV)],
                        reads=[t_wkvb, t_ckvg], writes=[tb])
                    op(E, "vector", lambda v, bk=bk, tc=tc: v.tensor_scalar(
                        vaug[:, tc, 0:DV], bk[:, 0:DV], rstdkvT[:, tc:tc + 1],
                        None, ALU.mult),
                       reads=[tb, t_rkvT], writes=[t_vaug])
                for mi in range(8):
                    g0 = mi * 512
                    cols = slice(g0, g0 + 512)
                    cqx, t_cqx = cqt[0], t_cqt[0]
                    op(E, "gpsimd", lambda g, cols=cols, cqx=cqx: g.dma_start(
                        out=cqx[:], in_=io["cqs"].ap()[:, :, cols]
                        .rearrange("c p t -> p c t")),
                       reads=[tk["cqs"]], writes=[t_cqx])
                    mmg(E, [(lambda t, cols=cols: t.matmul(
                        pb[7][:, :], P["ones_r32"][:], rstdq[0:1, cols],
                        start=True, stop=True))],
                        reads=[t_rq, P["t_or"]], writes=[tpb[7]])
                    op(E, "vector", lambda v: v.tensor_copy(rb[:], pb[7][:, :]),
                       reads=[tpb[7]], writes=[t_rb])
                    bk, tb = pb[0], tpb[0]
                    mmg(E, [
                        (lambda t, cc=cc, bk=bk, cqx=cqx:
                         t.matmul(bk[:], wqb_sl[:, cc, 0:DN], cqx[:, cc, :],
                                  start=(cc == 0), stop=(cc == NC_Q - 1)))
                        for cc in range(NC_Q)],
                        reads=[t_wqb, t_cqx], writes=[tb])
                    op(E, "vector", lambda v, bk=bk, cols=cols:
                       v.scalar_tensor_tensor(q1[:], bk[:], SCALE, rb[:],
                                              ALU.mult, ALU.mult),
                       reads=[tb, t_rb], writes=[t_q1])
                    bk2, tb2 = pb[1], tpb[1]
                    mmg(E, [
                        (lambda t, cc=cc, bk2=bk2, cqx=cqx:
                         t.matmul(bk2[0:DR, :], wqb_sl[:, cc, DN:DQ],
                                  cqx[:, cc, :],
                                  start=(cc == 0), stop=(cc == NC_Q - 1)))
                        for cc in range(NC_Q)],
                        reads=[t_wqb, t_cqx], writes=[tb2])
                    op(E, "vector", lambda v, bk2=bk2, cols=cols: v.tensor_mul(
                        qro1[:], bk2[0:DR, :], trig0[0:DR, cols]),
                       reads=[tb2, t_cos], writes=[t_qro1])
                    op(E, "vector", lambda v, bk2=bk2, cols=cols:
                       v.scalar_tensor_tensor(qro2[0:32, :], bk2[32:64, :], -1.0,
                                              trig0[DR:DR + 32, cols], ALU.mult, ALU.mult),
                       reads=[tb2, t_sin], writes=[t_qro2])
                    op(E, "vector", lambda v, bk2=bk2, cols=cols:
                       v.scalar_tensor_tensor(qro2[32:64, :], bk2[0:32, :], 1.0,
                                              trig0[DR + 32:2 * DR, cols], ALU.mult, ALU.mult),
                       reads=[tb2, t_sin], writes=[t_qro2])
                    op(E, "vector", lambda v: v.tensor_add(qro1[:], qro1[:],
                                                           qro2[:]),
                       reads=[t_qro2], writes=[t_qro1])
                    op(E, "vector", lambda v: v.scalar_tensor_tensor(
                        q2[0:DR, :], qro1[:], SCALE, rb[0:DR, :],
                        ALU.mult, ALU.mult),
                       reads=[t_qro1, t_rb], writes=[t_q2])
                    sb, tsb = sqb[0], t_sqb[0]
                    op(E, "scalar", lambda s_, cols=cols, sb=sb: s_.activation(
                        sb[:], q1[:], AF.Square, bias=0.0, scale=1.0),
                       reads=[t_q1], writes=[tsb])
                    mmg(E, [(lambda t, sb=sb: t.matmul(
                        pb[6][0:1, :], P["ones_col"][:], sb[:],
                        start=True, stop=False, skip_group_check=True))],
                        reads=[tsb, P["t_oc"]], writes=[tpb[6]])
                    sb2, tsb2 = sqb[1], t_sqb[1]
                    op(E, "scalar", lambda s_, cols=cols, sb2=sb2: s_.activation(
                        sb2[0:DR, :], q2[0:DR, :], AF.Square, bias=0.0,
                        scale=1.0),
                       reads=[t_q2], writes=[tsb2])
                    mmg(E, [(lambda t, sb2=sb2: t.matmul(
                        pb[6][0:1, :], P["ones_col"][0:DR, :], sb2[0:DR, :],
                        start=False, stop=True, skip_group_check=True))],
                        reads=[tsb2, P["t_oc"]], writes=[tpb[6]])
                    op(E, "scalar", lambda s_, cols=cols: s_.activation(
                        qnt[:], pb[6][0:1, :], AF.Sqrt, bias=0.0,
                        scale=1.0),
                       reads=[tpb[6]], writes=[t_qnt])

                    halves = ([(0, g0 + 512)] if g0 < 2048
                              else [(0, 2048), (2048, g0 + 512)])
                    qcols = slice(g0, g0 + 512)
                    for ih, (h0, h1) in enumerate(halves):
                        two = len(halves) == 2
                        mrow_t = m_new if (two and ih == 1) else m_old
                        mrow_trk = t_mn if (two and ih == 1) else t_mo
                        op(E, "vector",
                           lambda v, qcols=qcols, ih=ih, mrow_t=mrow_t:
                           v.tensor_scalar(mrow_t[:], qnt[:],
                                           maxk[0:1, ih:ih + 1], 0.05,
                                           ALU.mult, ALU.add),
                           reads=[t_qnt, t_maxk], writes=[mrow_trk])
                        if two and ih == 1:
                            op(E, "vector", lambda v: v.tensor_max(
                                m_new[:], m_new[:], m_old[:]),
                               reads=[t_mo], writes=[t_mn])
                            op(E, "vector", lambda v: v.tensor_sub(
                                arow[:], m_old[:], m_new[:]),
                               reads=[t_mo, t_mn], writes=[t_ar])
                            op(E, "scalar", lambda s_: s_.activation(
                                arow[:], arow[:], AF.Exp, bias=0.0, scale=1.0),
                               reads=[t_ar], writes=[t_ar])
                        op(E, "vector", lambda v, qcols=qcols, mrow_t=mrow_t:
                           v.tensor_scalar_mul(q2[DR:DR + 1, :], mrow_t[:],
                                               -1.0),
                           reads=[mrow_trk], writes=[t_q2])
                        nkb = (h1 - h0) // 128
                        for ik in range(nkb):
                            kb = h0 + ik * 128
                            kcols = slice(kb, kb + 128)
                            stb, t_stb = pb[4 + (ik % 2)], tpb[4 + (ik % 2)]
                            mmg(E, [
                                (lambda t, kcols=kcols, qcols=qcols, stb=stb:
                                 t.matmul(stb[:], knope[:, kcols], q1[:],
                                          start=True, stop=False)),
                                (lambda t, kcols=kcols, qcols=qcols, stb=stb:
                                 t.matmul(stb[:], kpe[:, kcols], q2[:],
                                          start=False, stop=True)),
                            ], reads=[t_knope, t_kpe, t_q1, t_q2], writes=[t_stb])
                            ptb, t_ptb = pt[ik % 3], t_pt[ik % 3]
                            op(E, "scalar", lambda s_, stb=stb, ptb=ptb:
                               s_.activation(ptb[:], stb[:], AF.Exp, bias=0.0,
                                             scale=1.0),
                               reads=[t_stb], writes=[t_ptb])
                            qlo = 0
                            if kb >= g0:
                                qlo = ((kb - g0) // 128) * 128
                                op(E, "vector", lambda v, ptb=ptb, qlo=qlo:
                                   v.tensor_mul(ptb[:, qlo:qlo + 128],
                                                ptb[:, qlo:qlo + 128],
                                                P["ltri"][:]),
                                   reads=[P["t_ltri"]], writes=[t_ptb])
                            kb_i = kb // 128
                            mmg(E, [(lambda t, ptb=ptb, kb_i=kb_i, qlo=qlo,
                                     ik=ik, nkb=nkb: t.matmul(
                                         pb[2][:, qlo:512], vaug[:, kb_i, 0:DV],
                                         ptb[:, qlo:512], start=(ik == 0),
                                         stop=(ik == nkb - 1),
                                         skip_group_check=True))],
                                reads=[t_ptb, t_vaug], writes=[tpb[2]])
                            mmg(E, [(lambda t, ptb=ptb, qlo=qlo, ik=ik, nkb=nkb:
                                     t.matmul(pb[3][0:1, qlo:512],
                                              P["ones_col"][:],
                                              ptb[:, qlo:512], start=(ik == 0),
                                              stop=(ik == nkb - 1),
                                              skip_group_check=True))],
                                reads=[t_ptb, P["t_oc"]], writes=[tpb[3]])
                        if two and ih == 0:
                            op(E, "vector", lambda v: v.tensor_copy(
                                oacc[:], pb[2][:, :]),
                               reads=[tpb[2]], writes=[t_oacc])
                            op(E, "vector", lambda v: v.tensor_copy(
                                lacc[:], pb[3][0:1, :]),
                               reads=[tpb[3]], writes=[t_lacc])
                            continue
                        if two:
                            mmg(E, [(lambda t: t.matmul(
                                pb[7][:, :], P["ones_r32"][:], arow[:],
                                start=True, stop=True))],
                                reads=[t_ar, P["t_or"]], writes=[tpb[7]])
                            op(E, "vector", lambda v: v.tensor_copy(
                                rb[:], pb[7][:, :]), reads=[tpb[7]],
                               writes=[t_rb])
                            op(E, "vector", lambda v: v.tensor_mul(
                                oacc[:], oacc[:], rb[:]),
                               reads=[t_rb], writes=[t_oacc])
                            op(E, "vector", lambda v: v.tensor_add(
                                oacc[:], oacc[:], pb[2][:, :]),
                               reads=[tpb[2]], writes=[t_oacc])
                            op(E, "vector", lambda v: v.tensor_mul(
                                lacc[:], lacc[:], arow[:]),
                               reads=[t_ar], writes=[t_lacc])
                            op(E, "vector", lambda v: v.tensor_add(
                                lacc[:], lacc[:], pb[3][0:1, :]),
                               reads=[tpb[3]], writes=[t_lacc])
                            op(E, "vector", lambda v: v.reciprocal(
                                rlrow[:], lacc[:]), reads=[t_lacc],
                               writes=[t_rl])
                        else:
                            op(E, "vector", lambda v: v.reciprocal(
                                rlrow[:], pb[3][0:1, :]),
                               reads=[tpb[3]], writes=[t_rl])
                        mmg(E, [(lambda t: t.matmul(
                            pb[7][:, :], P["ones_r32"][:], rlrow[:],
                            start=True, stop=True))],
                            reads=[t_rl, P["t_or"]], writes=[tpb[7]])
                        op(E, "vector", lambda v: v.tensor_copy(
                            rlb[:], pb[7][:, :]), reads=[tpb[7]], writes=[t_rlb])
                        if two:
                            op(E, "vector", lambda v: v.tensor_mul(
                                attn_t[:], oacc[:], rlb[:]),
                               reads=[t_oacc, t_rlb], writes=[t_at])
                        else:
                            op(E, "vector", lambda v: v.tensor_mul(
                                attn_t[:], pb[2][:, :], rlb[:]),
                               reads=[tpb[2], t_rlb], writes=[t_at])
                        op(E, "sync", lambda sy, hl=hl, qcols=qcols: sy.dma_start(
                            out=io["attnin"].ap()[hl * 128:(hl + 1) * 128, qcols],
                            in_=attn_t[:]),
                           reads=[t_at], writes=[tk["attnin"]])
                if hl % 2 == 1 and A_DO_AG:
                    k = hl // 2
                    op(E, "gpsimd", lambda g, k=k: g.collective_compute(
                        "AllGather", ALU.bypass,
                        replica_groups=[[0, 1], [2, 3], [4, 5], [6, 7]],
                        ins=[io["attnin"].ap()[k * 256:(k + 1) * 256, :].opt()],
                        outs=[io["attnout"].ap()[k].opt()]),
                       reads=[tk["attnin"]], writes=[tk["attnout"]], inc=1,
                       q="cc")

    E.join()
    if A_STAGES < 3:
        return
    with ExitStack() as a3:
        e3 = a3.enter_context
        attn_all = e3(nc.sbuf_tensor("attn_all", [128, H, TOWN], BF16))
        wo_sb = e3(nc.sbuf_tensor("wo_sb", [128, NC_I, D], BF16))
        x0t = e3(nc.sbuf_tensor("x0t", [128, D], F32))
        x1t = e3(nc.sbuf_tensor("x1t", [128, D], F32))
        x1o = e3(nc.sbuf_tensor("x1o", [128, D], F32))
        osb = e3(nc.sbuf_tensor("osb", [1, 1], I32))
        lns = mk_ln(nc, a3, "A")
        t_aall = Trk(); t_wo = Trk(); t_x0 = Trk(); t_x1 = Trk(); t_x1o = Trk()
        t_osb = Trk()

        op(E, "gpsimd", lambda g: g.dma_start(
            out=wo_sb[:], in_=io["woT"].ap().rearrange("(c p) f -> p c f", p=128)),
           writes=[t_wo])
        op(E, "gpsimd", lambda g: g.dma_start(out=osb[:], in_=io["ownoff"][:, :]),
           writes=[t_osb])
        rbase = a3.enter_context(nc.gpsimd.register("rbase"))
        rdyn = a3.enter_context(nc.gpsimd.register("rdyn"))
        E.raw_noinc("gpsimd", lambda g: g.reg_load(rbase, osb[0:1, 0:1]),
                    deps=_deps([t_osb], [], []))
        for hg in range(H):
            rp, k, loc = hg // 8, (hg % 8) // 2, hg % 2
            base = (k * 512 + rp * 256 + loc * 128) * S
            if A_DYN_OFF:
                E.raw_noinc("gpsimd",
                            lambda g, base=base: g.reg_add(rdyn, rbase, base))
                op(E, "gpsimd", lambda g, hg=hg: g.dma_start(
                    out=attn_all[:, hg, :],
                    in_=bass.AP(io["attnout"], rdyn, [[S, 128], [1, TOWN]])),
                   reads=[tk["attnout"]], writes=[t_aall])
            else:
                src = (io["attnout"] if A_DO_AG else io["attnin"])
                hl = hg % 8
                rows = ((k * 512 + rp * 256 + loc * 128) if A_DO_AG
                        else hl * 128)
                op(E, "gpsimd", lambda g, hg=hg, src=src, rows=rows:
                   g.dma_start(
                       out=attn_all[:, hg, :],
                       in_=bass.AP(src, rows * S, [[S, 128], [1, TOWN]])),
                   reads=[tk["attnout" if A_DO_AG else "attnin"]],
                   writes=[t_aall])
        for tc in range(TOWN // 128):
            trows = slice(tc * 128, (tc + 1) * 128)
            op(E, "sync", lambda sy, trows=trows: sy.dma_start(
                out=x0t[:], in_=io["xo_tm"].ap()[trows, :]),
               reads=[], writes=[t_x0])
            for ft in range(4):
                fcols = slice(ft * 512, (ft + 1) * 512)
                bk, tb = pb[ft % 2], tpb[ft % 2]
                mmg(E, [
                    (lambda t, hgi=hgi, tc=tc, ft=ft, bk=bk:
                     t.matmul(bk[:], attn_all[:, hgi, tc * 128:(tc + 1) * 128],
                              wo_sb[:, hgi, ft * 512:(ft + 1) * 512],
                              start=(hgi == 0), stop=(hgi == H - 1)))
                    for hgi in range(H)],
                    reads=[t_aall, t_wo], writes=[tb])
                op(E, "vector", lambda v, bk=bk, fcols=fcols: v.tensor_add(
                    x1t[:, fcols], bk[:], x0t[:, fcols]),
                   reads=[tb, t_x0], writes=[t_x1])
            ln_apply(E, lns, x1t[:], t_x1, x1o[:], t_x1o, P)
            op(E, "sync", lambda sy, trows=trows: sy.dma_start(
                out=io["x1s"].ap()[trows, :], in_=x1o[:]),
               reads=[t_x1o], writes=[tk["x1s"]])


def phase_B(E, nc, io, tk, P, ap):
    ecm = ap.enter_context
    pb = [ecm(nc.psum_tensor(f"psB{i}", [128, 512], F32)) for i in range(6)]
    tpb = [Trk(True) for _ in range(6)]
    pbt = ecm(nc.psum_tensor("psBt", [128, 128], BF16))
    t_pbt = Trk(True)

    x1fmh = ecm(nc.sbuf_tensor("x1fmh", [128, NC_I, 1024], BF16))
    y_acc = ecm(nc.sbuf_tensor("y_acc", [128, 8, D], F32))
    u_bf = ecm(nc.sbuf_tensor("u_bf", [128, NJ, 1024], BF16))
    w2_sb = ecm(nc.sbuf_tensor("w2_sb", [128, NJ, D], BF16))
    w1_sl = ecm(nc.sbuf_tensor("w1_sl", [128, NC_I, 128], BF16))
    w3_sl = ecm(nc.sbuf_tensor("w3_sl", [128, NC_I, 128], BF16))
    gw_sb = ecm(nc.sbuf_tensor("gw_sb", [128, NC_I, E_EXP], BF16))
    wgt = ecm(nc.sbuf_tensor("wgt", [128, 16, E_EXP], F32))
    silu_f = ecm(nc.sbuf_tensor("silu_f", [128, 512], F32))
    x1t = ecm(nc.sbuf_tensor("x1tB", [128, D], F32))
    x1bf = ecm(nc.sbuf_tensor("x1bfB", [128, D], BF16))
    x2t = ecm(nc.sbuf_tensor("x2tB", [128, D], F32))
    x2o = ecm(nc.sbuf_tensor("x2oB", [128, D], F32))
    g8 = ecm(nc.sbuf_tensor("g8", [128, E_EXP], F32))
    gcol = [ecm(nc.sbuf_tensor(f"gcol{i}", [128, 1], F32)) for i in range(4)]
    lns = mk_ln(nc, ap, "B")
    t_x1fmh = Trk(); t_yacc = Trk(); t_u = Trk(); t_w2 = Trk()
    t_w1s = Trk(); t_w3s = Trk(); t_gw = Trk(); t_wgt = Trk(); t_silu = Trk()
    t_x1t = Trk(); t_x1bf = Trk(); t_x2t = Trk(); t_x2o = Trk()
    t_g8 = Trk(); t_gc = [Trk() for _ in range(4)]

    op(E, "gpsimd", lambda g: g.dma_start(
        out=gw_sb[:], in_=io["gwT"].ap().rearrange("(c p) f -> p c f", p=128)),
       writes=[t_gw])

    for half in range(2):
        hoff = half * 1024
        for tci in range(8):
            tc = half * 8 + tci
            trows = slice(tc * 128, (tc + 1) * 128)
            op(E, "sync", lambda sy, trows=trows: sy.dma_start(
                out=x1t[:], in_=io["x1s"].ap()[trows, :]),
               reads=[tk["x1s"]], writes=[t_x1t])
            op(E, "scalar", lambda s_: s_.activation(
                x1bf[:], x1t[:], AF.Copy, bias=0.0, scale=1.0),
               reads=[t_x1t], writes=[t_x1bf])
            for ic in range(NC_I):
                mmg(E, [(lambda t, ic=ic: t.transpose(
                    pbt[:], x1bf[:, ic * 128:(ic + 1) * 128], P["ident"][:]))],
                    reads=[t_x1bf, P["t_ident"]], writes=[t_pbt])
                op(E, "vector", lambda v, ic=ic, tci=tci: v.tensor_copy(
                    x1fmh[:, ic, tci * 128:(tci + 1) * 128], pbt[:]),
                   reads=[t_pbt], writes=[t_x1fmh])
            mmg(E, [
                (lambda t, ic=ic, tci=tci:
                 t.matmul(pb[4][:, 0:E_EXP], x1fmh[:, ic,
                                                   tci * 128:(tci + 1) * 128],
                          gw_sb[:, ic, :], start=(ic == 0),
                          stop=(ic == NC_I - 1)))
                for ic in range(NC_I)],
                reads=[t_x1fmh, t_gw], writes=[tpb[4]])
            op(E, "vector", lambda v: v.tensor_copy(g8[:], pb[4][:, 0:E_EXP]),
               reads=[tpb[4]], writes=[t_g8])
            op(E, "vector", lambda v: v.reduce_max(gcol[0][:], g8[:], axis=AXX),
               reads=[t_g8], writes=[t_gc[0]])
            op(E, "vector", lambda v: v.tensor_scalar_mul(gcol[0][:], gcol[0][:],
                                                          -1.0),
               reads=[t_gc[0]], writes=[t_gc[0]])
            op(E, "scalar", lambda s_: s_.activation(
                g8[:], g8[:], AF.Exp, bias=gcol[0][:], scale=1.0,
                accum_out=gcol[1][:]),
               reads=[t_g8, t_gc[0]], writes=[t_g8, t_gc[1]])
            op(E, "vector", lambda v: v.reciprocal(gcol[1][:], gcol[1][:]),
               reads=[t_gc[1]], writes=[t_gc[1]])
            op(E, "vector", lambda v: v.tensor_scalar(
                g8[:], g8[:], gcol[1][:], None, ALU.mult),
               reads=[t_gc[1]], writes=[t_g8])
            op(E, "vector", lambda v: v.reduce_max(gcol[0][:], g8[:], axis=AXX),
               reads=[t_g8], writes=[t_gc[0]])
            op(E, "vector", lambda v: v.tensor_scalar(
                silu_f[:, 0:E_EXP], g8[:], gcol[0][:], None, ALU.is_ge),
               reads=[t_g8, t_gc[0]], writes=[t_silu])
            op(E, "vector", lambda v: v.scalar_tensor_tensor(
                silu_f[:, 0:E_EXP], silu_f[:, 0:E_EXP], -1e30, g8[:],
                ALU.mult, ALU.add),
               reads=[t_g8], writes=[t_silu])
            op(E, "vector", lambda v: v.reduce_max(gcol[1][:], silu_f[:, 0:E_EXP],
                                                   axis=AXX),
               reads=[t_silu], writes=[t_gc[1]])
            op(E, "vector", lambda v: v.tensor_scalar(
                silu_f[:, 0:E_EXP], g8[:], gcol[1][:], None, ALU.is_ge),
               reads=[t_g8, t_gc[1]], writes=[t_silu])
            op(E, "vector", lambda v: v.tensor_add(gcol[2][:], gcol[0][:],
                                                   gcol[1][:]),
               reads=[t_gc[0], t_gc[1]], writes=[t_gc[2]])
            op(E, "vector", lambda v: v.reciprocal(gcol[2][:], gcol[2][:]),
               reads=[t_gc[2]], writes=[t_gc[2]])
            op(E, "vector", lambda v, tc=tc: v.scalar_tensor_tensor(
                wgt[:, tc, :], g8[:], gcol[2][:], silu_f[:, 0:E_EXP],
                ALU.mult, ALU.mult),
               reads=[t_g8, t_gc[2], t_silu], writes=[t_wgt])
        for ep in range(10):
            is_sh = ep >= 8
            jlo = 0 if not is_sh else (ep - 8) * NJ
            njc = NJ
            for jc in range(njc):
                jg = jlo + jc
                if is_sh:
                    src1 = io["sw1"].ap()[:, jg * 128:(jg + 1) * 128]
                    src3 = io["sw3"].ap()[:, jg * 128:(jg + 1) * 128]
                else:
                    src1 = io["w1"].ap()[ep][:, jg * 128:(jg + 1) * 128]
                    src3 = io["w3"].ap()[ep][:, jg * 128:(jg + 1) * 128]
                op(E, "gpsimd", lambda g, src1=src1: g.dma_start(
                    out=w1_sl[:], in_=src1.rearrange("(c p) f -> p c f", p=128)),
                   writes=[t_w1s])
                op(E, "gpsimd", lambda g, src3=src3: g.dma_start(
                    out=w3_sl[:], in_=src3.rearrange("(c p) f -> p c f", p=128)),
                   writes=[t_w3s])
                for tt in range(2):
                    tcols = slice(tt * 512, (tt + 1) * 512)
                    bh, tbh = pb[0 + (tt % 2)], tpb[0 + (tt % 2)]
                    bg, tbg = pb[2 + (tt % 2)], tpb[2 + (tt % 2)]
                    mmg(E, [
                        (lambda t, ic=ic, bh=bh, tcols=tcols:
                         t.matmul(bh[:], w1_sl[:, ic, :], x1fmh[:, ic, tcols],
                                  start=(ic == 0), stop=(ic == NC_I - 1)))
                        for ic in range(NC_I)],
                        reads=[t_w1s, t_x1fmh], writes=[tbh])
                    mmg(E, [
                        (lambda t, ic=ic, bg=bg, tcols=tcols:
                         t.matmul(bg[:], w3_sl[:, ic, :], x1fmh[:, ic, tcols],
                                  start=(ic == 0), stop=(ic == NC_I - 1)))
                        for ic in range(NC_I)],
                        reads=[t_w3s, t_x1fmh], writes=[tbg])
                    op(E, "scalar", lambda s_, bh=bh: s_.activation(
                        silu_f[:], bh[:], AF.Silu, bias=0.0, scale=1.0),
                       reads=[tbh], writes=[t_silu])
                    op(E, "vector", lambda v, bg=bg, jc=jc, tcols=tcols:
                       v.tensor_mul(u_bf[:, jc, tcols], silu_f[:], bg[:]),
                       reads=[t_silu, tbg], writes=[t_u])
            if is_sh:
                src2 = io["sw2"].ap()[jlo * 128:(jlo + NJ) * 128, :]
            else:
                src2 = io["w2"].ap()[ep]
            op(E, "gpsimd", lambda g, src2=src2: g.dma_start(
                out=w2_sb[:], in_=src2.rearrange("(c p) f -> p c f", p=128)),
               writes=[t_w2])
            for ot in range(4):
                ocols = slice(ot * 512, (ot + 1) * 512)
                for tci in range(8):
                    tc = half * 8 + tci
                    bk, tb = pb[2 + (tci % 4)], tpb[2 + (tci % 4)]
                    mmg(E, [
                        (lambda t, jc=jc, tci=tci, bk=bk, ocols=ocols:
                         t.matmul(bk[:], u_bf[:, jc, tci * 128:(tci + 1) * 128],
                                  w2_sb[:, jc, ocols], start=(jc == 0),
                                  stop=(jc == NJ - 1)))
                        for jc in range(NJ)],
                        reads=[t_u, t_w2], writes=[tb])
                    if ep == 0:
                        op(E, "vector",
                           lambda v, bk=bk, tci=tci, tc=tc, ocols=ocols:
                           v.tensor_scalar(y_acc[:, tci, ocols], bk[:],
                                           wgt[:, tc, 0:1], None, ALU.mult),
                           reads=[tb, t_wgt], writes=[t_yacc])
                    elif not is_sh:
                        op(E, "vector",
                           lambda v, bk=bk, tci=tci, tc=tc, ocols=ocols, ep=ep:
                           v.scalar_tensor_tensor(
                               y_acc[:, tci, ocols], bk[:], wgt[:, tc, ep:ep + 1],
                               y_acc[:, tci, ocols], ALU.mult, ALU.add),
                           reads=[tb, t_wgt], writes=[t_yacc])
                    else:
                        op(E, "vector",
                           lambda v, bk=bk, tci=tci, ocols=ocols:
                           v.scalar_tensor_tensor(
                               y_acc[:, tci, ocols], bk[:], 1.0,
                               y_acc[:, tci, ocols], ALU.mult, ALU.add),
                           reads=[tb], writes=[t_yacc])
        for tci in range(8):
            tc = half * 8 + tci
            trows = slice(tc * 128, (tc + 1) * 128)
            op(E, "sync", lambda sy, trows=trows: sy.dma_start(
                out=x1t[:], in_=io["x1s"].ap()[trows, :]),
               reads=[tk["x1s"]], writes=[t_x1t])
            op(E, "vector", lambda v, tci=tci: v.tensor_add(
                x2t[:], x1t[:], y_acc[:, tci, :]),
               reads=[t_x1t, t_yacc], writes=[t_x2t])
            ln_apply(E, lns, x2t[:], t_x2t, x2o[:], t_x2o, P)
            op(E, "sync", lambda sy, trows=trows: sy.dma_start(
                out=io["x2s"].ap()[trows, :], in_=x2o[:]),
               reads=[t_x2o], writes=[tk["x2s"]])


def phase_C(E, nc, io, tk, P, ap):
    ecm = ap.enter_context
    pb = [ecm(nc.psum_tensor(f"psC{i}", [128, 512], F32)) for i in range(6)]
    tpb = [Trk(True) for _ in range(6)]
    pbm = ecm(nc.psum_tensor("psCm", [128, HD + 1], F32))
    t_pbm = Trk(True)
    pbt = ecm(nc.psum_tensor("psCt", [128, 128], BF16))
    t_pbt = Trk(True)

    x2fm = ecm(nc.sbuf_tensor("x2fm", [128, NC_I, TOWN], BF16))
    kffm = ecm(nc.sbuf_tensor("kffm", [128, KVH, TOWN], BF16))
    vaug1 = ecm(nc.sbuf_tensor("vaug1", [128, 16, KVH, HD + 1], BF16))
    states = ecm(nc.sbuf_tensor("states", [128, KVH, HD + 1], F32))
    statebf = ecm(nc.sbuf_tensor("statebf", [128, KVH, HD + 1], BF16))
    inits = ecm(nc.sbuf_tensor("inits", [128, 2, KVH, HD + 1], F32))
    summ_sb = ecm(nc.sbuf_tensor("summ_sb", [128, 2, KVH, SUMC], F32))
    smask_sb = ecm(nc.sbuf_tensor("smask_sb", [128, 2, 16], F32))
    ktm_t = ecm(nc.sbuf_tensor("ktm_t", [128, 128], BF16))
    t_x2fm = Trk(); t_kffm = Trk(); t_vaug = Trk()
    t_st = Trk(); t_stbf = Trk(); t_inits = Trk()
    t_summ = Trk(); t_smask = Trk(); t_ktm = Trk()

    op(E, "sync", lambda sy: sy.dma_start(out=smask_sb[:],
                                          in_=io["smask"][:, :, :]),
       writes=[t_smask])
    op(E, "vector", lambda v: v.memset(vaug1.ap()[:, :, :, HD:HD + 1], 1.0),
       writes=[t_vaug])
    op(E, "vector", lambda v: v.memset(summ_sb[:], 0.0), writes=[t_summ])

    def ktm_make(tc, kvh):
        mmg(E, [(lambda t, tc=tc, kvh=kvh: t.transpose(
            pbt[:], kffm[:, kvh, tc * 128:(tc + 1) * 128], P["ident"][:]))],
            reads=[t_kffm, P["t_ident"]], writes=[t_pbt])
        op(E, "vector", lambda v: v.tensor_copy(ktm_t[:], pbt[:]),
           reads=[t_pbt], writes=[t_ktm])

    _rope_ctr = [0]

    def mk_rope(scope):
        _rope_ctr[0] += 1
        i = _rope_ctr[0]
        r = {nm: scope.enter_context(nc.sbuf_tensor(f"{nm}_{i}", [HD, 512], F32))
             for nm in ("cosx", "sinx", "ro1", "ro2", "elu")}
        r.update({"t_cos": Trk(), "t_sin": Trk(), "t_ro1": Trk(),
                  "t_ro2": Trk(), "t_elu": Trk()})
        return r

    def rope_elu(r, bank, tb, out_ap, out_trk):
        hh = HD // 2
        op(E, "vector", lambda v: v.tensor_mul(r["ro1"][:], bank[0:HD, :],
                                               r["cosx"][:]),
           reads=[tb, r["t_cos"]], writes=[r["t_ro1"]])
        op(E, "vector", lambda v: v.scalar_tensor_tensor(
            r["ro2"][0:hh, :], bank[hh:HD, :], -1.0, r["sinx"][0:hh, :],
            ALU.mult, ALU.mult), reads=[tb, r["t_sin"]], writes=[r["t_ro2"]])
        op(E, "vector", lambda v: v.scalar_tensor_tensor(
            r["ro2"][hh:HD, :], bank[0:hh, :], 1.0, r["sinx"][hh:HD, :],
            ALU.mult, ALU.mult), reads=[tb, r["t_sin"]], writes=[r["t_ro2"]])
        op(E, "vector", lambda v: v.tensor_add(r["ro1"][:], r["ro1"][:],
                                               r["ro2"][:]),
           reads=[r["t_ro2"]], writes=[r["t_ro1"]])
        op(E, "vector", lambda v: v.tensor_scalar_min(r["elu"][:], r["ro1"][:],
                                                      0.0),
           reads=[r["t_ro1"]], writes=[r["t_elu"]])
        op(E, "scalar", lambda s_: s_.activation(r["elu"][:], r["elu"][:],
                                                 AF.Exp, bias=0.0, scale=1.0),
           reads=[r["t_elu"]], writes=[r["t_elu"]])
        op(E, "vector", lambda v: v.scalar_tensor_tensor(
            out_ap, r["ro1"][:], 0.0, r["elu"][:], ALU.max, ALU.add),
           reads=[r["t_ro1"], r["t_elu"]], writes=[out_trk])

    with ExitStack() as c12:
        ee = c12.enter_context
        x2t = ee(nc.sbuf_tensor("x2tC", [128, D], F32))
        x2bf = ee(nc.sbuf_tensor("x2bfC", [128, D], BF16))
        wk_sb = ee(nc.sbuf_tensor("wk_sb", [128, NC_I, KVH * HD], BF16))
        wv_sb = ee(nc.sbuf_tensor("wv_sb", [128, NC_I, KVH * HD], BF16))
        rr = mk_rope(c12)
        t_x2t = Trk(); t_x2bf = Trk(); t_wk = Trk(); t_wv = Trk()

        op(E, "gpsimd", lambda g: g.dma_start(
            out=wk_sb[:],
            in_=io["wkT1"].ap().rearrange("(c p) f -> p c f", p=128)),
           writes=[t_wk])
        op(E, "gpsimd", lambda g: g.dma_start(
            out=wv_sb[:],
            in_=io["wvT1"].ap().rearrange("(c p) f -> p c f", p=128)),
           writes=[t_wv])
        for tc in range(16):
            trows = slice(tc * 128, (tc + 1) * 128)
            op(E, "sync", lambda sy, trows=trows: sy.dma_start(
                out=x2t[:], in_=io["x2s"].ap()[trows, :]),
               reads=[tk["x2s"]], writes=[t_x2t])
            op(E, "scalar", lambda s_: s_.activation(x2bf[:], x2t[:], AF.Copy,
                                                     bias=0.0, scale=1.0),
               reads=[t_x2t], writes=[t_x2bf])
            for ic in range(NC_I):
                mmg(E, [(lambda t, ic=ic: t.transpose(
                    pbt[:], x2bf[:, ic * 128:(ic + 1) * 128], P["ident"][:]))],
                    reads=[t_x2bf, P["t_ident"]], writes=[t_pbt])
                op(E, "vector", lambda v, ic=ic, tc=tc: v.tensor_copy(
                    x2fm[:, ic, tc * 128:(tc + 1) * 128], pbt[:]),
                   reads=[t_pbt], writes=[t_x2fm])
        for tt in range(4):
            tcols = slice(tt * 512, (tt + 1) * 512)
            op(E, "sync", lambda sy, tcols=tcols, rr=rr: sy.dma_start(
                out=rr["cosx"][:], in_=io["cos1"].ap()[:, tcols]),
               writes=[rr["t_cos"]])
            op(E, "sync", lambda sy, tcols=tcols, rr=rr: sy.dma_start(
                out=rr["sinx"][:], in_=io["sin1"].ap()[:, tcols]),
               writes=[rr["t_sin"]])
            for kvh in range(KVH):
                bk, tb = pb[kvh % 2], tpb[kvh % 2]
                mmg(E, [
                    (lambda t, ic=ic, kvh=kvh, bk=bk, tcols=tcols:
                     t.matmul(bk[0:HD, :],
                              wk_sb[:, ic, kvh * HD:(kvh + 1) * HD],
                              x2fm[:, ic, tcols], start=(ic == 0),
                              stop=(ic == NC_I - 1)))
                    for ic in range(NC_I)],
                    reads=[t_wk, t_x2fm], writes=[tb])
                rope_elu(rr, bk, tb, kffm.ap()[:, kvh, tcols], t_kffm)
            for tci in range(4):
                tc = tt * 4 + tci
                bk, tb = pb[2 + (tci % 2)], tpb[2 + (tci % 2)]
                mmg(E, [
                    (lambda t, ic=ic, tc=tc, bk=bk:
                     t.matmul(bk[:], x2fm[:, ic, tc * 128:(tc + 1) * 128],
                              wv_sb[:, ic, :], start=(ic == 0),
                              stop=(ic == NC_I - 1)))
                    for ic in range(NC_I)],
                    reads=[t_wv, t_x2fm], writes=[tb])
                for kvh in range(KVH):
                    op(E, "vector", lambda v, bk=bk, tc=tc, kvh=kvh:
                       v.tensor_copy(vaug1[:, tc, kvh, 0:HD],
                                     bk[:, kvh * HD:(kvh + 1) * HD]),
                       reads=[tb], writes=[t_vaug])

    E.join()
    for osc in range(2):
        for kvh in range(KVH):
            for c8 in range(8):
                tc = osc * 8 + c8
                ktm_make(tc, kvh)
                mmg(E, [(lambda t, tc=tc, kvh=kvh, c8=c8:
                         t.matmul(pbm[:], ktm_t[:], vaug1[:, tc, kvh, :],
                                  start=(c8 == 0), stop=(c8 == 7),
                                  skip_group_check=True))],
                    reads=[t_ktm, t_vaug], writes=[t_pbm])
            op(E, "vector", lambda v, osc=osc, kvh=kvh: v.tensor_copy(
                summ_sb[:, osc, kvh, 0:HD + 1], pbm[:]),
               reads=[t_pbm], writes=[t_summ])
            r0 = (osc * KVH + kvh) * 128
            op(E, "sync", lambda sy, r0=r0, osc=osc, kvh=kvh: sy.dma_start(
                out=io["summin"].ap()[r0:r0 + 128, :],
                in_=summ_sb[:, osc, kvh, :]),
               reads=[t_summ], writes=[tk["summin"]])
    op(E, "gpsimd", lambda g: g.collective_compute(
        "AllGather", ALU.bypass, replica_groups=[list(range(8))],
        ins=[io["summin"].ap().opt()], outs=[io["summall"].ap().opt()]),
       reads=[tk["summin"]], writes=[tk["summall"]], inc=1, q="cc")
    with ExitStack() as c3:
        sall = c3.enter_context(nc.sbuf_tensor("sall", [128, 64, SUMC], F32))
        t_sall = Trk()
        op(E, "sync", lambda sy: sy.dma_start(
            out=sall[:], in_=io["summall"].ap().rearrange("(g p) c -> p g c",
                                                          p=128)),
           reads=[tk["summall"]], writes=[t_sall])
        for osc in range(2):
            for kvh in range(KVH):
                for rk in range(16):
                    g64 = (rk // 2) * 8 + (rk % 2) * 4 + kvh
                    if rk == 0:
                        op(E, "vector", lambda v, osc=osc, kvh=kvh, g64=g64,
                           rk=rk: v.tensor_scalar(
                               inits[:, osc, kvh, :], sall[:, g64, 0:HD + 1],
                               smask_sb[:, osc, rk:rk + 1], None, ALU.mult),
                           reads=[t_sall, t_smask], writes=[t_inits])
                    else:
                        op(E, "vector", lambda v, osc=osc, kvh=kvh, g64=g64,
                           rk=rk: v.scalar_tensor_tensor(
                               inits[:, osc, kvh, :], sall[:, g64, 0:HD + 1],
                               smask_sb[:, osc, rk:rk + 1],
                               inits[:, osc, kvh, :], ALU.mult, ALU.add),
                           reads=[t_sall, t_smask], writes=[t_inits])

    E.join()
    with ExitStack() as c4:
        ee = c4.enter_context
        qf = ee(nc.sbuf_tensor("qf", [128, H, 512], BF16))
        attn1g = ee(nc.sbuf_tensor("attn1g", [128, H, 512], BF16))
        wq_sl = ee(nc.sbuf_tensor("wq_sl", [128, NC_I, 128], BF16))
        wo_sl = ee(nc.sbuf_tensor("wo_sl", [128, H, 512], BF16))
        abuf = [ee(nc.sbuf_tensor(f"abuf{i}", [128, 128], BF16))
                for i in range(2)]
        at_t = ee(nc.sbuf_tensor("at_t", [128, 128], BF16))
        dcol = ee(nc.sbuf_tensor("dcol", [128, 1], F32))
        x2t4 = ee(nc.sbuf_tensor("x2tC4", [128, D], F32))
        x3t = [ee(nc.sbuf_tensor(f"x3t{i}", [128, D], F32)) for i in range(2)]
        rr = mk_rope(c4)
        lns = mk_ln(nc, c4, "C")
        t_qf = Trk(); t_a1g = Trk(); t_wq = Trk(); t_wo = Trk()
        t_ab = [Trk(), Trk()]; t_att = Trk(); t_dcol = Trk()
        t_x2t4 = Trk(); t_x3t = [Trk(), Trk()]

        for g in range(4):
            gcols = slice(g * 512, (g + 1) * 512)
            op(E, "sync", lambda sy, gcols=gcols, rr=rr: sy.dma_start(
                out=rr["cosx"][:], in_=io["cos1"].ap()[:, gcols]),
               writes=[rr["t_cos"]])
            op(E, "sync", lambda sy, gcols=gcols, rr=rr: sy.dma_start(
                out=rr["sinx"][:], in_=io["sin1"].ap()[:, gcols]),
               writes=[rr["t_sin"]])
            for h in range(H):
                op(E, "gpsimd", lambda g_, h=h: g_.dma_start(
                    out=wq_sl[:],
                    in_=io["wqT1"].ap()[:, h * HD:(h + 1) * HD]
                        .rearrange("(c p) f -> p c f", p=128)),
                   writes=[t_wq])
                bk, tb = pb[4], tpb[4]
                mmg(E, [
                    (lambda t, ic=ic, bk=bk, gcols=gcols:
                     t.matmul(bk[:], wq_sl[:, ic, :], x2fm[:, ic, gcols],
                              start=(ic == 0), stop=(ic == NC_I - 1)))
                    for ic in range(NC_I)],
                    reads=[t_wq, t_x2fm], writes=[tb])
                rope_elu(rr, bk, tb, qf.ap()[:, h, :], t_qf)
            for tci in range(4):
                tc = g * 4 + tci
                icols = slice(tci * 128, (tci + 1) * 128)
                if tc % 8 == 0:
                    osc = tc // 8
                    for kvh in range(KVH):
                        op(E, "vector", lambda v, osc=osc, kvh=kvh:
                           v.tensor_copy(states[:, kvh, :],
                                         inits[:, osc, kvh, :]),
                           reads=[t_inits], writes=[t_st])
                    op(E, "vector", lambda v: v.tensor_copy(statebf[:],
                                                            states[:]),
                       reads=[t_st], writes=[t_stbf])
                for h in range(H):
                    kvh = h // (H // KVH)
                    bk, tb = pb[5], tpb[5]
                    mmg(E, [(lambda t, h=h, kvh=kvh, tc=tc, icols=icols, bk=bk:
                             t.matmul(bk[:, 0:128],
                                      kffm[:, kvh, tc * 128:(tc + 1) * 128],
                                      qf[:, h, icols], start=True, stop=True))],
                        reads=[t_kffm, t_qf], writes=[tb])
                    ab, t_abx = abuf[h % 2], t_ab[h % 2]
                    op(E, "vector", lambda v, bk=bk, ab=ab: v.tensor_mul(
                        ab[:], bk[:, 0:128], P["ltri"][:]),
                       reads=[tb, P["t_ltri"]], writes=[t_abx])
                    mmg(E, [
                        (lambda t, ab=ab, tc=tc, kvh=kvh:
                         t.matmul(pbm[:], ab[:], vaug1[:, tc, kvh, :],
                                  start=True, stop=False)),
                        (lambda t, h=h, kvh=kvh, icols=icols:
                         t.matmul(pbm[:], qf[:, h, icols], statebf[:, kvh, :],
                                  start=False, stop=True)),
                    ], reads=[t_abx, t_vaug, t_qf, t_stbf], writes=[t_pbm])
                    op(E, "vector", lambda v: v.tensor_scalar(
                        dcol[:], pbm[:, HD:HD + 1], L1_EPS, None, ALU.add),
                       reads=[t_pbm], writes=[t_dcol])
                    op(E, "vector", lambda v: v.reciprocal(dcol[:], dcol[:]),
                       reads=[t_dcol], writes=[t_dcol])
                    op(E, "vector", lambda v: v.tensor_scalar(
                        at_t[:], pbm[:, 0:HD], dcol[:], None, ALU.mult),
                       reads=[t_pbm, t_dcol], writes=[t_att])
                    mmg(E, [(lambda t: t.transpose(pbt[:], at_t[:],
                                                   P["ident"][:]))],
                        reads=[t_att, P["t_ident"]], writes=[t_pbt])
                    op(E, "vector", lambda v, h=h, icols=icols: v.tensor_copy(
                        attn1g[:, h, icols], pbt[:]),
                       reads=[t_pbt], writes=[t_a1g])
                for kvh in range(KVH):
                    ktm_make(tc, kvh)
                    mmg(E, [(lambda t, tc=tc, kvh=kvh: t.matmul(
                        pbm[:], ktm_t[:], vaug1[:, tc, kvh, :],
                        start=True, stop=True))],
                        reads=[t_ktm, t_vaug], writes=[t_pbm])
                    op(E, "vector", lambda v, kvh=kvh: v.tensor_add(
                        states[:, kvh, :], states[:, kvh, :], pbm[:]),
                       reads=[t_pbm], writes=[t_st])
                op(E, "vector", lambda v: v.tensor_copy(statebf[:], states[:]),
                   reads=[t_st], writes=[t_stbf])
            for pairi in range(2):
                for ft in range(4):
                    fcols = slice(ft * 512, (ft + 1) * 512)
                    op(E, "gpsimd", lambda g_, ft=ft: g_.dma_start(
                        out=wo_sl[:],
                        in_=io["woT1"].ap()[:, ft * 512:(ft + 1) * 512]
                            .rearrange("(c p) f -> p c f", p=128)),
                       writes=[t_wo])
                    for tin in range(2):
                        tci = pairi * 2 + tin
                        bk, tb = pb[tin], tpb[tin]
                        mmg(E, [
                            (lambda t, h=h, tci=tci, bk=bk:
                             t.matmul(bk[:],
                                      attn1g[:, h, tci * 128:(tci + 1) * 128],
                                      wo_sl[:, h, :], start=(h == 0),
                                      stop=(h == H - 1)))
                            for h in range(H)],
                            reads=[t_a1g, t_wo], writes=[tb])
                        op(E, "vector", lambda v, bk=bk, tin=tin, fcols=fcols:
                           v.tensor_copy(x3t[tin][:, fcols], bk[:]),
                           reads=[tb], writes=[t_x3t[tin]])
                for tin in range(2):
                    tci = pairi * 2 + tin
                    tc = g * 4 + tci
                    trows = slice(tc * 128, (tc + 1) * 128)
                    op(E, "sync", lambda sy, trows=trows: sy.dma_start(
                        out=x2t4[:], in_=io["x2s"].ap()[trows, :]),
                       reads=[tk["x2s"]], writes=[t_x2t4])
                    op(E, "vector", lambda v, tin=tin: v.tensor_add(
                        x3t[tin][:], x3t[tin][:], x2t4[:]),
                       reads=[t_x2t4], writes=[t_x3t[tin]])
                    ln_apply(E, lns, x3t[tin][:], t_x3t[tin], x2t4[:], t_x2t4, P)
                    op(E, "sync", lambda sy, trows=trows: sy.dma_start(
                        out=io["x3s"].ap()[trows, :], in_=x2t4[:]),
                       reads=[t_x2t4], writes=[tk["x3s"]])


def phase_D(E, nc, io, tk, P, ap):
    ecm = ap.enter_context
    pb = [ecm(nc.psum_tensor(f"psD{i}", [128, 512], F32)) for i in range(4)]
    tpb = [Trk(True) for _ in range(4)]
    pbt = ecm(nc.psum_tensor("psDt", [128, 128], BF16))
    t_pbt = Trk(True)

    w_sb = ecm(nc.sbuf_tensor("l1w_sb", [128, NC_I, D], BF16))
    scb = ecm(nc.sbuf_tensor("scb", [128, D], F32))
    bib = ecm(nc.sbuf_tensor("bib", [128, D], F32))
    srow = ecm(nc.sbuf_tensor("srow", [1, D], F32))
    x3fm = ecm(nc.sbuf_tensor("x3fmD", [128, NC_I, 128], BF16))
    x3t = ecm(nc.sbuf_tensor("x3tD", [128, D], F32))
    x3bf = ecm(nc.sbuf_tensor("x3bfD", [128, D], BF16))
    yt = ecm(nc.sbuf_tensor("ytD", [128, D], F32))
    yo = ecm(nc.sbuf_tensor("yoD", [128, D], F32))
    lns = mk_ln(nc, ap, "D")
    t_w = Trk(); t_scb = Trk(); t_bib = Trk(); t_srow = Trk()
    t_x3fm = Trk(); t_x3t = Trk(); t_x3bf = Trk(); t_yt = Trk(); t_yo = Trk()

    op(E, "gpsimd", lambda g: g.dma_start(
        out=w_sb[:], in_=io["l1wT"].ap().rearrange("(c p) f -> p c f", p=128)),
       writes=[t_w])
    for nm, tile, trk in (("l1scale", scb, t_scb), ("l1bias", bib, t_bib)):
        op(E, "sync", lambda sy, nm=nm: sy.dma_start(out=srow[:],
                                                     in_=io[nm][:, :]),
           writes=[t_srow])
        for ft in range(4):
            fcols = slice(ft * 512, (ft + 1) * 512)
            mmg(E, [(lambda t, fcols=fcols: t.matmul(
                pb[3][:, :], P["ones_r32"][:], srow[0:1, fcols],
                start=True, stop=True))],
                reads=[t_srow, P["t_or"]], writes=[tpb[3]])
            op(E, "vector", lambda v, tile=tile, fcols=fcols: v.tensor_copy(
                tile[:, fcols], pb[3][:, :]),
               reads=[tpb[3]], writes=[trk])

    for tc in range(16):
        trows = slice(tc * 128, (tc + 1) * 128)
        op(E, "sync", lambda sy, trows=trows: sy.dma_start(
            out=x3t[:], in_=io["x3s"].ap()[trows, :]),
           reads=[tk["x3s"]], writes=[t_x3t])
        op(E, "scalar", lambda s_: s_.activation(x3bf[:], x3t[:], AF.Copy,
                                                 bias=0.0, scale=1.0),
           reads=[t_x3t], writes=[t_x3bf])
        for ic in range(NC_I):
            mmg(E, [(lambda t, ic=ic: t.transpose(
                pbt[:], x3bf[:, ic * 128:(ic + 1) * 128], P["ident"][:]))],
                reads=[t_x3bf, P["t_ident"]], writes=[t_pbt])
            op(E, "vector", lambda v, ic=ic: v.tensor_copy(
                x3fm[:, ic, :], pbt[:]),
               reads=[t_pbt], writes=[t_x3fm])
        for ft in range(4):
            fcols = slice(ft * 512, (ft + 1) * 512)
            bk, tb = pb[ft % 2], tpb[ft % 2]
            mmg(E, [
                (lambda t, ic=ic, ft=ft, bk=bk:
                 t.matmul(bk[:], x3fm[:, ic, :],
                          w_sb[:, ic, ft * 512:(ft + 1) * 512],
                          start=(ic == 0), stop=(ic == NC_I - 1)))
                for ic in range(NC_I)],
                reads=[t_x3fm, t_w], writes=[tb])
            op(E, "vector", lambda v, bk=bk, fcols=fcols: v.tensor_mul(
                yt[:, fcols], bk[:], scb[:, fcols]),
               reads=[tb, t_scb], writes=[t_yt])
            op(E, "vector", lambda v, fcols=fcols: v.tensor_add(
                yt[:, fcols], yt[:, fcols], bib[:, fcols]),
               reads=[t_bib], writes=[t_yt])
        op(E, "vector", lambda v: v.tensor_add(yt[:], yt[:], x3t[:]),
           reads=[t_x3t], writes=[t_yt])
        ln_apply(E, lns, yt[:], t_yt, yo[:], t_yo, P)
        op(E, "sync", lambda sy, trows=trows: sy.dma_start(
            out=io["y"].ap()[trows, :], in_=yo[:]),
           reads=[t_yo], writes=[tk["y"]])


def _rope_tables_fm(dim, positions):
    inv = 1.0 / (10000.0 ** (np.arange(0, dim, 2, dtype=np.float64) / dim))
    ang = positions[None, :].astype(np.float64) * np.concatenate([inv, inv])[:, None]
    return (np.cos(ang).astype(np.float32), np.sin(ang).astype(np.float32))


def _prep_inputs(inputs):
    x0 = np.ascontiguousarray(inputs["L0_attn_hidden_states"], dtype=np.float32)
    shared = {}
    shared["wqaT"] = np.ascontiguousarray(inputs["L0_attn_w_qa"].T)
    shared["wkvaT"] = np.ascontiguousarray(inputs["L0_attn_w_kva"].T)
    shared["woT"] = np.ascontiguousarray(inputs["L0_attn_w_o"].T)
    shared["qaln"] = np.ascontiguousarray(
        inputs["L0_attn_q_a_ln_w"].reshape(NC_Q, 128).T)
    shared["kvln"] = np.ascontiguousarray(
        inputs["L0_attn_kv_a_ln_w"].reshape(NC_KV, 128).T)
    cos0, sin0 = _rope_tables_fm(DR, np.arange(S))
    shared["cos0"] = cos0
    shared["sin0"] = sin0
    shared["gwT"] = np.ascontiguousarray(inputs["L0_ffn_gate_w"].T)
    shared["w1"] = np.ascontiguousarray(inputs["L0_ffn_w1"])
    shared["w3"] = np.ascontiguousarray(inputs["L0_ffn_w3"])
    shared["w2"] = np.ascontiguousarray(inputs["L0_ffn_w2"])
    shared["sw1"] = np.ascontiguousarray(inputs["L0_ffn_sw1"])
    shared["sw3"] = np.ascontiguousarray(inputs["L0_ffn_sw3"])
    shared["sw2"] = np.ascontiguousarray(inputs["L0_ffn_sw2"])
    shared["wqT1"] = np.ascontiguousarray(inputs["L1_attn_Wq"].T)
    shared["wkT1"] = np.ascontiguousarray(inputs["L1_attn_Wk"].T)
    shared["wvT1"] = np.ascontiguousarray(inputs["L1_attn_Wv"].T)
    shared["woT1"] = np.ascontiguousarray(inputs["L1_attn_Wo"].T)
    shared["l1wT"] = np.ascontiguousarray(inputs["L1_ffn_weight"].T)
    shared["l1scale"] = np.ascontiguousarray(
        inputs["L1_ffn_scale_weight"].reshape(1, D))
    shared["l1bias"] = np.ascontiguousarray(inputs["L1_ffn_bias"].reshape(1, D))
    shared["ltri"] = np.triu(np.ones((128, 128), dtype=np.float32))
    shared["ident"] = np.eye(128, dtype=np.float32)
    wqbT = np.ascontiguousarray(inputs["L0_attn_w_qb"].T)
    wkvbT = np.ascontiguousarray(inputs["L0_attn_w_kvb"].T)

    in_maps = []
    for c in range(8):
        b, j = c // 2, c % 2
        s0 = TOWN * j
        m = dict(shared)
        m["xo_tm"] = np.ascontiguousarray(x0[b, s0:s0 + TOWN, :])
        m["xa_fm"] = np.ascontiguousarray(x0[b].T)
        m["wqbT_own"] = np.ascontiguousarray(
            wqbT[:, 8 * j * DQ:(8 * j + 8) * DQ])
        m["wkvbT_own"] = np.ascontiguousarray(
            wkvbT[:, 8 * j * 256:(8 * j + 8) * 256])
        m["ownoff"] = np.array([[s0]], dtype=np.int32)
        cos1, sin1 = _rope_tables_fm(HD, np.arange(s0, s0 + TOWN))
        m["cos1"] = cos1
        m["sin1"] = sin1
        sm = np.zeros((128, 2, 16), dtype=np.float32)
        for oi, osc in enumerate((2 * j, 2 * j + 1)):
            for rk in range(16):
                r, k = rk // 2, rk % 2
                rb_, jr = r // 2, r % 2
                gsc = 2 * jr + k
                if rb_ == b and gsc < osc:
                    sm[:, oi, rk] = 1.0
        m["smask"] = sm
        in_maps.append(m)
    return in_maps


_PHASE_INPUTS = {
    "A": ["xo_tm", "xa_fm", "wqaT", "wqbT_own", "wkvaT", "wkvbT_own", "woT",
          "qaln", "kvln", "cos0", "sin0", "ownoff"],
    "B": ["gwT", "w1", "w3", "w2", "sw1", "sw3", "sw2"],
    "C": ["wkT1", "wvT1", "wqT1", "woT1", "cos1", "sin1", "smask"],
    "D": ["l1wT", "l1scale", "l1bias"],
}
_PHASE_STREAM_IN = {"B": ["x1s"], "C": ["x2s"], "D": ["x3s"]}
_PHASE_STREAM_OUT = {"A": "x1s", "B": "x2s", "C": "x3s", "D": "y"}

_prog_cache = {}


def kernel(**inputs):
    in_maps = _prep_inputs(inputs)
    stream = [dict() for _ in range(8)]
    out = np.empty((B, S, D), dtype=np.float32)
    for phases in LAUNCH_PLAN:
        key = tuple(phases)
        if key not in _prog_cache:
            _prog_cache[key] = build_program(list(phases))
        nc = _prog_cache[key]
        names = ["ltri", "ident"]
        for ph in phases:
            names += _PHASE_INPUTS[ph]
        first = phases[0]
        sins = _PHASE_STREAM_IN.get(first, [])
        maps = []
        for c in range(8):
            mm = {nm: in_maps[c][nm] for nm in names}
            for snm in sins:
                mm[snm] = stream[c][snm]
            maps.append(mm)
        res = run_bass_kernel_spmd(nc, maps, core_ids=list(range(8)))
        last = phases[-1]
        onm = _PHASE_STREAM_OUT[last]
        for c in range(8):
            arr = res.results[c][onm]
            if onm == "y":
                b, j = c // 2, c % 2
                out[b, TOWN * j:TOWN * (j + 1), :] = arr
            else:
                stream[c][onm] = arr
    return out


# version 05
# speedup vs baseline: 1.0380x; 1.0380x over previous
import math
from contextlib import ExitStack

import numpy as np

import concourse.bass as bass
import concourse.mybir as mybir
from concourse.bass_utils import run_bass_kernel_spmd

F32 = mybir.dt.float32
BF16 = mybir.dt.bfloat16
I32 = mybir.dt.int32
AF = mybir.ActivationFunctionType
ALU = mybir.AluOpType
AXX = mybir.AxisListType.X

B, S, D = 4, 4096, 2048
TOWN = 2048
H, QLR, KVLR, DN, DR, DV = 16, 1536, 512, 128, 64, 128
DQ = DN + DR
SCALE = DQ ** (-0.5)
RMS_EPS = 1e-6
LN_EPS = 1e-5
E_EXP, I_FF, SH_FF = 8, 1408, 2816
KVH, HD = 4, 128
L1_EPS = 1e-6
NC_I = D // 128
NC_Q = QLR // 128
NC_KV = KVLR // 128
NJ = I_FF // 128
NJS = SH_FF // 128
SUMC = 256

LAUNCH_PLAN = [["A", "B", "C", "D"]]
A_DO_AG = True
A_DYN_OFF = True
A_STAGES = 3
A_A1B = True
A_CQS = True
A_NT = 8
A_RMS = True
A_SQ = True
A_TS = True
A_MM = True
A_COL = True


class Trk:
    __slots__ = ("w", "r", "excl")

    def __init__(self, excl=False):
        self.w = []
        self.r = []
        self.excl = excl


class Emit:
    ENGINES = ("sync", "gpsimd", "tensor", "vector", "scalar")
    QOF = {"sync": "qSY", "gpsimd": "qGP", "tensor": "eT", "vector": "eV",
           "scalar": "eS"}

    def __init__(self, nc, stack):
        self.nc = nc
        names = ("qSY", "qGP", "eT", "eV", "eS", "cc")
        self.sems = {q: stack.enter_context(nc.semaphore(q)) for q in names}
        self.count = {q: 0 for q in names}
        self.ops = {e: [] for e in self.ENGINES}
        self.satisfied = {e: {} for e in self.ENGINES}
        self.joins = {}

    def join(self):
        toks = self.all_tokens()
        for e in self.ENGINES:
            self.joins[e] = toks

    def _wait(self, engine, deps):
        deps = list(deps) + self.joins.pop(engine, [])
        need = {}
        for d in deps:
            if d is None:
                continue
            q, v = d
            if v > need.get(q, 0):
                need[q] = v
        for q, v in sorted(need.items()):
            if self.satisfied[engine].get(q, 0) >= v:
                continue
            self.satisfied[engine][q] = v
            sem = self.sems[q]
            self.ops[engine].append(lambda eng, sem=sem, v=v: eng.wait_ge(sem, v))

    def raw_op(self, engine, fn, deps=(), inc=None, q=None):
        q = q or self.QOF[engine]
        if inc is None:
            inc = 16 if q in ("qSY", "qGP") else 1
        if q in ("qSY", "qGP", "cc") and self.count[q] > 0:
            deps = list(deps) + [(q, self.count[q])]
        self._wait(engine, deps)
        self.count[q] += inc
        tok = (q, self.count[q])
        sem = self.sems[q]
        self.ops[engine].append(
            lambda eng, fn=fn, sem=sem, inc=inc: fn(eng).then_inc(sem, inc))
        return tok

    def raw_noinc(self, engine, fn, deps=()):
        self._wait(engine, deps)
        self.ops[engine].append(fn)

    def all_tokens(self):
        return [(q, v) for q, v in self.count.items() if v > 0]

    def flush(self, block):
        for e in self.ENGINES:
            lst = self.ops[e]
            if not lst:
                continue

            def body(eng, lst=lst):
                for f in lst:
                    f(eng)
            getattr(block, e)(body)


def _deps(reads, writes, extra):
    deps = list(extra)
    for t in reads:
        deps += t.w
        if t.excl:
            deps += t.r
    for t in writes:
        deps += t.w + t.r
    return deps


def _commit(tok, reads, writes):
    for t in reads:
        t.r.append(tok)
    for t in writes:
        t.w = [tok]
        t.r = []
    return tok


def op(E, engine, fn, reads=(), writes=(), extra=(), inc=None, q=None):
    tok = E.raw_op(engine, fn, deps=_deps(reads, writes, extra), inc=inc, q=q)
    return _commit(tok, reads, writes)


def mmg(E, steps, reads=(), writes=(), extra=()):
    deps = _deps(reads, writes, extra)
    for fn in steps[:-1]:
        E.raw_noinc("tensor", fn, deps=deps)
        deps = ()
    tok = E.raw_op("tensor", steps[-1], deps=deps)
    return _commit(tok, reads, writes)


def mk_ln(nc, pstack, tag):
    s = {nm: pstack.enter_context(nc.sbuf_tensor(f"ln{tag}_{nm}", [128, 1], F32))
         for nm in ("sum", "ssq", "mu", "var", "rstd", "nbias", "t1")}
    s["dump"] = pstack.enter_context(
        nc.sbuf_tensor(f"ln{tag}_dump", [128, D], BF16))
    s["trk"] = Trk()
    return s


def ln_apply(E, sc, x_ap, x_trk, out_ap, out_trk, P=None):
    t = sc["trk"]
    op(E, "scalar", lambda s_: s_.activation(sc["dump"][:], x_ap, AF.Copy,
                                             bias=0.0, scale=1.0,
                                             accum_out=sc["sum"][:]),
       reads=[x_trk], writes=[t])
    op(E, "scalar", lambda s_: s_.activation(sc["dump"][:], x_ap, AF.Square,
                                             bias=0.0, scale=1.0,
                                             accum_out=sc["ssq"][:]),
       reads=[x_trk], writes=[t])
    op(E, "vector", lambda v: v.tensor_scalar_mul(sc["mu"][:], sc["sum"][:],
                                                  1.0 / D), reads=[t], writes=[t])
    op(E, "vector", lambda v: v.tensor_mul(sc["t1"][:], sc["mu"][:], sc["mu"][:]),
       reads=[t], writes=[t])
    op(E, "vector", lambda v: v.scalar_tensor_tensor(
        sc["var"][:], sc["ssq"][:], 1.0 / D, sc["t1"][:], ALU.mult, ALU.subtract),
        reads=[t], writes=[t])
    op(E, "scalar", lambda s_: s_.activation(sc["t1"][:], sc["var"][:], AF.Sqrt,
                                             bias=P["eps_ln"][:], scale=1.0),
       reads=[t, P["t_eps"]], writes=[t])
    op(E, "vector", lambda v: v.reciprocal(sc["rstd"][:], sc["t1"][:]),
       reads=[t], writes=[t])
    op(E, "vector", lambda v: v.scalar_tensor_tensor(
        sc["nbias"][:], sc["mu"][:], -1.0, sc["rstd"][:], ALU.mult, ALU.mult),
        reads=[t], writes=[t])
    return op(E, "vector", lambda v: v.tensor_scalar(
        out_ap, x_ap, sc["rstd"][:], sc["nbias"][:], ALU.mult, ALU.add),
        reads=[x_trk, t], writes=[out_trk])


def build_program(phases):
    nc = bass.Bass("TRN2", target_bir_lowering=False)
    io = {}

    def dram(name, shape, dtype, kind):
        io[name] = nc.dram_tensor(name, shape, dtype, kind=kind)
        return io[name]

    def stream_kind(prod, cons):
        if prod in phases and cons in phases:
            return "Internal"
        if prod in phases:
            return "ExternalOutput"
        return "ExternalInput"

    if "A" in phases:
        dram("xo_tm", [TOWN, D], F32, "ExternalInput")
        dram("xa_fm", [D, S], F32, "ExternalInput")
        dram("wqaT", [D, QLR], F32, "ExternalInput")
        dram("wqbT_own", [QLR, 8 * DQ], F32, "ExternalInput")
        dram("wkvaT", [D, KVLR + DR], F32, "ExternalInput")
        dram("wkvbT_own", [KVLR, 8 * (DN + DV)], F32, "ExternalInput")
        dram("woT", [D, D], F32, "ExternalInput")
        dram("qaln", [128, NC_Q], F32, "ExternalInput")
        dram("kvln", [128, NC_KV], F32, "ExternalInput")
        dram("cos0", [DR, S], F32, "ExternalInput")
        dram("sin0", [DR, S], F32, "ExternalInput")
        dram("ownoff", [1, 1], I32, "ExternalInput")
        dram("cqs", [NC_Q, 128, S], BF16, "Internal")
        dram("attnin", [8 * 128, S], BF16, "Internal")
        dram("attnout", [4, 512, S], BF16, "Internal")
    if "B" in phases:
        dram("gwT", [D, E_EXP], F32, "ExternalInput")
        dram("w1", [E_EXP, D, I_FF], F32, "ExternalInput")
        dram("w3", [E_EXP, D, I_FF], F32, "ExternalInput")
        dram("w2", [E_EXP, I_FF, D], F32, "ExternalInput")
        dram("sw1", [D, SH_FF], F32, "ExternalInput")
        dram("sw3", [D, SH_FF], F32, "ExternalInput")
        dram("sw2", [SH_FF, D], F32, "ExternalInput")
    if "C" in phases:
        dram("wkT1", [D, KVH * HD], F32, "ExternalInput")
        dram("wvT1", [D, KVH * HD], F32, "ExternalInput")
        dram("wqT1", [D, D], F32, "ExternalInput")
        dram("woT1", [D, D], F32, "ExternalInput")
        dram("cos1", [HD, TOWN], F32, "ExternalInput")
        dram("sin1", [HD, TOWN], F32, "ExternalInput")
        dram("smask", [128, 2, 16], F32, "ExternalInput")
        dram("summin", [2 * KVH * 128, SUMC], F32, "Internal")
        dram("summall", [8 * 2 * KVH * 128, SUMC], F32, "Internal")
    if "D" in phases:
        dram("l1wT", [D, D], F32, "ExternalInput")
        dram("l1scale", [1, D], F32, "ExternalInput")
        dram("l1bias", [1, D], F32, "ExternalInput")
    dram("ltri", [128, 128], F32, "ExternalInput")
    dram("ident", [128, 128], F32, "ExternalInput")

    if "A" in phases or "B" in phases:
        dram("x1s", [TOWN, D], F32, stream_kind("A", "B"))
    if "B" in phases or "C" in phases:
        dram("x2s", [TOWN, D], F32, stream_kind("B", "C"))
    if "C" in phases or "D" in phases:
        dram("x3s", [TOWN, D], F32, stream_kind("C", "D"))
    if "D" in phases:
        dram("y", [TOWN, D], F32, "ExternalOutput")

    tk = {name: Trk() for name in io}

    with ExitStack() as stack:
        E = Emit(nc, stack)
        ltri_bf = stack.enter_context(nc.sbuf_tensor("ltri_bf", [128, 128], BF16))
        ident_bf = stack.enter_context(nc.sbuf_tensor("ident_bf", [128, 128], BF16))
        ones_col = stack.enter_context(nc.sbuf_tensor("ones_col", [128, 1], BF16))
        ones_r32 = stack.enter_context(nc.sbuf_tensor("ones_r32", [1, 128], F32))
        ones_r65 = stack.enter_context(nc.sbuf_tensor("ones_r65", [65, 128], F32))
        eps_ln = stack.enter_context(nc.sbuf_tensor("eps_ln", [128, 1], F32))
        eps_rms = stack.enter_context(nc.sbuf_tensor("eps_rms", [128, 1], F32))
        P = {"ltri": ltri_bf, "ident": ident_bf, "ones_col": ones_col,
             "ones_r32": ones_r32, "ones_r65": ones_r65,
             "eps_ln": eps_ln, "eps_rms": eps_rms,
             "t_ltri": Trk(), "t_ident": Trk(),
             "t_oc": Trk(), "t_or": Trk(), "t_eps": Trk()}
        op(E, "gpsimd", lambda g: g.dma_start(out=ltri_bf[:], in_=io["ltri"][:, :]),
           writes=[P["t_ltri"]])
        op(E, "gpsimd", lambda g: g.dma_start(out=ident_bf[:], in_=io["ident"][:, :]),
           writes=[P["t_ident"]])
        op(E, "vector", lambda v: v.memset(ones_col[:], 1.0), writes=[P["t_oc"]])
        op(E, "vector", lambda v: v.memset(ones_r32[:], 1.0), writes=[P["t_or"]])
        op(E, "vector", lambda v: v.memset(ones_r65[:], 1.0), writes=[P["t_or"]])
        op(E, "vector", lambda v: v.memset(eps_ln[:], LN_EPS), writes=[P["t_eps"]])
        op(E, "vector", lambda v: v.memset(eps_rms[:], RMS_EPS), writes=[P["t_eps"]])

        for ph in phases:
            with ExitStack() as pstack:
                if ph == "A":
                    phase_A(E, nc, io, tk, P, pstack)
                elif ph == "B":
                    phase_B(E, nc, io, tk, P, pstack)
                elif ph == "C":
                    phase_C(E, nc, io, tk, P, pstack)
                elif ph == "D":
                    phase_D(E, nc, io, tk, P, pstack)
            E.join()

        E._wait("sync", E.all_tokens())
        with nc.Block() as block:
            E.flush(block)
    return nc


def phase_A(E, nc, io, tk, P, ap):
    ecm = ap.enter_context
    pb = [ecm(nc.psum_tensor(f"psA{i}", [128, 512], F32)) for i in range(8)]
    tpb = [Trk(True) for _ in range(8)]

    with ExitStack() as a12:
        e1 = a12.enter_context
        trig0 = e1(nc.sbuf_tensor("trig0", [128, S], F32))
        ckvg = e1(nc.sbuf_tensor("ckvg", [128, NC_KV, S], BF16))
        kpe = e1(nc.sbuf_tensor("kpe", [DR + 1, S], BF16))
        rstdq = e1(nc.sbuf_tensor("rstdq", [1, S], F32))
        rstdkv = e1(nc.sbuf_tensor("rstdkv", [1, S], F32))
        kpen2 = e1(nc.sbuf_tensor("kpen2", [1, S], F32))
        rstdkvT = e1(nc.sbuf_tensor("rstdkvT", [128, S // 128], F32))
        t_cos = Trk(); t_sin = Trk(); t_ckvg = Trk(); t_kpe = Trk()
        t_kpen2 = Trk(); t_rq = Trk(); t_rkv = Trk(); t_rkvT = Trk()

        op(E, "sync", lambda sy: sy.dma_start(out=trig0[0:DR, :],
                                              in_=io["cos0"][:, :]),
           writes=[t_cos])
        op(E, "sync", lambda sy: sy.dma_start(out=trig0[DR:2 * DR, :],
                                              in_=io["sin0"][:, :]),
           writes=[t_sin])
        op(E, "vector", lambda v: v.memset(kpe[DR:DR + 1, :], 1.0),
           writes=[t_kpe])
        t_cqg = Trk()
        with ExitStack() as a1:
            ee = a1.enter_context
            wqa_sb = ee(nc.sbuf_tensor("wqa_sb", [128, NC_I, QLR], BF16))
            qaln_sb = ee(nc.sbuf_tensor("qaln_sb", [128, NC_Q], F32))
            xa2 = [ee(nc.sbuf_tensor(f"xaf{i}", [128, NC_I, 512], BF16))
                   for i in range(2)]
            sqb = [ee(nc.sbuf_tensor(f"sqb{i}", [128, 512], BF16))
                   for i in range(2)]
            cqstage = [ee(nc.sbuf_tensor(f"cqstage{i}", [128, 512], BF16))
                       for i in range(2)]
            rowt = ee(nc.sbuf_tensor("rowt", [1, 512], F32))
            t_wqa = Trk(); t_qaln = Trk(); t_xa2 = [Trk(), Trk()]
            t_sqb = [Trk(), Trk()]; t_cqst = [Trk(), Trk()]; t_rowt = Trk()

            op(E, "gpsimd", lambda g: g.dma_start(
                out=wqa_sb[:],
                in_=io["wqaT"].ap().rearrange("(c p) f -> p c f", p=128)),
               writes=[t_wqa])
            op(E, "sync", lambda sy: sy.dma_start(out=qaln_sb[:],
                                                  in_=io["qaln"][:, :]),
               writes=[t_qaln])
            for it in range(A_NT):
                cols = slice(it * 512, (it + 1) * 512)
                xa, t_xa = xa2[it % 2], t_xa2[it % 2]
                op(E, "gpsimd", lambda g, cols=cols, xa=xa: g.dma_start(
                    out=xa[:],
                    in_=io["xa_fm"].ap()[:, cols]
                        .rearrange("(c p) t -> p c t", p=128)),
                   writes=[t_xa])
                for cc in range(A_MM if isinstance(A_MM, int) and not isinstance(A_MM, bool) else (NC_Q if A_MM else 0)):
                    bk, tb = pb[cc % 4], tpb[cc % 4]
                    mmg(E, [
                        (lambda t, ic=ic, cc=cc, bk=bk, xa=xa:
                         t.matmul(bk[:], wqa_sb[:, ic, cc * 128:(cc + 1) * 128],
                                  xa[:, ic, :], start=(ic == 0),
                                  stop=(ic == NC_I - 1)))
                        for ic in range(NC_I)],
                        reads=[t_wqa, t_xa], writes=[tb])
                    sb, tsb = sqb[cc % 2], t_sqb[cc % 2]
                    if A_SQ:
                        op(E, "scalar", lambda s_, bk=bk, sb=sb: s_.activation(
                            sb[:], bk[:], AF.Square, bias=0.0, scale=1.0),
                           reads=[tb], writes=[tsb])
                    if A_RMS:
                        mmg(E, [(lambda t, sb=sb, cc=cc: t.matmul(
                            pb[6][0:1, :], P["ones_col"][:], sb[:],
                            start=(cc == 0), stop=(cc == NC_Q - 1),
                            skip_group_check=True))],
                            reads=[tsb, P["t_oc"]], writes=[tpb[6]])
                    cst, t_cst = cqstage[cc % 2], t_cqst[cc % 2]
                    if A_TS:
                        op(E, "vector", lambda v, bk=bk, cst=cst, cc=cc:
                           v.tensor_scalar(cst[:], bk[:],
                                           qaln_sb[:, cc:cc + 1], None,
                                           ALU.mult),
                           reads=[tb, t_qaln], writes=[t_cst])
                    if A_CQS:
                        op(E, "sync", lambda sy, cc=cc, cols=cols, cst=cst:
                           sy.dma_start(out=io["cqs"].ap()[cc, :, cols],
                                        in_=cst[:]),
                           reads=[t_cst], writes=[tk["cqs"]])
                if A_RMS:
                    op(E, "scalar", lambda s_: s_.activation(
                        rowt[:], pb[6][0:1, :], AF.Sqrt,
                        bias=P["eps_rms"][0:1, :], scale=1.0 / QLR),
                       reads=[tpb[6], P["t_eps"]], writes=[t_rowt])
                    op(E, "vector", lambda v, cols=cols: v.reciprocal(
                        rstdq[0:1, cols], rowt[:]), reads=[t_rowt], writes=[t_rq])

        E.join()
        with ExitStack() as a1:
            ee = a1.enter_context
            wkva_sb = ee(nc.sbuf_tensor("wkva_sb", [128, NC_I, KVLR + DR], BF16))
            kvln_sb = ee(nc.sbuf_tensor("kvln_sb", [128, NC_KV], F32))
            xb2 = [ee(nc.sbuf_tensor(f"xbf{i}", [128, NC_I, 512], BF16))
                   for i in range(2)]
            sqb = [ee(nc.sbuf_tensor(f"sqd{i}", [128, 512], BF16))
                   for i in range(2)]
            kro1 = ee(nc.sbuf_tensor("kro1", [DR, 512], F32))
            kro2 = ee(nc.sbuf_tensor("kro2", [DR, 512], F32))
            rowtv = ee(nc.sbuf_tensor("rowtb", [1, 512], F32))
            t_wkva = Trk(); t_kvln = Trk(); t_xb2 = [Trk(), Trk()]
            t_sqb = [Trk(), Trk()]; t_kro1 = Trk(); t_kro2 = Trk(); t_rowtv = Trk()

            op(E, "gpsimd", lambda g: g.dma_start(
                out=wkva_sb[:],
                in_=io["wkvaT"].ap().rearrange("(c p) f -> p c f", p=128)),
               writes=[t_wkva])
            op(E, "sync", lambda sy: sy.dma_start(out=kvln_sb[:],
                                                  in_=io["kvln"][:, :]),
               writes=[t_kvln])
            for it in range(S // 512 if A_A1B else 0):
                cols = slice(it * 512, (it + 1) * 512)
                xb, t_xb = xb2[it % 2], t_xb2[it % 2]
                op(E, "gpsimd", lambda g, cols=cols, xb=xb: g.dma_start(
                    out=xb[:],
                    in_=io["xa_fm"].ap()[:, cols]
                        .rearrange("(c p) t -> p c t", p=128)),
                   writes=[t_xb])
                for cc in range(NC_KV):
                    bk, tb = pb[cc], tpb[cc]
                    mmg(E, [
                        (lambda t, ic=ic, cc=cc, bk=bk, xb=xb:
                         t.matmul(bk[:], wkva_sb[:, ic, cc * 128:(cc + 1) * 128],
                                  xb[:, ic, :], start=(ic == 0),
                                  stop=(ic == NC_I - 1)))
                        for ic in range(NC_I)],
                        reads=[t_wkva, t_xb], writes=[tb])
                    sb, tsb = sqb[cc % 2], t_sqb[cc % 2]
                    op(E, "scalar", lambda s_, bk=bk, sb=sb: s_.activation(
                        sb[:], bk[:], AF.Square, bias=0.0, scale=1.0),
                       reads=[tb], writes=[tsb])
                    mmg(E, [(lambda t, sb=sb, cc=cc: t.matmul(
                        pb[6][0:1, :], P["ones_col"][:], sb[:],
                        start=(cc == 0), stop=(cc == NC_KV - 1),
                        skip_group_check=True))],
                        reads=[tsb, P["t_oc"]], writes=[tpb[6]])
                    op(E, "vector", lambda v, bk=bk, cc=cc, cols=cols:
                       v.tensor_scalar(ckvg[:, cc, cols], bk[:],
                                       kvln_sb[:, cc:cc + 1], None, ALU.mult),
                       reads=[tb, t_kvln], writes=[t_ckvg])
                op(E, "scalar", lambda s_: s_.activation(
                    rowtv[:], pb[6][0:1, :], AF.Sqrt, bias=P["eps_rms"][0:1, :],
                    scale=1.0 / KVLR),
                   reads=[tpb[6], P["t_eps"]], writes=[t_rowtv])
                op(E, "vector", lambda v, cols=cols: v.reciprocal(
                    rstdkv[0:1, cols], rowtv[:]), reads=[t_rowtv], writes=[t_rkv])
                mmg(E, [
                    (lambda t, ic=ic, xb=xb:
                     t.matmul(pb[7][0:DR, :], wkva_sb[:, ic, KVLR:KVLR + DR],
                              xb[:, ic, :], start=(ic == 0),
                              stop=(ic == NC_I - 1)))
                    for ic in range(NC_I)],
                    reads=[t_wkva, t_xb], writes=[tpb[7]])
                op(E, "vector", lambda v, cols=cols: v.tensor_mul(
                    kro1[:], pb[7][0:DR, :], trig0[0:DR, cols]),
                   reads=[tpb[7], t_cos], writes=[t_kro1])
                op(E, "vector", lambda v, cols=cols: v.scalar_tensor_tensor(
                    kro2[0:32, :], pb[7][32:64, :], -1.0, trig0[DR:DR + 32, cols],
                    ALU.mult, ALU.mult), reads=[tpb[7], t_sin], writes=[t_kro2])
                op(E, "vector", lambda v, cols=cols: v.scalar_tensor_tensor(
                    kro2[32:64, :], pb[7][0:32, :], 1.0, trig0[DR + 32:2 * DR, cols],
                    ALU.mult, ALU.mult), reads=[tpb[7], t_sin], writes=[t_kro2])
                op(E, "vector", lambda v: v.tensor_add(kro1[:], kro1[:],
                                                       kro2[:]),
                   reads=[t_kro2], writes=[t_kro1])
                op(E, "vector", lambda v, cols=cols: v.tensor_copy(
                    kpe[0:DR, cols], kro1[:]), reads=[t_kro1], writes=[t_kpe])
                sb, tsb = sqb[0], t_sqb[0]
                op(E, "scalar", lambda s_, sb=sb: s_.activation(
                    sb[0:DR, :], kro1[:], AF.Square, bias=0.0, scale=1.0),
                   reads=[t_kro1], writes=[tsb])
                mmg(E, [(lambda t, sb=sb: t.matmul(
                    pb[6][0:1, :], P["ones_col"][0:DR, :], sb[0:DR, :],
                    start=True, stop=True))],
                    reads=[tsb, P["t_oc"]], writes=[tpb[6]])
                op(E, "vector", lambda v, cols=cols: v.tensor_copy(
                    kpen2[0:1, cols], pb[6][0:1, :]), reads=[tpb[6]],
                   writes=[t_kpen2])

            for tc in range(S // 128 if A_COL else 0):
                mmg(E, [(lambda t, tc=tc: t.matmul(
                    pb[5][:, 0:1], rstdkv[0:1, tc * 128:(tc + 1) * 128],
                    P["ones_r32"][0:1, 0:1], start=True, stop=True))],
                    reads=[t_rkv, P["t_or"]], writes=[tpb[5]])
                op(E, "vector", lambda v, tc=tc: v.tensor_copy(
                    rstdkvT[:, tc:tc + 1], pb[5][:, 0:1]),
                   reads=[tpb[5]], writes=[t_rkvT])

        E.join()
        with ExitStack() as a2:
            e2 = a2.enter_context
            knope = e2(nc.sbuf_tensor("knope", [128, S], BF16))
            vaug = e2(nc.sbuf_tensor("vaug", [128, S // 128, DV + 1], BF16))
            q1 = e2(nc.sbuf_tensor("q1", [128, 512], BF16))
            q2 = e2(nc.sbuf_tensor("q2", [DR + 1, 512], BF16))
            maxk = e2(nc.sbuf_tensor("maxk", [1, 2], F32))
            kn2t = e2(nc.sbuf_tensor("kn2t", [1, 512], F32))
            qnt = e2(nc.sbuf_tensor("qnt", [1, 512], F32))
            mk1 = e2(nc.sbuf_tensor("mk1", [1, 1], F32))
            cqt = [e2(nc.sbuf_tensor(f"cqt{i}", [128, NC_Q, 512], BF16))
                   for i in range(2)]
            t_cqt = [Trk(), Trk()]
            wqb_sl = e2(nc.sbuf_tensor("wqb_sl", [128, NC_Q, DQ], BF16))
            wkvb_sl = e2(nc.sbuf_tensor("wkvb_sl", [128, NC_KV, DN + DV], BF16))
            pt = [e2(nc.sbuf_tensor(f"pt{i}", [128, 512], BF16)) for i in range(3)]
            sqb = [e2(nc.sbuf_tensor(f"sqc{i}", [128, 512], BF16)) for i in range(2)]
            rb = e2(nc.sbuf_tensor("rb", [128, 512], F32))
            qro1 = e2(nc.sbuf_tensor("qro1", [DR, 512], F32))
            qro2 = e2(nc.sbuf_tensor("qro2", [DR, 512], F32))
            oacc = e2(nc.sbuf_tensor("oacc", [128, 512], F32))
            lacc = e2(nc.sbuf_tensor("lacc", [1, 512], F32))
            m_old = e2(nc.sbuf_tensor("m_old", [1, 512], F32))
            m_new = e2(nc.sbuf_tensor("m_new", [1, 512], F32))
            arow = e2(nc.sbuf_tensor("arow", [1, 512], F32))
            rlrow = e2(nc.sbuf_tensor("rlrow", [1, 512], F32))
            rlb = e2(nc.sbuf_tensor("rlb", [128, 512], F32))
            attn_t = e2(nc.sbuf_tensor("attn_t", [128, 512], BF16))
            rowt2 = e2(nc.sbuf_tensor("rowt2", [1, 512], F32))
            t_knope = Trk(); t_vaug = Trk(); t_q1 = Trk(); t_q2 = Trk()
            t_kn2t = Trk(); t_qnt = Trk(); t_mk1 = Trk()
            t_maxk = Trk()
            t_wqb = Trk(); t_wkvb = Trk()
            t_pt = [Trk() for _ in range(3)]
            t_sqb = [Trk(), Trk()]
            t_rb = Trk(); t_qro1 = Trk(); t_qro2 = Trk()
            t_oacc = Trk(); t_lacc = Trk(); t_mo = Trk(); t_mn = Trk()
            t_ar = Trk(); t_rl = Trk(); t_rlb = Trk(); t_at = Trk()
            t_rowt2 = Trk()

            op(E, "vector", lambda v: v.memset(vaug.ap()[:, :, DV:DV + 1], 1.0),
               writes=[t_vaug])

            for hl in range(8 if A_STAGES >= 2 else 0):
                op(E, "gpsimd", lambda g, hl=hl: g.dma_start(
                    out=wqb_sl[:],
                    in_=io["wqbT_own"].ap()[:, hl * DQ:(hl + 1) * DQ]
                        .rearrange("(c p) f -> p c f", p=128)),
                   writes=[t_wqb])
                op(E, "gpsimd", lambda g, hl=hl: g.dma_start(
                    out=wkvb_sl[:],
                    in_=io["wkvbT_own"].ap()[:, hl * 256:(hl + 1) * 256]
                        .rearrange("(c p) f -> p c f", p=128)),
                   writes=[t_wkvb])
                for it in range(S // 512):
                    cols = slice(it * 512, (it + 1) * 512)
                    bk, tb = pb[it % 2], tpb[it % 2]
                    mmg(E, [
                        (lambda t, cc=cc, bk=bk, cols=cols:
                         t.matmul(bk[:], wkvb_sl[:, cc, 0:DN], ckvg[:, cc, cols],
                                  start=(cc == 0), stop=(cc == NC_KV - 1)))
                        for cc in range(NC_KV)],
                        reads=[t_wkvb, t_ckvg], writes=[tb])
                    sb, tsb = sqb[it % 2], t_sqb[it % 2]
                    op(E, "scalar", lambda s_, bk=bk, sb=sb: s_.activation(
                        sb[:], bk[:], AF.Square, bias=0.0, scale=1.0),
                       reads=[tb], writes=[tsb])
                    mmg(E, [(lambda t, sb=sb: t.matmul(
                        pb[6][0:1, :], P["ones_col"][:], sb[:],
                        start=True, stop=True))],
                        reads=[tsb, P["t_oc"]], writes=[tpb[6]])
                    op(E, "vector", lambda v, cols=cols: v.tensor_mul(
                        rowt2[:], rstdkv[0:1, cols], rstdkv[0:1, cols]),
                       reads=[t_rkv], writes=[t_rowt2])
                    op(E, "vector", lambda v: v.tensor_mul(
                        rowt2[:], rowt2[:], pb[6][0:1, :]),
                       reads=[tpb[6]], writes=[t_rowt2])
                    op(E, "vector", lambda v, cols=cols: v.tensor_add(
                        kn2t[:], rowt2[:], kpen2[0:1, cols]),
                       reads=[t_rowt2, t_kpen2], writes=[t_kn2t])
                    op(E, "vector", lambda v: v.reduce_max(
                        mk1[:], kn2t[:], axis=AXX),
                       reads=[t_kn2t], writes=[t_mk1])
                    half = it // 4
                    if it % 4 == 0:
                        op(E, "vector", lambda v, half=half: v.tensor_copy(
                            maxk[0:1, half:half + 1], mk1[:]),
                           reads=[t_mk1], writes=[t_maxk])
                    else:
                        op(E, "vector", lambda v, half=half: v.tensor_max(
                            maxk[0:1, half:half + 1], maxk[0:1, half:half + 1],
                            mk1[:]),
                           reads=[t_mk1], writes=[t_maxk])
                    mmg(E, [(lambda t, cols=cols: t.matmul(
                        pb[7][:, :], P["ones_r32"][:], rstdkv[0:1, cols],
                        start=True, stop=True))],
                        reads=[t_rkv, P["t_or"]], writes=[tpb[7]])
                    op(E, "vector", lambda v: v.tensor_copy(rb[:], pb[7][:, :]),
                       reads=[tpb[7]], writes=[t_rb])
                    op(E, "vector", lambda v, bk=bk, cols=cols: v.tensor_mul(
                        knope[:, cols], bk[:], rb[:]),
                       reads=[tb, t_rb], writes=[t_knope])
                op(E, "scalar", lambda s_: s_.activation(
                    maxk[:], maxk[:], AF.Sqrt, bias=0.0, scale=1.0),
                   reads=[t_maxk], writes=[t_maxk])
                for tc in range(S // 128):
                    bk, tb = pb[2 + (tc % 2)], tpb[2 + (tc % 2)]
                    mmg(E, [
                        (lambda t, cc=cc, bk=bk, tc=tc:
                         t.matmul(bk[:, 0:DV],
                                  ckvg[:, cc, tc * 128:(tc + 1) * 128],
                                  wkvb_sl[:, cc, DN:DN + DV],
                                  start=(cc == 0), stop=(cc == NC_KV - 1)))
                        for cc in range(NC_KV)],
                        reads=[t_wkvb, t_ckvg], writes=[tb])
                    op(E, "vector", lambda v, bk=bk, tc=tc: v.tensor_scalar(
                        vaug[:, tc, 0:DV], bk[:, 0:DV], rstdkvT[:, tc:tc + 1],
                        None, ALU.mult),
                       reads=[tb, t_rkvT], writes=[t_vaug])
                for mi in range(8):
                    g0 = mi * 512
                    cols = slice(g0, g0 + 512)
                    cqx, t_cqx = cqt[mi % 2], t_cqt[mi % 2]
                    op(E, "gpsimd", lambda g, cols=cols, cqx=cqx: g.dma_start(
                        out=cqx[:], in_=io["cqs"].ap()[:, :, cols]
                        .rearrange("c p t -> p c t")),
                       reads=[tk["cqs"]], writes=[t_cqx])
                    mmg(E, [(lambda t, cols=cols: t.matmul(
                        pb[7][:, :], P["ones_r32"][:], rstdq[0:1, cols],
                        start=True, stop=True))],
                        reads=[t_rq, P["t_or"]], writes=[tpb[7]])
                    op(E, "vector", lambda v: v.tensor_copy(rb[:], pb[7][:, :]),
                       reads=[tpb[7]], writes=[t_rb])
                    bk, tb = pb[0], tpb[0]
                    mmg(E, [
                        (lambda t, cc=cc, bk=bk, cqx=cqx:
                         t.matmul(bk[:], wqb_sl[:, cc, 0:DN], cqx[:, cc, :],
                                  start=(cc == 0), stop=(cc == NC_Q - 1)))
                        for cc in range(NC_Q)],
                        reads=[t_wqb, t_cqx], writes=[tb])
                    op(E, "vector", lambda v, bk=bk, cols=cols:
                       v.scalar_tensor_tensor(q1[:], bk[:], SCALE, rb[:],
                                              ALU.mult, ALU.mult),
                       reads=[tb, t_rb], writes=[t_q1])
                    bk2, tb2 = pb[1], tpb[1]
                    mmg(E, [
                        (lambda t, cc=cc, bk2=bk2, cqx=cqx:
                         t.matmul(bk2[0:DR, :], wqb_sl[:, cc, DN:DQ],
                                  cqx[:, cc, :],
                                  start=(cc == 0), stop=(cc == NC_Q - 1)))
                        for cc in range(NC_Q)],
                        reads=[t_wqb, t_cqx], writes=[tb2])
                    op(E, "vector", lambda v, bk2=bk2, cols=cols: v.tensor_mul(
                        qro1[:], bk2[0:DR, :], trig0[0:DR, cols]),
                       reads=[tb2, t_cos], writes=[t_qro1])
                    op(E, "vector", lambda v, bk2=bk2, cols=cols:
                       v.scalar_tensor_tensor(qro2[0:32, :], bk2[32:64, :], -1.0,
                                              trig0[DR:DR + 32, cols], ALU.mult, ALU.mult),
                       reads=[tb2, t_sin], writes=[t_qro2])
                    op(E, "vector", lambda v, bk2=bk2, cols=cols:
                       v.scalar_tensor_tensor(qro2[32:64, :], bk2[0:32, :], 1.0,
                                              trig0[DR + 32:2 * DR, cols], ALU.mult, ALU.mult),
                       reads=[tb2, t_sin], writes=[t_qro2])
                    op(E, "vector", lambda v: v.tensor_add(qro1[:], qro1[:],
                                                           qro2[:]),
                       reads=[t_qro2], writes=[t_qro1])
                    op(E, "vector", lambda v: v.scalar_tensor_tensor(
                        q2[0:DR, :], qro1[:], SCALE, rb[0:DR, :],
                        ALU.mult, ALU.mult),
                       reads=[t_qro1, t_rb], writes=[t_q2])
                    sb, tsb = sqb[0], t_sqb[0]
                    op(E, "scalar", lambda s_, cols=cols, sb=sb: s_.activation(
                        sb[:], q1[:], AF.Square, bias=0.0, scale=1.0),
                       reads=[t_q1], writes=[tsb])
                    mmg(E, [(lambda t, sb=sb: t.matmul(
                        pb[6][0:1, :], P["ones_col"][:], sb[:],
                        start=True, stop=False, skip_group_check=True))],
                        reads=[tsb, P["t_oc"]], writes=[tpb[6]])
                    sb2, tsb2 = sqb[1], t_sqb[1]
                    op(E, "scalar", lambda s_, cols=cols, sb2=sb2: s_.activation(
                        sb2[0:DR, :], q2[0:DR, :], AF.Square, bias=0.0,
                        scale=1.0),
                       reads=[t_q2], writes=[tsb2])
                    mmg(E, [(lambda t, sb2=sb2: t.matmul(
                        pb[6][0:1, :], P["ones_col"][0:DR, :], sb2[0:DR, :],
                        start=False, stop=True, skip_group_check=True))],
                        reads=[tsb2, P["t_oc"]], writes=[tpb[6]])
                    op(E, "scalar", lambda s_, cols=cols: s_.activation(
                        qnt[:], pb[6][0:1, :], AF.Sqrt, bias=0.0,
                        scale=1.0),
                       reads=[tpb[6]], writes=[t_qnt])

                    halves = ([(0, g0 + 512)] if g0 < 2048
                              else [(0, 2048), (2048, g0 + 512)])
                    qcols = slice(g0, g0 + 512)
                    for ih, (h0, h1) in enumerate(halves):
                        two = len(halves) == 2
                        mrow_t = m_new if (two and ih == 1) else m_old
                        mrow_trk = t_mn if (two and ih == 1) else t_mo
                        op(E, "vector",
                           lambda v, qcols=qcols, ih=ih, mrow_t=mrow_t:
                           v.tensor_scalar(mrow_t[:], qnt[:],
                                           maxk[0:1, ih:ih + 1], 0.05,
                                           ALU.mult, ALU.add),
                           reads=[t_qnt, t_maxk], writes=[mrow_trk])
                        if two and ih == 1:
                            op(E, "vector", lambda v: v.tensor_max(
                                m_new[:], m_new[:], m_old[:]),
                               reads=[t_mo], writes=[t_mn])
                            op(E, "vector", lambda v: v.tensor_sub(
                                arow[:], m_old[:], m_new[:]),
                               reads=[t_mo, t_mn], writes=[t_ar])
                            op(E, "scalar", lambda s_: s_.activation(
                                arow[:], arow[:], AF.Exp, bias=0.0, scale=1.0),
                               reads=[t_ar], writes=[t_ar])
                        op(E, "vector", lambda v, qcols=qcols, mrow_t=mrow_t:
                           v.tensor_scalar_mul(q2[DR:DR + 1, :], mrow_t[:],
                                               -1.0),
                           reads=[mrow_trk], writes=[t_q2])
                        nkb = (h1 - h0) // 128
                        for ik in range(nkb):
                            kb = h0 + ik * 128
                            kcols = slice(kb, kb + 128)
                            stb, t_stb = pb[4 + (ik % 2)], tpb[4 + (ik % 2)]
                            mmg(E, [
                                (lambda t, kcols=kcols, qcols=qcols, stb=stb:
                                 t.matmul(stb[:], knope[:, kcols], q1[:],
                                          start=True, stop=False)),
                                (lambda t, kcols=kcols, qcols=qcols, stb=stb:
                                 t.matmul(stb[:], kpe[:, kcols], q2[:],
                                          start=False, stop=True)),
                            ], reads=[t_knope, t_kpe, t_q1, t_q2], writes=[t_stb])
                            ptb, t_ptb = pt[ik % 3], t_pt[ik % 3]
                            op(E, "scalar", lambda s_, stb=stb, ptb=ptb:
                               s_.activation(ptb[:], stb[:], AF.Exp, bias=0.0,
                                             scale=1.0),
                               reads=[t_stb], writes=[t_ptb])
                            qlo = 0
                            if kb >= g0:
                                qlo = ((kb - g0) // 128) * 128
                                op(E, "vector", lambda v, ptb=ptb, qlo=qlo:
                                   v.tensor_mul(ptb[:, qlo:qlo + 128],
                                                ptb[:, qlo:qlo + 128],
                                                P["ltri"][:]),
                                   reads=[P["t_ltri"]], writes=[t_ptb])
                            kb_i = kb // 128
                            mmg(E, [(lambda t, ptb=ptb, kb_i=kb_i, qlo=qlo,
                                     ik=ik, nkb=nkb: t.matmul(
                                         pb[2][:, qlo:512], vaug[:, kb_i, 0:DV],
                                         ptb[:, qlo:512], start=(ik == 0),
                                         stop=(ik == nkb - 1),
                                         skip_group_check=True))],
                                reads=[t_ptb, t_vaug], writes=[tpb[2]])
                            mmg(E, [(lambda t, ptb=ptb, qlo=qlo, ik=ik, nkb=nkb:
                                     t.matmul(pb[3][0:1, qlo:512],
                                              P["ones_col"][:],
                                              ptb[:, qlo:512], start=(ik == 0),
                                              stop=(ik == nkb - 1),
                                              skip_group_check=True))],
                                reads=[t_ptb, P["t_oc"]], writes=[tpb[3]])
                        if two and ih == 0:
                            op(E, "vector", lambda v: v.tensor_copy(
                                oacc[:], pb[2][:, :]),
                               reads=[tpb[2]], writes=[t_oacc])
                            op(E, "vector", lambda v: v.tensor_copy(
                                lacc[:], pb[3][0:1, :]),
                               reads=[tpb[3]], writes=[t_lacc])
                            continue
                        if two:
                            mmg(E, [(lambda t: t.matmul(
                                pb[7][:, :], P["ones_r32"][:], arow[:],
                                start=True, stop=True))],
                                reads=[t_ar, P["t_or"]], writes=[tpb[7]])
                            op(E, "vector", lambda v: v.tensor_copy(
                                rb[:], pb[7][:, :]), reads=[tpb[7]],
                               writes=[t_rb])
                            op(E, "vector", lambda v: v.tensor_mul(
                                oacc[:], oacc[:], rb[:]),
                               reads=[t_rb], writes=[t_oacc])
                            op(E, "vector", lambda v: v.tensor_add(
                                oacc[:], oacc[:], pb[2][:, :]),
                               reads=[tpb[2]], writes=[t_oacc])
                            op(E, "vector", lambda v: v.tensor_mul(
                                lacc[:], lacc[:], arow[:]),
                               reads=[t_ar], writes=[t_lacc])
                            op(E, "vector", lambda v: v.tensor_add(
                                lacc[:], lacc[:], pb[3][0:1, :]),
                               reads=[tpb[3]], writes=[t_lacc])
                            op(E, "vector", lambda v: v.reciprocal(
                                rlrow[:], lacc[:]), reads=[t_lacc],
                               writes=[t_rl])
                        else:
                            op(E, "vector", lambda v: v.reciprocal(
                                rlrow[:], pb[3][0:1, :]),
                               reads=[tpb[3]], writes=[t_rl])
                        mmg(E, [(lambda t: t.matmul(
                            pb[7][:, :], P["ones_r32"][:], rlrow[:],
                            start=True, stop=True))],
                            reads=[t_rl, P["t_or"]], writes=[tpb[7]])
                        op(E, "vector", lambda v: v.tensor_copy(
                            rlb[:], pb[7][:, :]), reads=[tpb[7]], writes=[t_rlb])
                        if two:
                            op(E, "vector", lambda v: v.tensor_mul(
                                attn_t[:], oacc[:], rlb[:]),
                               reads=[t_oacc, t_rlb], writes=[t_at])
                        else:
                            op(E, "vector", lambda v: v.tensor_mul(
                                attn_t[:], pb[2][:, :], rlb[:]),
                               reads=[tpb[2], t_rlb], writes=[t_at])
                        op(E, "sync", lambda sy, hl=hl, qcols=qcols: sy.dma_start(
                            out=io["attnin"].ap()[hl * 128:(hl + 1) * 128, qcols],
                            in_=attn_t[:]),
                           reads=[t_at], writes=[tk["attnin"]])
                if hl % 2 == 1 and A_DO_AG:
                    k = hl // 2
                    op(E, "gpsimd", lambda g, k=k: g.collective_compute(
                        "AllGather", ALU.bypass,
                        replica_groups=[[0, 1], [2, 3], [4, 5], [6, 7]],
                        ins=[io["attnin"].ap()[k * 256:(k + 1) * 256, :].opt()],
                        outs=[io["attnout"].ap()[k].opt()]),
                       reads=[tk["attnin"]], writes=[tk["attnout"]], inc=1,
                       q="cc")

    E.join()
    if A_STAGES < 3:
        return
    with ExitStack() as a3:
        e3 = a3.enter_context
        attn_all = e3(nc.sbuf_tensor("attn_all", [128, H, TOWN], BF16))
        wo_sb = e3(nc.sbuf_tensor("wo_sb", [128, NC_I, D], BF16))
        x0t = e3(nc.sbuf_tensor("x0t", [128, D], F32))
        x1t = e3(nc.sbuf_tensor("x1t", [128, D], F32))
        x1o = e3(nc.sbuf_tensor("x1o", [128, D], F32))
        osb = e3(nc.sbuf_tensor("osb", [1, 1], I32))
        lns = mk_ln(nc, a3, "A")
        t_aall = Trk(); t_wo = Trk(); t_x0 = Trk(); t_x1 = Trk(); t_x1o = Trk()
        t_osb = Trk()

        op(E, "gpsimd", lambda g: g.dma_start(
            out=wo_sb[:], in_=io["woT"].ap().rearrange("(c p) f -> p c f", p=128)),
           writes=[t_wo])
        op(E, "gpsimd", lambda g: g.dma_start(out=osb[:], in_=io["ownoff"][:, :]),
           writes=[t_osb])
        rbase = a3.enter_context(nc.gpsimd.register("rbase"))
        rdyn = a3.enter_context(nc.gpsimd.register("rdyn"))
        E.raw_noinc("gpsimd", lambda g: g.reg_load(rbase, osb[0:1, 0:1]),
                    deps=_deps([t_osb], [], []))
        for hg in range(H):
            rp, k, loc = hg // 8, (hg % 8) // 2, hg % 2
            base = (k * 512 + rp * 256 + loc * 128) * S
            if A_DYN_OFF:
                E.raw_noinc("gpsimd",
                            lambda g, base=base: g.reg_add(rdyn, rbase, base))
                op(E, "gpsimd", lambda g, hg=hg: g.dma_start(
                    out=attn_all[:, hg, :],
                    in_=bass.AP(io["attnout"], rdyn, [[S, 128], [1, TOWN]])),
                   reads=[tk["attnout"]], writes=[t_aall])
            else:
                src = (io["attnout"] if A_DO_AG else io["attnin"])
                hl = hg % 8
                rows = ((k * 512 + rp * 256 + loc * 128) if A_DO_AG
                        else hl * 128)
                op(E, "gpsimd", lambda g, hg=hg, src=src, rows=rows:
                   g.dma_start(
                       out=attn_all[:, hg, :],
                       in_=bass.AP(src, rows * S, [[S, 128], [1, TOWN]])),
                   reads=[tk["attnout" if A_DO_AG else "attnin"]],
                   writes=[t_aall])
        for tc in range(TOWN // 128):
            trows = slice(tc * 128, (tc + 1) * 128)
            op(E, "sync", lambda sy, trows=trows: sy.dma_start(
                out=x0t[:], in_=io["xo_tm"].ap()[trows, :]),
               reads=[], writes=[t_x0])
            for ft in range(4):
                fcols = slice(ft * 512, (ft + 1) * 512)
                bk, tb = pb[ft % 2], tpb[ft % 2]
                mmg(E, [
                    (lambda t, hgi=hgi, tc=tc, ft=ft, bk=bk:
                     t.matmul(bk[:], attn_all[:, hgi, tc * 128:(tc + 1) * 128],
                              wo_sb[:, hgi, ft * 512:(ft + 1) * 512],
                              start=(hgi == 0), stop=(hgi == H - 1)))
                    for hgi in range(H)],
                    reads=[t_aall, t_wo], writes=[tb])
                op(E, "vector", lambda v, bk=bk, fcols=fcols: v.tensor_add(
                    x1t[:, fcols], bk[:], x0t[:, fcols]),
                   reads=[tb, t_x0], writes=[t_x1])
            ln_apply(E, lns, x1t[:], t_x1, x1o[:], t_x1o, P)
            op(E, "sync", lambda sy, trows=trows: sy.dma_start(
                out=io["x1s"].ap()[trows, :], in_=x1o[:]),
               reads=[t_x1o], writes=[tk["x1s"]])


def phase_B(E, nc, io, tk, P, ap):
    ecm = ap.enter_context
    pb = [ecm(nc.psum_tensor(f"psB{i}", [128, 512], F32)) for i in range(6)]
    tpb = [Trk(True) for _ in range(6)]
    pbt = ecm(nc.psum_tensor("psBt", [128, 128], BF16))
    t_pbt = Trk(True)

    x1fmh = ecm(nc.sbuf_tensor("x1fmh", [128, NC_I, 1024], BF16))
    y_acc = ecm(nc.sbuf_tensor("y_acc", [128, 8, D], F32))
    u_bf = ecm(nc.sbuf_tensor("u_bf", [128, NJ, 1024], BF16))
    w2_sb = ecm(nc.sbuf_tensor("w2_sb", [128, NJ, D], BF16))
    w1_sl = [ecm(nc.sbuf_tensor(f"w1_sl{i}", [128, NC_I, 128], BF16))
             for i in range(2)]
    w3_sl = [ecm(nc.sbuf_tensor(f"w3_sl{i}", [128, NC_I, 128], BF16))
             for i in range(2)]
    gw_sb = ecm(nc.sbuf_tensor("gw_sb", [128, NC_I, E_EXP], BF16))
    wgt = ecm(nc.sbuf_tensor("wgt", [128, 16, E_EXP], F32))
    silu_f = ecm(nc.sbuf_tensor("silu_f", [128, 512], F32))
    x1t = ecm(nc.sbuf_tensor("x1tB", [128, D], F32))
    x1bf = ecm(nc.sbuf_tensor("x1bfB", [128, D], BF16))
    x2t = ecm(nc.sbuf_tensor("x2tB", [128, D], F32))
    g8 = ecm(nc.sbuf_tensor("g8", [128, E_EXP], F32))
    gcol = [ecm(nc.sbuf_tensor(f"gcol{i}", [128, 1], F32)) for i in range(4)]
    lns = mk_ln(nc, ap, "B")
    t_x1fmh = Trk(); t_yacc = Trk(); t_u = Trk(); t_w2 = Trk()
    t_w1s = [Trk(), Trk()]; t_w3s = [Trk(), Trk()]
    t_gw = Trk(); t_wgt = Trk(); t_silu = Trk()
    t_x1t = Trk(); t_x1bf = Trk(); t_x2t = Trk()
    t_g8 = Trk(); t_gc = [Trk() for _ in range(4)]

    op(E, "gpsimd", lambda g: g.dma_start(
        out=gw_sb[:], in_=io["gwT"].ap().rearrange("(c p) f -> p c f", p=128)),
       writes=[t_gw])

    for half in range(2):
        hoff = half * 1024
        for tci in range(8):
            tc = half * 8 + tci
            trows = slice(tc * 128, (tc + 1) * 128)
            op(E, "sync", lambda sy, trows=trows: sy.dma_start(
                out=x1t[:], in_=io["x1s"].ap()[trows, :]),
               reads=[tk["x1s"]], writes=[t_x1t])
            op(E, "scalar", lambda s_: s_.activation(
                x1bf[:], x1t[:], AF.Copy, bias=0.0, scale=1.0),
               reads=[t_x1t], writes=[t_x1bf])
            for ic in range(NC_I):
                mmg(E, [(lambda t, ic=ic: t.transpose(
                    pbt[:], x1bf[:, ic * 128:(ic + 1) * 128], P["ident"][:]))],
                    reads=[t_x1bf, P["t_ident"]], writes=[t_pbt])
                op(E, "vector", lambda v, ic=ic, tci=tci: v.tensor_copy(
                    x1fmh[:, ic, tci * 128:(tci + 1) * 128], pbt[:]),
                   reads=[t_pbt], writes=[t_x1fmh])
            mmg(E, [
                (lambda t, ic=ic, tci=tci:
                 t.matmul(pb[4][:, 0:E_EXP], x1fmh[:, ic,
                                                   tci * 128:(tci + 1) * 128],
                          gw_sb[:, ic, :], start=(ic == 0),
                          stop=(ic == NC_I - 1)))
                for ic in range(NC_I)],
                reads=[t_x1fmh, t_gw], writes=[tpb[4]])
            op(E, "vector", lambda v: v.tensor_copy(g8[:], pb[4][:, 0:E_EXP]),
               reads=[tpb[4]], writes=[t_g8])
            op(E, "vector", lambda v: v.reduce_max(gcol[0][:], g8[:], axis=AXX),
               reads=[t_g8], writes=[t_gc[0]])
            op(E, "vector", lambda v: v.tensor_scalar_mul(gcol[0][:], gcol[0][:],
                                                          -1.0),
               reads=[t_gc[0]], writes=[t_gc[0]])
            op(E, "scalar", lambda s_: s_.activation(
                g8[:], g8[:], AF.Exp, bias=gcol[0][:], scale=1.0,
                accum_out=gcol[1][:]),
               reads=[t_g8, t_gc[0]], writes=[t_g8, t_gc[1]])
            op(E, "vector", lambda v: v.reciprocal(gcol[1][:], gcol[1][:]),
               reads=[t_gc[1]], writes=[t_gc[1]])
            op(E, "vector", lambda v: v.tensor_scalar(
                g8[:], g8[:], gcol[1][:], None, ALU.mult),
               reads=[t_gc[1]], writes=[t_g8])
            op(E, "vector", lambda v: v.reduce_max(gcol[0][:], g8[:], axis=AXX),
               reads=[t_g8], writes=[t_gc[0]])
            op(E, "vector", lambda v: v.tensor_scalar(
                silu_f[:, 0:E_EXP], g8[:], gcol[0][:], None, ALU.is_ge),
               reads=[t_g8, t_gc[0]], writes=[t_silu])
            op(E, "vector", lambda v: v.scalar_tensor_tensor(
                silu_f[:, 0:E_EXP], silu_f[:, 0:E_EXP], -1e30, g8[:],
                ALU.mult, ALU.add),
               reads=[t_g8], writes=[t_silu])
            op(E, "vector", lambda v: v.reduce_max(gcol[1][:], silu_f[:, 0:E_EXP],
                                                   axis=AXX),
               reads=[t_silu], writes=[t_gc[1]])
            op(E, "vector", lambda v: v.tensor_scalar(
                silu_f[:, 0:E_EXP], g8[:], gcol[1][:], None, ALU.is_ge),
               reads=[t_g8, t_gc[1]], writes=[t_silu])
            op(E, "vector", lambda v: v.tensor_add(gcol[2][:], gcol[0][:],
                                                   gcol[1][:]),
               reads=[t_gc[0], t_gc[1]], writes=[t_gc[2]])
            op(E, "vector", lambda v: v.reciprocal(gcol[2][:], gcol[2][:]),
               reads=[t_gc[2]], writes=[t_gc[2]])
            op(E, "vector", lambda v, tc=tc: v.scalar_tensor_tensor(
                wgt[:, tc, :], g8[:], gcol[2][:], silu_f[:, 0:E_EXP],
                ALU.mult, ALU.mult),
               reads=[t_g8, t_gc[2], t_silu], writes=[t_wgt])
        for ep in range(10):
            is_sh = ep >= 8
            jlo = 0 if not is_sh else (ep - 8) * NJ
            njc = NJ
            if is_sh:
                src2 = io["sw2"].ap()[jlo * 128:(jlo + NJ) * 128, :]
            else:
                src2 = io["w2"].ap()[ep]
            op(E, "gpsimd", lambda g, src2=src2: g.dma_start(
                out=w2_sb[:], in_=src2.rearrange("(c p) f -> p c f", p=128)),
               writes=[t_w2])
            for jc in range(njc):
                jg = jlo + jc
                if is_sh:
                    src1 = io["sw1"].ap()[:, jg * 128:(jg + 1) * 128]
                    src3 = io["sw3"].ap()[:, jg * 128:(jg + 1) * 128]
                else:
                    src1 = io["w1"].ap()[ep][:, jg * 128:(jg + 1) * 128]
                    src3 = io["w3"].ap()[ep][:, jg * 128:(jg + 1) * 128]
                wb = jc % 2
                op(E, "gpsimd", lambda g, src1=src1, wb=wb: g.dma_start(
                    out=w1_sl[wb][:],
                    in_=src1.rearrange("(c p) f -> p c f", p=128)),
                   writes=[t_w1s[wb]])
                op(E, "gpsimd", lambda g, src3=src3, wb=wb: g.dma_start(
                    out=w3_sl[wb][:],
                    in_=src3.rearrange("(c p) f -> p c f", p=128)),
                   writes=[t_w3s[wb]])
                for tt in range(2):
                    tcols = slice(tt * 512, (tt + 1) * 512)
                    bh, tbh = pb[0 + (tt % 2)], tpb[0 + (tt % 2)]
                    bg, tbg = pb[2 + (tt % 2)], tpb[2 + (tt % 2)]
                    mmg(E, [
                        (lambda t, ic=ic, bh=bh, tcols=tcols, wb=wb:
                         t.matmul(bh[:], w1_sl[wb][:, ic, :],
                                  x1fmh[:, ic, tcols],
                                  start=(ic == 0), stop=(ic == NC_I - 1)))
                        for ic in range(NC_I)],
                        reads=[t_w1s[wb], t_x1fmh], writes=[tbh])
                    mmg(E, [
                        (lambda t, ic=ic, bg=bg, tcols=tcols, wb=wb:
                         t.matmul(bg[:], w3_sl[wb][:, ic, :],
                                  x1fmh[:, ic, tcols],
                                  start=(ic == 0), stop=(ic == NC_I - 1)))
                        for ic in range(NC_I)],
                        reads=[t_w3s[wb], t_x1fmh], writes=[tbg])
                    op(E, "scalar", lambda s_, bh=bh: s_.activation(
                        silu_f[:], bh[:], AF.Silu, bias=0.0, scale=1.0),
                       reads=[tbh], writes=[t_silu])
                    op(E, "vector", lambda v, bg=bg, jc=jc, tcols=tcols:
                       v.tensor_mul(u_bf[:, jc, tcols], silu_f[:], bg[:]),
                       reads=[t_silu, tbg], writes=[t_u])
            for ot in range(4):
                ocols = slice(ot * 512, (ot + 1) * 512)
                for tci in range(8):
                    tc = half * 8 + tci
                    bk, tb = pb[2 + (tci % 4)], tpb[2 + (tci % 4)]
                    mmg(E, [
                        (lambda t, jc=jc, tci=tci, bk=bk, ocols=ocols:
                         t.matmul(bk[:], u_bf[:, jc, tci * 128:(tci + 1) * 128],
                                  w2_sb[:, jc, ocols], start=(jc == 0),
                                  stop=(jc == NJ - 1)))
                        for jc in range(NJ)],
                        reads=[t_u, t_w2], writes=[tb])
                    if ep == 0:
                        op(E, "vector",
                           lambda v, bk=bk, tci=tci, tc=tc, ocols=ocols:
                           v.tensor_scalar(y_acc[:, tci, ocols], bk[:],
                                           wgt[:, tc, 0:1], None, ALU.mult),
                           reads=[tb, t_wgt], writes=[t_yacc])
                    elif not is_sh:
                        op(E, "vector",
                           lambda v, bk=bk, tci=tci, tc=tc, ocols=ocols, ep=ep:
                           v.scalar_tensor_tensor(
                               y_acc[:, tci, ocols], bk[:], wgt[:, tc, ep:ep + 1],
                               y_acc[:, tci, ocols], ALU.mult, ALU.add),
                           reads=[tb, t_wgt], writes=[t_yacc])
                    else:
                        op(E, "vector",
                           lambda v, bk=bk, tci=tci, ocols=ocols:
                           v.scalar_tensor_tensor(
                               y_acc[:, tci, ocols], bk[:], 1.0,
                               y_acc[:, tci, ocols], ALU.mult, ALU.add),
                           reads=[tb], writes=[t_yacc])
        for tci in range(8):
            tc = half * 8 + tci
            trows = slice(tc * 128, (tc + 1) * 128)
            op(E, "sync", lambda sy, trows=trows: sy.dma_start(
                out=x1t[:], in_=io["x1s"].ap()[trows, :]),
               reads=[tk["x1s"]], writes=[t_x1t])
            op(E, "vector", lambda v, tci=tci: v.tensor_add(
                x2t[:], x1t[:], y_acc[:, tci, :]),
               reads=[t_x1t, t_yacc], writes=[t_x2t])
            ln_apply(E, lns, x2t[:], t_x2t, x1t[:], t_x1t, P)
            op(E, "sync", lambda sy, trows=trows: sy.dma_start(
                out=io["x2s"].ap()[trows, :], in_=x1t[:]),
               reads=[t_x1t], writes=[tk["x2s"]])


def phase_C(E, nc, io, tk, P, ap):
    ecm = ap.enter_context
    pb = [ecm(nc.psum_tensor(f"psC{i}", [128, 512], F32)) for i in range(6)]
    tpb = [Trk(True) for _ in range(6)]
    pbm = ecm(nc.psum_tensor("psCm", [128, HD + 1], F32))
    t_pbm = Trk(True)
    pbt = ecm(nc.psum_tensor("psCt", [128, 128], BF16))
    t_pbt = Trk(True)

    x2fm = ecm(nc.sbuf_tensor("x2fm", [128, NC_I, TOWN], BF16))
    kffm = ecm(nc.sbuf_tensor("kffm", [128, KVH, TOWN], BF16))
    vaug1 = ecm(nc.sbuf_tensor("vaug1", [128, 16, KVH, HD + 1], BF16))
    states = ecm(nc.sbuf_tensor("states", [128, KVH, HD + 1], F32))
    statebf = ecm(nc.sbuf_tensor("statebf", [128, KVH, HD + 1], BF16))
    inits = ecm(nc.sbuf_tensor("inits", [128, 2, KVH, HD + 1], F32))
    summ_sb = ecm(nc.sbuf_tensor("summ_sb", [128, 2, KVH, SUMC], F32))
    smask_sb = ecm(nc.sbuf_tensor("smask_sb", [128, 2, 16], F32))
    ktm_t = ecm(nc.sbuf_tensor("ktm_t", [128, 128], BF16))
    t_x2fm = Trk(); t_kffm = Trk(); t_vaug = Trk()
    t_st = Trk(); t_stbf = Trk(); t_inits = Trk()
    t_summ = Trk(); t_smask = Trk(); t_ktm = Trk()

    op(E, "sync", lambda sy: sy.dma_start(out=smask_sb[:],
                                          in_=io["smask"][:, :, :]),
       writes=[t_smask])
    op(E, "vector", lambda v: v.memset(vaug1.ap()[:, :, :, HD:HD + 1], 1.0),
       writes=[t_vaug])
    op(E, "vector", lambda v: v.memset(summ_sb[:], 0.0), writes=[t_summ])

    def ktm_make(tc, kvh):
        mmg(E, [(lambda t, tc=tc, kvh=kvh: t.transpose(
            pbt[:], kffm[:, kvh, tc * 128:(tc + 1) * 128], P["ident"][:]))],
            reads=[t_kffm, P["t_ident"]], writes=[t_pbt])
        op(E, "vector", lambda v: v.tensor_copy(ktm_t[:], pbt[:]),
           reads=[t_pbt], writes=[t_ktm])

    _rope_ctr = [0]

    def mk_rope(scope):
        _rope_ctr[0] += 1
        i = _rope_ctr[0]
        r = {nm: scope.enter_context(nc.sbuf_tensor(f"{nm}_{i}", [HD, 512], F32))
             for nm in ("cosx", "sinx", "ro1", "ro2")}
        r["elu"] = r["ro2"]
        r.update({"t_cos": Trk(), "t_sin": Trk(), "t_ro1": Trk(),
                  "t_ro2": Trk()})
        r["t_elu"] = r["t_ro2"]
        return r

    def rope_elu(r, bank, tb, out_ap, out_trk):
        hh = HD // 2
        op(E, "vector", lambda v: v.tensor_mul(r["ro1"][:], bank[0:HD, :],
                                               r["cosx"][:]),
           reads=[tb, r["t_cos"]], writes=[r["t_ro1"]])
        op(E, "vector", lambda v: v.scalar_tensor_tensor(
            r["ro2"][0:hh, :], bank[hh:HD, :], -1.0, r["sinx"][0:hh, :],
            ALU.mult, ALU.mult), reads=[tb, r["t_sin"]], writes=[r["t_ro2"]])
        op(E, "vector", lambda v: v.scalar_tensor_tensor(
            r["ro2"][hh:HD, :], bank[0:hh, :], 1.0, r["sinx"][hh:HD, :],
            ALU.mult, ALU.mult), reads=[tb, r["t_sin"]], writes=[r["t_ro2"]])
        op(E, "vector", lambda v: v.tensor_add(r["ro1"][:], r["ro1"][:],
                                               r["ro2"][:]),
           reads=[r["t_ro2"]], writes=[r["t_ro1"]])
        op(E, "vector", lambda v: v.tensor_scalar_min(r["elu"][:], r["ro1"][:],
                                                      0.0),
           reads=[r["t_ro1"]], writes=[r["t_elu"]])
        op(E, "scalar", lambda s_: s_.activation(r["elu"][:], r["elu"][:],
                                                 AF.Exp, bias=0.0, scale=1.0),
           reads=[r["t_elu"]], writes=[r["t_elu"]])
        op(E, "vector", lambda v: v.scalar_tensor_tensor(
            out_ap, r["ro1"][:], 0.0, r["elu"][:], ALU.max, ALU.add),
           reads=[r["t_ro1"], r["t_elu"]], writes=[out_trk])

    with ExitStack() as c12:
        ee = c12.enter_context
        x2t = ee(nc.sbuf_tensor("x2tC", [128, D], F32))
        x2bf = ee(nc.sbuf_tensor("x2bfC", [128, D], BF16))
        wk_sb = ee(nc.sbuf_tensor("wk_sb", [128, NC_I, KVH * HD], BF16))
        wv_sb = ee(nc.sbuf_tensor("wv_sb", [128, NC_I, KVH * HD], BF16))
        rr = mk_rope(c12)
        t_x2t = Trk(); t_x2bf = Trk(); t_wk = Trk(); t_wv = Trk()

        op(E, "gpsimd", lambda g: g.dma_start(
            out=wk_sb[:],
            in_=io["wkT1"].ap().rearrange("(c p) f -> p c f", p=128)),
           writes=[t_wk])
        op(E, "gpsimd", lambda g: g.dma_start(
            out=wv_sb[:],
            in_=io["wvT1"].ap().rearrange("(c p) f -> p c f", p=128)),
           writes=[t_wv])
        for tc in range(16):
            trows = slice(tc * 128, (tc + 1) * 128)
            op(E, "sync", lambda sy, trows=trows: sy.dma_start(
                out=x2t[:], in_=io["x2s"].ap()[trows, :]),
               reads=[tk["x2s"]], writes=[t_x2t])
            op(E, "scalar", lambda s_: s_.activation(x2bf[:], x2t[:], AF.Copy,
                                                     bias=0.0, scale=1.0),
               reads=[t_x2t], writes=[t_x2bf])
            for ic in range(NC_I):
                mmg(E, [(lambda t, ic=ic: t.transpose(
                    pbt[:], x2bf[:, ic * 128:(ic + 1) * 128], P["ident"][:]))],
                    reads=[t_x2bf, P["t_ident"]], writes=[t_pbt])
                op(E, "vector", lambda v, ic=ic, tc=tc: v.tensor_copy(
                    x2fm[:, ic, tc * 128:(tc + 1) * 128], pbt[:]),
                   reads=[t_pbt], writes=[t_x2fm])
        for tt in range(4):
            tcols = slice(tt * 512, (tt + 1) * 512)
            op(E, "sync", lambda sy, tcols=tcols, rr=rr: sy.dma_start(
                out=rr["cosx"][:], in_=io["cos1"].ap()[:, tcols]),
               writes=[rr["t_cos"]])
            op(E, "sync", lambda sy, tcols=tcols, rr=rr: sy.dma_start(
                out=rr["sinx"][:], in_=io["sin1"].ap()[:, tcols]),
               writes=[rr["t_sin"]])
            for kvh in range(KVH):
                bk, tb = pb[kvh % 2], tpb[kvh % 2]
                mmg(E, [
                    (lambda t, ic=ic, kvh=kvh, bk=bk, tcols=tcols:
                     t.matmul(bk[0:HD, :],
                              wk_sb[:, ic, kvh * HD:(kvh + 1) * HD],
                              x2fm[:, ic, tcols], start=(ic == 0),
                              stop=(ic == NC_I - 1)))
                    for ic in range(NC_I)],
                    reads=[t_wk, t_x2fm], writes=[tb])
                rope_elu(rr, bk, tb, kffm.ap()[:, kvh, tcols], t_kffm)
            for tci in range(4):
                tc = tt * 4 + tci
                bk, tb = pb[2 + (tci % 2)], tpb[2 + (tci % 2)]
                mmg(E, [
                    (lambda t, ic=ic, tc=tc, bk=bk:
                     t.matmul(bk[:], x2fm[:, ic, tc * 128:(tc + 1) * 128],
                              wv_sb[:, ic, :], start=(ic == 0),
                              stop=(ic == NC_I - 1)))
                    for ic in range(NC_I)],
                    reads=[t_wv, t_x2fm], writes=[tb])
                for kvh in range(KVH):
                    op(E, "vector", lambda v, bk=bk, tc=tc, kvh=kvh:
                       v.tensor_copy(vaug1[:, tc, kvh, 0:HD],
                                     bk[:, kvh * HD:(kvh + 1) * HD]),
                       reads=[tb], writes=[t_vaug])

    E.join()
    for osc in range(2):
        for kvh in range(KVH):
            for c8 in range(8):
                tc = osc * 8 + c8
                ktm_make(tc, kvh)
                mmg(E, [(lambda t, tc=tc, kvh=kvh, c8=c8:
                         t.matmul(pbm[:], ktm_t[:], vaug1[:, tc, kvh, :],
                                  start=(c8 == 0), stop=(c8 == 7),
                                  skip_group_check=True))],
                    reads=[t_ktm, t_vaug], writes=[t_pbm])
            op(E, "vector", lambda v, osc=osc, kvh=kvh: v.tensor_copy(
                summ_sb[:, osc, kvh, 0:HD + 1], pbm[:]),
               reads=[t_pbm], writes=[t_summ])
            r0 = (osc * KVH + kvh) * 128
            op(E, "sync", lambda sy, r0=r0, osc=osc, kvh=kvh: sy.dma_start(
                out=io["summin"].ap()[r0:r0 + 128, :],
                in_=summ_sb[:, osc, kvh, :]),
               reads=[t_summ], writes=[tk["summin"]])
    op(E, "gpsimd", lambda g: g.collective_compute(
        "AllGather", ALU.bypass, replica_groups=[list(range(8))],
        ins=[io["summin"].ap().opt()], outs=[io["summall"].ap().opt()]),
       reads=[tk["summin"]], writes=[tk["summall"]], inc=1, q="cc")
    with ExitStack() as c3:
        sall = c3.enter_context(nc.sbuf_tensor("sall", [128, 64, SUMC], F32))
        t_sall = Trk()
        op(E, "sync", lambda sy: sy.dma_start(
            out=sall[:], in_=io["summall"].ap().rearrange("(g p) c -> p g c",
                                                          p=128)),
           reads=[tk["summall"]], writes=[t_sall])
        for osc in range(2):
            for kvh in range(KVH):
                for rk in range(16):
                    g64 = (rk // 2) * 8 + (rk % 2) * 4 + kvh
                    if rk == 0:
                        op(E, "vector", lambda v, osc=osc, kvh=kvh, g64=g64,
                           rk=rk: v.tensor_scalar(
                               inits[:, osc, kvh, :], sall[:, g64, 0:HD + 1],
                               smask_sb[:, osc, rk:rk + 1], None, ALU.mult),
                           reads=[t_sall, t_smask], writes=[t_inits])
                    else:
                        op(E, "vector", lambda v, osc=osc, kvh=kvh, g64=g64,
                           rk=rk: v.scalar_tensor_tensor(
                               inits[:, osc, kvh, :], sall[:, g64, 0:HD + 1],
                               smask_sb[:, osc, rk:rk + 1],
                               inits[:, osc, kvh, :], ALU.mult, ALU.add),
                           reads=[t_sall, t_smask], writes=[t_inits])

    E.join()
    with ExitStack() as c4:
        ee = c4.enter_context
        qf = ee(nc.sbuf_tensor("qf", [128, H, 512], BF16))
        attn1g = ee(nc.sbuf_tensor("attn1g", [128, H, 512], BF16))
        wq_sl = [ee(nc.sbuf_tensor(f"wq_sl{i}", [128, NC_I, 128], BF16))
                 for i in range(2)]
        wo_sl = ee(nc.sbuf_tensor("wo_sl", [128, H, 512], BF16))
        abuf = [ee(nc.sbuf_tensor(f"abuf{i}", [128, 128], BF16))
                for i in range(2)]
        at_t = ee(nc.sbuf_tensor("at_t", [128, 128], BF16))
        dcol = ee(nc.sbuf_tensor("dcol", [128, 1], F32))
        x2t4 = ee(nc.sbuf_tensor("x2tC4", [128, D], F32))
        x3t = [ee(nc.sbuf_tensor(f"x3t{i}", [128, D], F32)) for i in range(2)]
        rr = mk_rope(c4)
        lns = mk_ln(nc, c4, "C")
        t_qf = Trk(); t_a1g = Trk(); t_wq = [Trk(), Trk()]; t_wo = Trk()
        t_ab = [Trk(), Trk()]; t_att = Trk(); t_dcol = Trk()
        t_x2t4 = Trk(); t_x3t = [Trk(), Trk()]

        for g in range(4):
            gcols = slice(g * 512, (g + 1) * 512)
            op(E, "sync", lambda sy, gcols=gcols, rr=rr: sy.dma_start(
                out=rr["cosx"][:], in_=io["cos1"].ap()[:, gcols]),
               writes=[rr["t_cos"]])
            op(E, "sync", lambda sy, gcols=gcols, rr=rr: sy.dma_start(
                out=rr["sinx"][:], in_=io["sin1"].ap()[:, gcols]),
               writes=[rr["t_sin"]])
            for h in range(H):
                hb = h % 2
                op(E, "gpsimd", lambda g_, h=h, hb=hb: g_.dma_start(
                    out=wq_sl[hb][:],
                    in_=io["wqT1"].ap()[:, h * HD:(h + 1) * HD]
                        .rearrange("(c p) f -> p c f", p=128)),
                   writes=[t_wq[hb]])
                bk, tb = pb[4], tpb[4]
                mmg(E, [
                    (lambda t, ic=ic, bk=bk, gcols=gcols, hb=hb:
                     t.matmul(bk[:], wq_sl[hb][:, ic, :], x2fm[:, ic, gcols],
                              start=(ic == 0), stop=(ic == NC_I - 1)))
                    for ic in range(NC_I)],
                    reads=[t_wq[hb], t_x2fm], writes=[tb])
                rope_elu(rr, bk, tb, qf.ap()[:, h, :], t_qf)
            for tci in range(4):
                tc = g * 4 + tci
                icols = slice(tci * 128, (tci + 1) * 128)
                if tc % 8 == 0:
                    osc = tc // 8
                    for kvh in range(KVH):
                        op(E, "vector", lambda v, osc=osc, kvh=kvh:
                           v.tensor_copy(states[:, kvh, :],
                                         inits[:, osc, kvh, :]),
                           reads=[t_inits], writes=[t_st])
                    op(E, "vector", lambda v: v.tensor_copy(statebf[:],
                                                            states[:]),
                       reads=[t_st], writes=[t_stbf])
                for h in range(H):
                    kvh = h // (H // KVH)
                    bk, tb = pb[5], tpb[5]
                    mmg(E, [(lambda t, h=h, kvh=kvh, tc=tc, icols=icols, bk=bk:
                             t.matmul(bk[:, 0:128],
                                      kffm[:, kvh, tc * 128:(tc + 1) * 128],
                                      qf[:, h, icols], start=True, stop=True))],
                        reads=[t_kffm, t_qf], writes=[tb])
                    ab, t_abx = abuf[h % 2], t_ab[h % 2]
                    op(E, "vector", lambda v, bk=bk, ab=ab: v.tensor_mul(
                        ab[:], bk[:, 0:128], P["ltri"][:]),
                       reads=[tb, P["t_ltri"]], writes=[t_abx])
                    mmg(E, [
                        (lambda t, ab=ab, tc=tc, kvh=kvh:
                         t.matmul(pbm[:], ab[:], vaug1[:, tc, kvh, :],
                                  start=True, stop=False)),
                        (lambda t, h=h, kvh=kvh, icols=icols:
                         t.matmul(pbm[:], qf[:, h, icols], statebf[:, kvh, :],
                                  start=False, stop=True)),
                    ], reads=[t_abx, t_vaug, t_qf, t_stbf], writes=[t_pbm])
                    op(E, "vector", lambda v: v.tensor_scalar(
                        dcol[:], pbm[:, HD:HD + 1], L1_EPS, None, ALU.add),
                       reads=[t_pbm], writes=[t_dcol])
                    op(E, "vector", lambda v: v.reciprocal(dcol[:], dcol[:]),
                       reads=[t_dcol], writes=[t_dcol])
                    op(E, "vector", lambda v: v.tensor_scalar(
                        at_t[:], pbm[:, 0:HD], dcol[:], None, ALU.mult),
                       reads=[t_pbm, t_dcol], writes=[t_att])
                    mmg(E, [(lambda t: t.transpose(pbt[:], at_t[:],
                                                   P["ident"][:]))],
                        reads=[t_att, P["t_ident"]], writes=[t_pbt])
                    op(E, "vector", lambda v, h=h, icols=icols: v.tensor_copy(
                        attn1g[:, h, icols], pbt[:]),
                       reads=[t_pbt], writes=[t_a1g])
                for kvh in range(KVH):
                    ktm_make(tc, kvh)
                    mmg(E, [(lambda t, tc=tc, kvh=kvh: t.matmul(
                        pbm[:], ktm_t[:], vaug1[:, tc, kvh, :],
                        start=True, stop=True))],
                        reads=[t_ktm, t_vaug], writes=[t_pbm])
                    op(E, "vector", lambda v, kvh=kvh: v.tensor_add(
                        states[:, kvh, :], states[:, kvh, :], pbm[:]),
                       reads=[t_pbm], writes=[t_st])
                op(E, "vector", lambda v: v.tensor_copy(statebf[:], states[:]),
                   reads=[t_st], writes=[t_stbf])
            for pairi in range(2):
                for ft in range(4):
                    fcols = slice(ft * 512, (ft + 1) * 512)
                    op(E, "gpsimd", lambda g_, ft=ft: g_.dma_start(
                        out=wo_sl[:],
                        in_=io["woT1"].ap()[:, ft * 512:(ft + 1) * 512]
                            .rearrange("(c p) f -> p c f", p=128)),
                       writes=[t_wo])
                    for tin in range(2):
                        tci = pairi * 2 + tin
                        bk, tb = pb[tin], tpb[tin]
                        mmg(E, [
                            (lambda t, h=h, tci=tci, bk=bk:
                             t.matmul(bk[:],
                                      attn1g[:, h, tci * 128:(tci + 1) * 128],
                                      wo_sl[:, h, :], start=(h == 0),
                                      stop=(h == H - 1)))
                            for h in range(H)],
                            reads=[t_a1g, t_wo], writes=[tb])
                        op(E, "vector", lambda v, bk=bk, tin=tin, fcols=fcols:
                           v.tensor_copy(x3t[tin][:, fcols], bk[:]),
                           reads=[tb], writes=[t_x3t[tin]])
                for tin in range(2):
                    tci = pairi * 2 + tin
                    tc = g * 4 + tci
                    trows = slice(tc * 128, (tc + 1) * 128)
                    op(E, "sync", lambda sy, trows=trows: sy.dma_start(
                        out=x2t4[:], in_=io["x2s"].ap()[trows, :]),
                       reads=[tk["x2s"]], writes=[t_x2t4])
                    op(E, "vector", lambda v, tin=tin: v.tensor_add(
                        x3t[tin][:], x3t[tin][:], x2t4[:]),
                       reads=[t_x2t4], writes=[t_x3t[tin]])
                    ln_apply(E, lns, x3t[tin][:], t_x3t[tin], x2t4[:], t_x2t4, P)
                    op(E, "sync", lambda sy, trows=trows: sy.dma_start(
                        out=io["x3s"].ap()[trows, :], in_=x2t4[:]),
                       reads=[t_x2t4], writes=[tk["x3s"]])


def phase_D(E, nc, io, tk, P, ap):
    ecm = ap.enter_context
    pb = [ecm(nc.psum_tensor(f"psD{i}", [128, 512], F32)) for i in range(4)]
    tpb = [Trk(True) for _ in range(4)]
    pbt = ecm(nc.psum_tensor("psDt", [128, 128], BF16))
    t_pbt = Trk(True)

    w_sb = ecm(nc.sbuf_tensor("l1w_sb", [128, NC_I, D], BF16))
    scb = ecm(nc.sbuf_tensor("scb", [128, D], F32))
    bib = ecm(nc.sbuf_tensor("bib", [128, D], F32))
    srow = ecm(nc.sbuf_tensor("srow", [1, D], F32))
    x3fm = ecm(nc.sbuf_tensor("x3fmD", [128, NC_I, 128], BF16))
    x3t = ecm(nc.sbuf_tensor("x3tD", [128, D], F32))
    x3bf = ecm(nc.sbuf_tensor("x3bfD", [128, D], BF16))
    yt = ecm(nc.sbuf_tensor("ytD", [128, D], F32))
    yo = ecm(nc.sbuf_tensor("yoD", [128, D], F32))
    lns = mk_ln(nc, ap, "D")
    t_w = Trk(); t_scb = Trk(); t_bib = Trk(); t_srow = Trk()
    t_x3fm = Trk(); t_x3t = Trk(); t_x3bf = Trk(); t_yt = Trk(); t_yo = Trk()

    op(E, "gpsimd", lambda g: g.dma_start(
        out=w_sb[:], in_=io["l1wT"].ap().rearrange("(c p) f -> p c f", p=128)),
       writes=[t_w])
    for nm, tile, trk in (("l1scale", scb, t_scb), ("l1bias", bib, t_bib)):
        op(E, "sync", lambda sy, nm=nm: sy.dma_start(out=srow[:],
                                                     in_=io[nm][:, :]),
           writes=[t_srow])
        for ft in range(4):
            fcols = slice(ft * 512, (ft + 1) * 512)
            mmg(E, [(lambda t, fcols=fcols: t.matmul(
                pb[3][:, :], P["ones_r32"][:], srow[0:1, fcols],
                start=True, stop=True))],
                reads=[t_srow, P["t_or"]], writes=[tpb[3]])
            op(E, "vector", lambda v, tile=tile, fcols=fcols: v.tensor_copy(
                tile[:, fcols], pb[3][:, :]),
               reads=[tpb[3]], writes=[trk])

    for tc in range(16):
        trows = slice(tc * 128, (tc + 1) * 128)
        op(E, "sync", lambda sy, trows=trows: sy.dma_start(
            out=x3t[:], in_=io["x3s"].ap()[trows, :]),
           reads=[tk["x3s"]], writes=[t_x3t])
        op(E, "scalar", lambda s_: s_.activation(x3bf[:], x3t[:], AF.Copy,
                                                 bias=0.0, scale=1.0),
           reads=[t_x3t], writes=[t_x3bf])
        for ic in range(NC_I):
            mmg(E, [(lambda t, ic=ic: t.transpose(
                pbt[:], x3bf[:, ic * 128:(ic + 1) * 128], P["ident"][:]))],
                reads=[t_x3bf, P["t_ident"]], writes=[t_pbt])
            op(E, "vector", lambda v, ic=ic: v.tensor_copy(
                x3fm[:, ic, :], pbt[:]),
               reads=[t_pbt], writes=[t_x3fm])
        for ft in range(4):
            fcols = slice(ft * 512, (ft + 1) * 512)
            bk, tb = pb[ft % 2], tpb[ft % 2]
            mmg(E, [
                (lambda t, ic=ic, ft=ft, bk=bk:
                 t.matmul(bk[:], x3fm[:, ic, :],
                          w_sb[:, ic, ft * 512:(ft + 1) * 512],
                          start=(ic == 0), stop=(ic == NC_I - 1)))
                for ic in range(NC_I)],
                reads=[t_x3fm, t_w], writes=[tb])
            op(E, "vector", lambda v, bk=bk, fcols=fcols: v.tensor_mul(
                yt[:, fcols], bk[:], scb[:, fcols]),
               reads=[tb, t_scb], writes=[t_yt])
            op(E, "vector", lambda v, fcols=fcols: v.tensor_add(
                yt[:, fcols], yt[:, fcols], bib[:, fcols]),
               reads=[t_bib], writes=[t_yt])
        op(E, "vector", lambda v: v.tensor_add(yt[:], yt[:], x3t[:]),
           reads=[t_x3t], writes=[t_yt])
        ln_apply(E, lns, yt[:], t_yt, yo[:], t_yo, P)
        op(E, "sync", lambda sy, trows=trows: sy.dma_start(
            out=io["y"].ap()[trows, :], in_=yo[:]),
           reads=[t_yo], writes=[tk["y"]])


def _rope_tables_fm(dim, positions):
    inv = 1.0 / (10000.0 ** (np.arange(0, dim, 2, dtype=np.float64) / dim))
    ang = positions[None, :].astype(np.float64) * np.concatenate([inv, inv])[:, None]
    return (np.cos(ang).astype(np.float32), np.sin(ang).astype(np.float32))


def _prep_inputs(inputs):
    x0 = np.ascontiguousarray(inputs["L0_attn_hidden_states"], dtype=np.float32)
    shared = {}
    shared["wqaT"] = np.ascontiguousarray(inputs["L0_attn_w_qa"].T)
    shared["wkvaT"] = np.ascontiguousarray(inputs["L0_attn_w_kva"].T)
    shared["woT"] = np.ascontiguousarray(inputs["L0_attn_w_o"].T)
    shared["qaln"] = np.ascontiguousarray(
        inputs["L0_attn_q_a_ln_w"].reshape(NC_Q, 128).T)
    shared["kvln"] = np.ascontiguousarray(
        inputs["L0_attn_kv_a_ln_w"].reshape(NC_KV, 128).T)
    cos0, sin0 = _rope_tables_fm(DR, np.arange(S))
    shared["cos0"] = cos0
    shared["sin0"] = sin0
    shared["gwT"] = np.ascontiguousarray(inputs["L0_ffn_gate_w"].T)
    shared["w1"] = np.ascontiguousarray(inputs["L0_ffn_w1"])
    shared["w3"] = np.ascontiguousarray(inputs["L0_ffn_w3"])
    shared["w2"] = np.ascontiguousarray(inputs["L0_ffn_w2"])
    shared["sw1"] = np.ascontiguousarray(inputs["L0_ffn_sw1"])
    shared["sw3"] = np.ascontiguousarray(inputs["L0_ffn_sw3"])
    shared["sw2"] = np.ascontiguousarray(inputs["L0_ffn_sw2"])
    shared["wqT1"] = np.ascontiguousarray(inputs["L1_attn_Wq"].T)
    shared["wkT1"] = np.ascontiguousarray(inputs["L1_attn_Wk"].T)
    shared["wvT1"] = np.ascontiguousarray(inputs["L1_attn_Wv"].T)
    shared["woT1"] = np.ascontiguousarray(inputs["L1_attn_Wo"].T)
    shared["l1wT"] = np.ascontiguousarray(inputs["L1_ffn_weight"].T)
    shared["l1scale"] = np.ascontiguousarray(
        inputs["L1_ffn_scale_weight"].reshape(1, D))
    shared["l1bias"] = np.ascontiguousarray(inputs["L1_ffn_bias"].reshape(1, D))
    shared["ltri"] = np.triu(np.ones((128, 128), dtype=np.float32))
    shared["ident"] = np.eye(128, dtype=np.float32)
    wqbT = np.ascontiguousarray(inputs["L0_attn_w_qb"].T)
    wkvbT = np.ascontiguousarray(inputs["L0_attn_w_kvb"].T)

    in_maps = []
    for c in range(8):
        b, j = c // 2, c % 2
        s0 = TOWN * j
        m = dict(shared)
        m["xo_tm"] = np.ascontiguousarray(x0[b, s0:s0 + TOWN, :])
        m["xa_fm"] = np.ascontiguousarray(x0[b].T)
        m["wqbT_own"] = np.ascontiguousarray(
            wqbT[:, 8 * j * DQ:(8 * j + 8) * DQ])
        m["wkvbT_own"] = np.ascontiguousarray(
            wkvbT[:, 8 * j * 256:(8 * j + 8) * 256])
        m["ownoff"] = np.array([[s0]], dtype=np.int32)
        cos1, sin1 = _rope_tables_fm(HD, np.arange(s0, s0 + TOWN))
        m["cos1"] = cos1
        m["sin1"] = sin1
        sm = np.zeros((128, 2, 16), dtype=np.float32)
        for oi, osc in enumerate((2 * j, 2 * j + 1)):
            for rk in range(16):
                r, k = rk // 2, rk % 2
                rb_, jr = r // 2, r % 2
                gsc = 2 * jr + k
                if rb_ == b and gsc < osc:
                    sm[:, oi, rk] = 1.0
        m["smask"] = sm
        in_maps.append(m)
    return in_maps


_PHASE_INPUTS = {
    "A": ["xo_tm", "xa_fm", "wqaT", "wqbT_own", "wkvaT", "wkvbT_own", "woT",
          "qaln", "kvln", "cos0", "sin0", "ownoff"],
    "B": ["gwT", "w1", "w3", "w2", "sw1", "sw3", "sw2"],
    "C": ["wkT1", "wvT1", "wqT1", "woT1", "cos1", "sin1", "smask"],
    "D": ["l1wT", "l1scale", "l1bias"],
}
_PHASE_STREAM_IN = {"B": ["x1s"], "C": ["x2s"], "D": ["x3s"]}
_PHASE_STREAM_OUT = {"A": "x1s", "B": "x2s", "C": "x3s", "D": "y"}

_prog_cache = {}


def kernel(**inputs):
    in_maps = _prep_inputs(inputs)
    stream = [dict() for _ in range(8)]
    out = np.empty((B, S, D), dtype=np.float32)
    for phases in LAUNCH_PLAN:
        key = tuple(phases)
        if key not in _prog_cache:
            _prog_cache[key] = build_program(list(phases))
        nc = _prog_cache[key]
        names = ["ltri", "ident"]
        for ph in phases:
            names += _PHASE_INPUTS[ph]
        first = phases[0]
        sins = _PHASE_STREAM_IN.get(first, [])
        maps = []
        for c in range(8):
            mm = {nm: in_maps[c][nm] for nm in names}
            for snm in sins:
                mm[snm] = stream[c][snm]
            maps.append(mm)
        res = run_bass_kernel_spmd(nc, maps, core_ids=list(range(8)))
        last = phases[-1]
        onm = _PHASE_STREAM_OUT[last]
        for c in range(8):
            arr = res.results[c][onm]
            if onm == "y":
                b, j = c // 2, c % 2
                out[b, TOWN * j:TOWN * (j + 1), :] = arr
            else:
                stream[c][onm] = arr
    return out


# version 06 (final)
# speedup vs baseline: 1.0549x; 1.0162x over previous
import math
from contextlib import ExitStack

import numpy as np

import concourse.bass as bass
import concourse.mybir as mybir
from concourse.bass_utils import run_bass_kernel_spmd

F32 = mybir.dt.float32
BF16 = mybir.dt.bfloat16
I32 = mybir.dt.int32
AF = mybir.ActivationFunctionType
ALU = mybir.AluOpType
AXX = mybir.AxisListType.X

B, S, D = 4, 4096, 2048
TOWN = 2048
H, QLR, KVLR, DN, DR, DV = 16, 1536, 512, 128, 64, 128
DQ = DN + DR
SCALE = DQ ** (-0.5)
RMS_EPS = 1e-6
LN_EPS = 1e-5
E_EXP, I_FF, SH_FF = 8, 1408, 2816
KVH, HD = 4, 128
L1_EPS = 1e-6
NC_I = D // 128
NC_Q = QLR // 128
NC_KV = KVLR // 128
NJ = I_FF // 128
NJS = SH_FF // 128
SUMC = 256

LAUNCH_PLAN = [["A", "B", "C", "D"]]
A_DO_AG = True
A_DYN_OFF = True
A_STAGES = 3
A_A1B = True
A_CQS = True
A_NT = 8
A_RMS = True
A_SQ = True
A_TS = True
A_MM = True
A_COL = True


class Trk:
    __slots__ = ("w", "r", "excl")

    def __init__(self, excl=False):
        self.w = []
        self.r = []
        self.excl = excl


class Emit:
    ENGINES = ("sync", "gpsimd", "tensor", "vector", "scalar")
    QOF = {"sync": "qSY", "gpsimd": "qGP", "tensor": "eT", "vector": "eV",
           "scalar": "eS"}

    def __init__(self, nc, stack):
        self.nc = nc
        names = ("qSY", "qGP", "eT", "eV", "eS", "cc")
        self.sems = {q: stack.enter_context(nc.semaphore(q)) for q in names}
        self.count = {q: 0 for q in names}
        self.ops = {e: [] for e in self.ENGINES}
        self.satisfied = {e: {} for e in self.ENGINES}
        self.joins = {}

    def join(self):
        toks = self.all_tokens()
        for e in self.ENGINES:
            self.joins[e] = toks

    def _wait(self, engine, deps):
        deps = list(deps) + self.joins.pop(engine, [])
        need = {}
        for d in deps:
            if d is None:
                continue
            q, v = d
            if v > need.get(q, 0):
                need[q] = v
        for q, v in sorted(need.items()):
            if self.satisfied[engine].get(q, 0) >= v:
                continue
            self.satisfied[engine][q] = v
            sem = self.sems[q]
            self.ops[engine].append(lambda eng, sem=sem, v=v: eng.wait_ge(sem, v))

    def raw_op(self, engine, fn, deps=(), inc=None, q=None):
        q = q or self.QOF[engine]
        if inc is None:
            inc = 16 if q in ("qSY", "qGP") else 1
        if q in ("qSY", "qGP", "cc") and self.count[q] > 0:
            deps = list(deps) + [(q, self.count[q])]
        self._wait(engine, deps)
        self.count[q] += inc
        tok = (q, self.count[q])
        sem = self.sems[q]
        self.ops[engine].append(
            lambda eng, fn=fn, sem=sem, inc=inc: fn(eng).then_inc(sem, inc))
        return tok

    def raw_noinc(self, engine, fn, deps=()):
        self._wait(engine, deps)
        self.ops[engine].append(fn)

    def all_tokens(self):
        return [(q, v) for q, v in self.count.items() if v > 0]

    def flush(self, block):
        for e in self.ENGINES:
            lst = self.ops[e]
            if not lst:
                continue

            def body(eng, lst=lst):
                for f in lst:
                    f(eng)
            getattr(block, e)(body)


def _deps(reads, writes, extra):
    deps = list(extra)
    for t in reads:
        deps += t.w
        if t.excl:
            deps += t.r
    for t in writes:
        deps += t.w + t.r
    return deps


def _commit(tok, reads, writes):
    for t in reads:
        t.r.append(tok)
    for t in writes:
        t.w = [tok]
        t.r = []
    return tok


def op(E, engine, fn, reads=(), writes=(), extra=(), inc=None, q=None):
    tok = E.raw_op(engine, fn, deps=_deps(reads, writes, extra), inc=inc, q=q)
    return _commit(tok, reads, writes)


def mmg(E, steps, reads=(), writes=(), extra=()):
    deps = _deps(reads, writes, extra)
    for fn in steps[:-1]:
        E.raw_noinc("tensor", fn, deps=deps)
        deps = ()
    tok = E.raw_op("tensor", steps[-1], deps=deps)
    return _commit(tok, reads, writes)


def mk_ln(nc, pstack, tag):
    s = {nm: pstack.enter_context(nc.sbuf_tensor(f"ln{tag}_{nm}", [128, 1], F32))
         for nm in ("sum", "ssq", "mu", "var", "rstd", "nbias", "t1")}
    s["dump"] = pstack.enter_context(
        nc.sbuf_tensor(f"ln{tag}_dump", [128, D], BF16))
    s["trk"] = Trk()
    return s


def ln_apply(E, sc, x_ap, x_trk, out_ap, out_trk, P=None):
    t = sc["trk"]
    op(E, "scalar", lambda s_: s_.activation(sc["dump"][:], x_ap, AF.Copy,
                                             bias=0.0, scale=1.0,
                                             accum_out=sc["sum"][:]),
       reads=[x_trk], writes=[t])
    op(E, "scalar", lambda s_: s_.activation(sc["dump"][:], x_ap, AF.Square,
                                             bias=0.0, scale=1.0,
                                             accum_out=sc["ssq"][:]),
       reads=[x_trk], writes=[t])
    op(E, "vector", lambda v: v.tensor_scalar_mul(sc["mu"][:], sc["sum"][:],
                                                  1.0 / D), reads=[t], writes=[t])
    op(E, "vector", lambda v: v.tensor_mul(sc["t1"][:], sc["mu"][:], sc["mu"][:]),
       reads=[t], writes=[t])
    op(E, "vector", lambda v: v.scalar_tensor_tensor(
        sc["var"][:], sc["ssq"][:], 1.0 / D, sc["t1"][:], ALU.mult, ALU.subtract),
        reads=[t], writes=[t])
    op(E, "scalar", lambda s_: s_.activation(sc["t1"][:], sc["var"][:], AF.Sqrt,
                                             bias=P["eps_ln"][:], scale=1.0),
       reads=[t, P["t_eps"]], writes=[t])
    op(E, "vector", lambda v: v.reciprocal(sc["rstd"][:], sc["t1"][:]),
       reads=[t], writes=[t])
    op(E, "vector", lambda v: v.scalar_tensor_tensor(
        sc["nbias"][:], sc["mu"][:], -1.0, sc["rstd"][:], ALU.mult, ALU.mult),
        reads=[t], writes=[t])
    return op(E, "vector", lambda v: v.tensor_scalar(
        out_ap, x_ap, sc["rstd"][:], sc["nbias"][:], ALU.mult, ALU.add),
        reads=[x_trk, t], writes=[out_trk])


def build_program(phases):
    nc = bass.Bass("TRN2", target_bir_lowering=False)
    io = {}

    def dram(name, shape, dtype, kind):
        io[name] = nc.dram_tensor(name, shape, dtype, kind=kind)
        return io[name]

    def stream_kind(prod, cons):
        if prod in phases and cons in phases:
            return "Internal"
        if prod in phases:
            return "ExternalOutput"
        return "ExternalInput"

    if "A" in phases:
        dram("xo_tm", [TOWN, D], F32, "ExternalInput")
        dram("xa_fm", [D, S], F32, "ExternalInput")
        dram("wqaT", [D, QLR], F32, "ExternalInput")
        dram("wqbT_own", [QLR, 8 * DQ], F32, "ExternalInput")
        dram("wkvaT", [D, KVLR + DR], F32, "ExternalInput")
        dram("wkvbT_own", [KVLR, 8 * (DN + DV)], F32, "ExternalInput")
        dram("woT", [D, D], F32, "ExternalInput")
        dram("qaln", [128, NC_Q], F32, "ExternalInput")
        dram("kvln", [128, NC_KV], F32, "ExternalInput")
        dram("cos0", [DR, S], F32, "ExternalInput")
        dram("sin0", [DR, S], F32, "ExternalInput")
        dram("ownoff", [1, 1], I32, "ExternalInput")
        dram("cqs", [NC_Q, 128, S], BF16, "Internal")
        dram("attnin", [8 * 128, S], BF16, "Internal")
        dram("attnout", [4, 512, S], BF16, "Internal")
    if "B" in phases:
        dram("gwT", [D, E_EXP], F32, "ExternalInput")
        dram("w1", [E_EXP, D, I_FF], F32, "ExternalInput")
        dram("w3", [E_EXP, D, I_FF], F32, "ExternalInput")
        dram("w2", [E_EXP, I_FF, D], F32, "ExternalInput")
        dram("sw1", [D, SH_FF], F32, "ExternalInput")
        dram("sw3", [D, SH_FF], F32, "ExternalInput")
        dram("sw2", [SH_FF, D], F32, "ExternalInput")
    if "C" in phases:
        dram("wkT1", [D, KVH * HD], F32, "ExternalInput")
        dram("wvT1", [D, KVH * HD], F32, "ExternalInput")
        dram("wqT1", [D, D], F32, "ExternalInput")
        dram("woT1", [D, D], F32, "ExternalInput")
        dram("cos1", [HD, TOWN], F32, "ExternalInput")
        dram("sin1", [HD, TOWN], F32, "ExternalInput")
        dram("smask", [128, 2, 16], F32, "ExternalInput")
        dram("summin", [2 * KVH * 128, SUMC], F32, "Internal")
        dram("summall", [8 * 2 * KVH * 128, SUMC], F32, "Internal")
    if "D" in phases:
        dram("l1wT", [D, D], F32, "ExternalInput")
        dram("l1scale", [1, D], F32, "ExternalInput")
        dram("l1bias", [1, D], F32, "ExternalInput")
    dram("ltri", [128, 128], F32, "ExternalInput")
    dram("ident", [128, 128], F32, "ExternalInput")

    if "A" in phases or "B" in phases:
        dram("x1s", [TOWN, D], F32, stream_kind("A", "B"))
    if "B" in phases or "C" in phases:
        dram("x2s", [TOWN, D], F32, stream_kind("B", "C"))
    if "C" in phases or "D" in phases:
        dram("x3s", [TOWN, D], F32, stream_kind("C", "D"))
    if "D" in phases:
        dram("y", [TOWN, D], F32, "ExternalOutput")

    tk = {name: Trk() for name in io}

    with ExitStack() as stack:
        E = Emit(nc, stack)
        ltri_bf = stack.enter_context(nc.sbuf_tensor("ltri_bf", [128, 128], BF16))
        ident_bf = stack.enter_context(nc.sbuf_tensor("ident_bf", [128, 128], BF16))
        ones_col = stack.enter_context(nc.sbuf_tensor("ones_col", [128, 1], BF16))
        ones_r32 = stack.enter_context(nc.sbuf_tensor("ones_r32", [1, 128], F32))
        ones_r65 = stack.enter_context(nc.sbuf_tensor("ones_r65", [65, 128], F32))
        eps_ln = stack.enter_context(nc.sbuf_tensor("eps_ln", [128, 1], F32))
        eps_rms = stack.enter_context(nc.sbuf_tensor("eps_rms", [128, 1], F32))
        P = {"ltri": ltri_bf, "ident": ident_bf, "ones_col": ones_col,
             "ones_r32": ones_r32, "ones_r65": ones_r65,
             "eps_ln": eps_ln, "eps_rms": eps_rms,
             "t_ltri": Trk(), "t_ident": Trk(),
             "t_oc": Trk(), "t_or": Trk(), "t_eps": Trk()}
        op(E, "gpsimd", lambda g: g.dma_start(out=ltri_bf[:], in_=io["ltri"][:, :]),
           writes=[P["t_ltri"]])
        op(E, "gpsimd", lambda g: g.dma_start(out=ident_bf[:], in_=io["ident"][:, :]),
           writes=[P["t_ident"]])
        op(E, "vector", lambda v: v.memset(ones_col[:], 1.0), writes=[P["t_oc"]])
        op(E, "vector", lambda v: v.memset(ones_r32[:], 1.0), writes=[P["t_or"]])
        op(E, "vector", lambda v: v.memset(ones_r65[:], 1.0), writes=[P["t_or"]])
        op(E, "vector", lambda v: v.memset(eps_ln[:], LN_EPS), writes=[P["t_eps"]])
        op(E, "vector", lambda v: v.memset(eps_rms[:], RMS_EPS), writes=[P["t_eps"]])

        for ph in phases:
            with ExitStack() as pstack:
                if ph == "A":
                    phase_A(E, nc, io, tk, P, pstack)
                elif ph == "B":
                    phase_B(E, nc, io, tk, P, pstack)
                elif ph == "C":
                    phase_C(E, nc, io, tk, P, pstack)
                elif ph == "D":
                    phase_D(E, nc, io, tk, P, pstack)
            E.join()

        E._wait("sync", E.all_tokens())
        with nc.Block() as block:
            E.flush(block)
    return nc


def phase_A(E, nc, io, tk, P, ap):
    ecm = ap.enter_context
    pb = [ecm(nc.psum_tensor(f"psA{i}", [128, 512], F32)) for i in range(8)]
    tpb = [Trk(True) for _ in range(8)]

    with ExitStack() as a12:
        e1 = a12.enter_context
        trig0 = e1(nc.sbuf_tensor("trig0", [128, S], F32))
        ckvg = e1(nc.sbuf_tensor("ckvg", [128, NC_KV, S], BF16))
        kpe = e1(nc.sbuf_tensor("kpe", [DR + 1, S], BF16))
        rstdq = e1(nc.sbuf_tensor("rstdq", [1, S], F32))
        rstdkv = e1(nc.sbuf_tensor("rstdkv", [1, S], F32))
        kpen2 = e1(nc.sbuf_tensor("kpen2", [1, S], F32))
        rstdkvT = e1(nc.sbuf_tensor("rstdkvT", [128, S // 128], F32))
        t_cos = Trk(); t_sin = Trk(); t_ckvg = Trk(); t_kpe = Trk()
        t_kpen2 = Trk(); t_rq = Trk(); t_rkv = Trk(); t_rkvT = Trk()

        op(E, "sync", lambda sy: sy.dma_start(out=trig0[0:DR, :],
                                              in_=io["cos0"][:, :]),
           writes=[t_cos])
        op(E, "sync", lambda sy: sy.dma_start(out=trig0[DR:2 * DR, :],
                                              in_=io["sin0"][:, :]),
           writes=[t_sin])
        op(E, "vector", lambda v: v.memset(kpe[DR:DR + 1, :], 1.0),
           writes=[t_kpe])
        t_cqg = Trk()
        with ExitStack() as a1:
            ee = a1.enter_context
            wqa_sb = ee(nc.sbuf_tensor("wqa_sb", [128, NC_I, QLR], BF16))
            qaln_sb = ee(nc.sbuf_tensor("qaln_sb", [128, NC_Q], F32))
            xa2 = [ee(nc.sbuf_tensor(f"xaf{i}", [128, NC_I, 512], BF16))
                   for i in range(2)]
            sqb = [ee(nc.sbuf_tensor(f"sqb{i}", [128, 512], BF16))
                   for i in range(2)]
            cqstage = [ee(nc.sbuf_tensor(f"cqstage{i}", [128, 512], BF16))
                       for i in range(2)]
            rowt = ee(nc.sbuf_tensor("rowt", [1, 512], F32))
            t_wqa = Trk(); t_qaln = Trk(); t_xa2 = [Trk(), Trk()]
            t_sqb = [Trk(), Trk()]; t_cqst = [Trk(), Trk()]; t_rowt = Trk()

            op(E, "gpsimd", lambda g: g.dma_start(
                out=wqa_sb[:],
                in_=io["wqaT"].ap().rearrange("(c p) f -> p c f", p=128)),
               writes=[t_wqa])
            op(E, "sync", lambda sy: sy.dma_start(out=qaln_sb[:],
                                                  in_=io["qaln"][:, :]),
               writes=[t_qaln])
            for it in range(A_NT):
                cols = slice(it * 512, (it + 1) * 512)
                xa, t_xa = xa2[it % 2], t_xa2[it % 2]
                op(E, "gpsimd", lambda g, cols=cols, xa=xa: g.dma_start(
                    out=xa[:],
                    in_=io["xa_fm"].ap()[:, cols]
                        .rearrange("(c p) t -> p c t", p=128)),
                   writes=[t_xa])
                for cc in range(A_MM if isinstance(A_MM, int) and not isinstance(A_MM, bool) else (NC_Q if A_MM else 0)):
                    bk, tb = pb[cc % 4], tpb[cc % 4]
                    mmg(E, [
                        (lambda t, ic=ic, cc=cc, bk=bk, xa=xa:
                         t.matmul(bk[:], wqa_sb[:, ic, cc * 128:(cc + 1) * 128],
                                  xa[:, ic, :], start=(ic == 0),
                                  stop=(ic == NC_I - 1)))
                        for ic in range(NC_I)],
                        reads=[t_wqa, t_xa], writes=[tb])
                    sb, tsb = sqb[cc % 2], t_sqb[cc % 2]
                    if A_SQ:
                        op(E, "scalar", lambda s_, bk=bk, sb=sb: s_.activation(
                            sb[:], bk[:], AF.Square, bias=0.0, scale=1.0),
                           reads=[tb], writes=[tsb])
                    if A_RMS:
                        mmg(E, [(lambda t, sb=sb, cc=cc: t.matmul(
                            pb[6][0:1, :], P["ones_col"][:], sb[:],
                            start=(cc == 0), stop=(cc == NC_Q - 1),
                            skip_group_check=True))],
                            reads=[tsb, P["t_oc"]], writes=[tpb[6]])
                    cst, t_cst = cqstage[cc % 2], t_cqst[cc % 2]
                    if A_TS:
                        op(E, "vector", lambda v, bk=bk, cst=cst, cc=cc:
                           v.tensor_scalar(cst[:], bk[:],
                                           qaln_sb[:, cc:cc + 1], None,
                                           ALU.mult),
                           reads=[tb, t_qaln], writes=[t_cst])
                    if A_CQS:
                        op(E, "sync", lambda sy, cc=cc, cols=cols, cst=cst:
                           sy.dma_start(out=io["cqs"].ap()[cc, :, cols],
                                        in_=cst[:]),
                           reads=[t_cst], writes=[tk["cqs"]])
                if A_RMS:
                    op(E, "scalar", lambda s_: s_.activation(
                        rowt[:], pb[6][0:1, :], AF.Sqrt,
                        bias=P["eps_rms"][0:1, :], scale=1.0 / QLR),
                       reads=[tpb[6], P["t_eps"]], writes=[t_rowt])
                    op(E, "vector", lambda v, cols=cols: v.reciprocal(
                        rstdq[0:1, cols], rowt[:]), reads=[t_rowt], writes=[t_rq])

        E.join()
        with ExitStack() as a1:
            ee = a1.enter_context
            wkva_sb = ee(nc.sbuf_tensor("wkva_sb", [128, NC_I, KVLR + DR], BF16))
            kvln_sb = ee(nc.sbuf_tensor("kvln_sb", [128, NC_KV], F32))
            xb2 = [ee(nc.sbuf_tensor(f"xbf{i}", [128, NC_I, 512], BF16))
                   for i in range(2)]
            sqb = [ee(nc.sbuf_tensor(f"sqd{i}", [128, 512], BF16))
                   for i in range(2)]
            kro1 = ee(nc.sbuf_tensor("kro1", [DR, 512], F32))
            kro2 = ee(nc.sbuf_tensor("kro2", [DR, 512], F32))
            rowtv = ee(nc.sbuf_tensor("rowtb", [1, 512], F32))
            t_wkva = Trk(); t_kvln = Trk(); t_xb2 = [Trk(), Trk()]
            t_sqb = [Trk(), Trk()]; t_kro1 = Trk(); t_kro2 = Trk(); t_rowtv = Trk()

            op(E, "gpsimd", lambda g: g.dma_start(
                out=wkva_sb[:],
                in_=io["wkvaT"].ap().rearrange("(c p) f -> p c f", p=128)),
               writes=[t_wkva])
            op(E, "sync", lambda sy: sy.dma_start(out=kvln_sb[:],
                                                  in_=io["kvln"][:, :]),
               writes=[t_kvln])
            for it in range(S // 512 if A_A1B else 0):
                cols = slice(it * 512, (it + 1) * 512)
                xb, t_xb = xb2[it % 2], t_xb2[it % 2]
                op(E, "gpsimd", lambda g, cols=cols, xb=xb: g.dma_start(
                    out=xb[:],
                    in_=io["xa_fm"].ap()[:, cols]
                        .rearrange("(c p) t -> p c t", p=128)),
                   writes=[t_xb])
                for cc in range(NC_KV):
                    bk, tb = pb[cc], tpb[cc]
                    mmg(E, [
                        (lambda t, ic=ic, cc=cc, bk=bk, xb=xb:
                         t.matmul(bk[:], wkva_sb[:, ic, cc * 128:(cc + 1) * 128],
                                  xb[:, ic, :], start=(ic == 0),
                                  stop=(ic == NC_I - 1)))
                        for ic in range(NC_I)],
                        reads=[t_wkva, t_xb], writes=[tb])
                    sb, tsb = sqb[cc % 2], t_sqb[cc % 2]
                    op(E, "scalar", lambda s_, bk=bk, sb=sb: s_.activation(
                        sb[:], bk[:], AF.Square, bias=0.0, scale=1.0),
                       reads=[tb], writes=[tsb])
                    mmg(E, [(lambda t, sb=sb, cc=cc: t.matmul(
                        pb[6][0:1, :], P["ones_col"][:], sb[:],
                        start=(cc == 0), stop=(cc == NC_KV - 1),
                        skip_group_check=True))],
                        reads=[tsb, P["t_oc"]], writes=[tpb[6]])
                    op(E, "vector", lambda v, bk=bk, cc=cc, cols=cols:
                       v.tensor_scalar(ckvg[:, cc, cols], bk[:],
                                       kvln_sb[:, cc:cc + 1], None, ALU.mult),
                       reads=[tb, t_kvln], writes=[t_ckvg])
                op(E, "scalar", lambda s_: s_.activation(
                    rowtv[:], pb[6][0:1, :], AF.Sqrt, bias=P["eps_rms"][0:1, :],
                    scale=1.0 / KVLR),
                   reads=[tpb[6], P["t_eps"]], writes=[t_rowtv])
                op(E, "vector", lambda v, cols=cols: v.reciprocal(
                    rstdkv[0:1, cols], rowtv[:]), reads=[t_rowtv], writes=[t_rkv])
                mmg(E, [
                    (lambda t, ic=ic, xb=xb:
                     t.matmul(pb[7][0:DR, :], wkva_sb[:, ic, KVLR:KVLR + DR],
                              xb[:, ic, :], start=(ic == 0),
                              stop=(ic == NC_I - 1)))
                    for ic in range(NC_I)],
                    reads=[t_wkva, t_xb], writes=[tpb[7]])
                op(E, "vector", lambda v, cols=cols: v.tensor_mul(
                    kro1[:], pb[7][0:DR, :], trig0[0:DR, cols]),
                   reads=[tpb[7], t_cos], writes=[t_kro1])
                op(E, "vector", lambda v, cols=cols: v.scalar_tensor_tensor(
                    kro2[0:32, :], pb[7][32:64, :], -1.0, trig0[DR:DR + 32, cols],
                    ALU.mult, ALU.mult), reads=[tpb[7], t_sin], writes=[t_kro2])
                op(E, "vector", lambda v, cols=cols: v.scalar_tensor_tensor(
                    kro2[32:64, :], pb[7][0:32, :], 1.0, trig0[DR + 32:2 * DR, cols],
                    ALU.mult, ALU.mult), reads=[tpb[7], t_sin], writes=[t_kro2])
                op(E, "vector", lambda v: v.tensor_add(kro1[:], kro1[:],
                                                       kro2[:]),
                   reads=[t_kro2], writes=[t_kro1])
                op(E, "vector", lambda v, cols=cols: v.tensor_copy(
                    kpe[0:DR, cols], kro1[:]), reads=[t_kro1], writes=[t_kpe])
                sb, tsb = sqb[0], t_sqb[0]
                op(E, "scalar", lambda s_, sb=sb: s_.activation(
                    sb[0:DR, :], kro1[:], AF.Square, bias=0.0, scale=1.0),
                   reads=[t_kro1], writes=[tsb])
                mmg(E, [(lambda t, sb=sb: t.matmul(
                    pb[6][0:1, :], P["ones_col"][0:DR, :], sb[0:DR, :],
                    start=True, stop=True))],
                    reads=[tsb, P["t_oc"]], writes=[tpb[6]])
                op(E, "vector", lambda v, cols=cols: v.tensor_copy(
                    kpen2[0:1, cols], pb[6][0:1, :]), reads=[tpb[6]],
                   writes=[t_kpen2])

            for tc in range(S // 128 if A_COL else 0):
                mmg(E, [(lambda t, tc=tc: t.matmul(
                    pb[5][:, 0:1], rstdkv[0:1, tc * 128:(tc + 1) * 128],
                    P["ones_r32"][0:1, 0:1], start=True, stop=True))],
                    reads=[t_rkv, P["t_or"]], writes=[tpb[5]])
                op(E, "vector", lambda v, tc=tc: v.tensor_copy(
                    rstdkvT[:, tc:tc + 1], pb[5][:, 0:1]),
                   reads=[tpb[5]], writes=[t_rkvT])

        E.join()
        with ExitStack() as a2:
            e2 = a2.enter_context
            knope = e2(nc.sbuf_tensor("knope", [128, S], BF16))
            vaug = e2(nc.sbuf_tensor("vaug", [128, S // 128, DV + 1], BF16))
            q1 = e2(nc.sbuf_tensor("q1", [128, 512], BF16))
            q2 = e2(nc.sbuf_tensor("q2", [DR + 1, 512], BF16))
            maxk = e2(nc.sbuf_tensor("maxk", [1, 2], F32))
            kn2t = e2(nc.sbuf_tensor("kn2t", [1, 512], F32))
            qnt = e2(nc.sbuf_tensor("qnt", [1, 512], F32))
            mk1 = e2(nc.sbuf_tensor("mk1", [1, 1], F32))
            cqt = [e2(nc.sbuf_tensor(f"cqt{i}", [128, NC_Q, 512], BF16))
                   for i in range(2)]
            t_cqt = [Trk(), Trk()]
            wqb_sl = e2(nc.sbuf_tensor("wqb_sl", [128, NC_Q, DQ], BF16))
            wkvb_sl = e2(nc.sbuf_tensor("wkvb_sl", [128, NC_KV, DN + DV], BF16))
            pt = [e2(nc.sbuf_tensor(f"pt{i}", [128, 512], BF16)) for i in range(3)]
            sqb = [e2(nc.sbuf_tensor(f"sqc{i}", [128, 512], BF16)) for i in range(2)]
            rb = e2(nc.sbuf_tensor("rb", [128, 512], F32))
            qro1 = e2(nc.sbuf_tensor("qro1", [DR, 512], F32))
            qro2 = e2(nc.sbuf_tensor("qro2", [DR, 512], F32))
            oacc = e2(nc.sbuf_tensor("oacc", [128, 512], F32))
            lacc = e2(nc.sbuf_tensor("lacc", [1, 512], F32))
            m_old = e2(nc.sbuf_tensor("m_old", [1, 512], F32))
            m_new = e2(nc.sbuf_tensor("m_new", [1, 512], F32))
            arow = e2(nc.sbuf_tensor("arow", [1, 512], F32))
            rlrow = e2(nc.sbuf_tensor("rlrow", [1, 512], F32))
            rlb = e2(nc.sbuf_tensor("rlb", [128, 512], F32))
            attn_t = e2(nc.sbuf_tensor("attn_t", [128, 512], BF16))
            rowt2 = e2(nc.sbuf_tensor("rowt2", [1, 512], F32))
            t_knope = Trk(); t_vaug = Trk(); t_q1 = Trk(); t_q2 = Trk()
            t_kn2t = Trk(); t_qnt = Trk(); t_mk1 = Trk()
            t_maxk = Trk()
            t_wqb = Trk(); t_wkvb = Trk()
            t_pt = [Trk() for _ in range(3)]
            t_sqb = [Trk(), Trk()]
            t_rb = Trk(); t_qro1 = Trk(); t_qro2 = Trk()
            t_oacc = Trk(); t_lacc = Trk(); t_mo = Trk(); t_mn = Trk()
            t_ar = Trk(); t_rl = Trk(); t_rlb = Trk(); t_at = Trk()
            t_rowt2 = Trk()

            op(E, "vector", lambda v: v.memset(vaug.ap()[:, :, DV:DV + 1], 1.0),
               writes=[t_vaug])

            for hl in range(8 if A_STAGES >= 2 else 0):
                op(E, "gpsimd", lambda g, hl=hl: g.dma_start(
                    out=wqb_sl[:],
                    in_=io["wqbT_own"].ap()[:, hl * DQ:(hl + 1) * DQ]
                        .rearrange("(c p) f -> p c f", p=128)),
                   writes=[t_wqb])
                op(E, "gpsimd", lambda g, hl=hl: g.dma_start(
                    out=wkvb_sl[:],
                    in_=io["wkvbT_own"].ap()[:, hl * 256:(hl + 1) * 256]
                        .rearrange("(c p) f -> p c f", p=128)),
                   writes=[t_wkvb])
                for it in range(S // 512):
                    cols = slice(it * 512, (it + 1) * 512)
                    bk, tb = pb[it % 2], tpb[it % 2]
                    mmg(E, [
                        (lambda t, cc=cc, bk=bk, cols=cols:
                         t.matmul(bk[:], wkvb_sl[:, cc, 0:DN], ckvg[:, cc, cols],
                                  start=(cc == 0), stop=(cc == NC_KV - 1)))
                        for cc in range(NC_KV)],
                        reads=[t_wkvb, t_ckvg], writes=[tb])
                    sb, tsb = sqb[it % 2], t_sqb[it % 2]
                    op(E, "scalar", lambda s_, bk=bk, sb=sb: s_.activation(
                        sb[:], bk[:], AF.Square, bias=0.0, scale=1.0),
                       reads=[tb], writes=[tsb])
                    mmg(E, [(lambda t, sb=sb: t.matmul(
                        pb[6][0:1, :], P["ones_col"][:], sb[:],
                        start=True, stop=True))],
                        reads=[tsb, P["t_oc"]], writes=[tpb[6]])
                    op(E, "vector", lambda v, cols=cols: v.tensor_mul(
                        rowt2[:], rstdkv[0:1, cols], rstdkv[0:1, cols]),
                       reads=[t_rkv], writes=[t_rowt2])
                    op(E, "vector", lambda v: v.tensor_mul(
                        rowt2[:], rowt2[:], pb[6][0:1, :]),
                       reads=[tpb[6]], writes=[t_rowt2])
                    op(E, "vector", lambda v, cols=cols: v.tensor_add(
                        kn2t[:], rowt2[:], kpen2[0:1, cols]),
                       reads=[t_rowt2, t_kpen2], writes=[t_kn2t])
                    op(E, "vector", lambda v: v.reduce_max(
                        mk1[:], kn2t[:], axis=AXX),
                       reads=[t_kn2t], writes=[t_mk1])
                    half = it // 4
                    if it % 4 == 0:
                        op(E, "vector", lambda v, half=half: v.tensor_copy(
                            maxk[0:1, half:half + 1], mk1[:]),
                           reads=[t_mk1], writes=[t_maxk])
                    else:
                        op(E, "vector", lambda v, half=half: v.tensor_max(
                            maxk[0:1, half:half + 1], maxk[0:1, half:half + 1],
                            mk1[:]),
                           reads=[t_mk1], writes=[t_maxk])
                    mmg(E, [(lambda t, cols=cols: t.matmul(
                        pb[7][:, :], P["ones_r32"][:], rstdkv[0:1, cols],
                        start=True, stop=True))],
                        reads=[t_rkv, P["t_or"]], writes=[tpb[7]])
                    op(E, "vector", lambda v: v.tensor_copy(rb[:], pb[7][:, :]),
                       reads=[tpb[7]], writes=[t_rb])
                    op(E, "vector", lambda v, bk=bk, cols=cols: v.tensor_mul(
                        knope[:, cols], bk[:], rb[:]),
                       reads=[tb, t_rb], writes=[t_knope])
                op(E, "scalar", lambda s_: s_.activation(
                    maxk[:], maxk[:], AF.Sqrt, bias=0.0, scale=1.0),
                   reads=[t_maxk], writes=[t_maxk])
                for tc in range(S // 128):
                    bk, tb = pb[2 + (tc % 2)], tpb[2 + (tc % 2)]
                    mmg(E, [
                        (lambda t, cc=cc, bk=bk, tc=tc:
                         t.matmul(bk[:, 0:DV],
                                  ckvg[:, cc, tc * 128:(tc + 1) * 128],
                                  wkvb_sl[:, cc, DN:DN + DV],
                                  start=(cc == 0), stop=(cc == NC_KV - 1)))
                        for cc in range(NC_KV)],
                        reads=[t_wkvb, t_ckvg], writes=[tb])
                    op(E, "vector", lambda v, bk=bk, tc=tc: v.tensor_scalar(
                        vaug[:, tc, 0:DV], bk[:, 0:DV], rstdkvT[:, tc:tc + 1],
                        None, ALU.mult),
                       reads=[tb, t_rkvT], writes=[t_vaug])
                for mi in range(8):
                    g0 = mi * 512
                    cols = slice(g0, g0 + 512)
                    cqx, t_cqx = cqt[mi % 2], t_cqt[mi % 2]
                    op(E, "gpsimd", lambda g, cols=cols, cqx=cqx: g.dma_start(
                        out=cqx[:], in_=io["cqs"].ap()[:, :, cols]
                        .rearrange("c p t -> p c t")),
                       reads=[tk["cqs"]], writes=[t_cqx])
                    mmg(E, [(lambda t, cols=cols: t.matmul(
                        pb[7][:, :], P["ones_r32"][:], rstdq[0:1, cols],
                        start=True, stop=True))],
                        reads=[t_rq, P["t_or"]], writes=[tpb[7]])
                    op(E, "vector", lambda v: v.tensor_copy(rb[:], pb[7][:, :]),
                       reads=[tpb[7]], writes=[t_rb])
                    bk, tb = pb[0], tpb[0]
                    mmg(E, [
                        (lambda t, cc=cc, bk=bk, cqx=cqx:
                         t.matmul(bk[:], wqb_sl[:, cc, 0:DN], cqx[:, cc, :],
                                  start=(cc == 0), stop=(cc == NC_Q - 1)))
                        for cc in range(NC_Q)],
                        reads=[t_wqb, t_cqx], writes=[tb])
                    op(E, "vector", lambda v, bk=bk, cols=cols:
                       v.scalar_tensor_tensor(q1[:], bk[:], SCALE, rb[:],
                                              ALU.mult, ALU.mult),
                       reads=[tb, t_rb], writes=[t_q1])
                    bk2, tb2 = pb[1], tpb[1]
                    mmg(E, [
                        (lambda t, cc=cc, bk2=bk2, cqx=cqx:
                         t.matmul(bk2[0:DR, :], wqb_sl[:, cc, DN:DQ],
                                  cqx[:, cc, :],
                                  start=(cc == 0), stop=(cc == NC_Q - 1)))
                        for cc in range(NC_Q)],
                        reads=[t_wqb, t_cqx], writes=[tb2])
                    op(E, "vector", lambda v, bk2=bk2, cols=cols: v.tensor_mul(
                        qro1[:], bk2[0:DR, :], trig0[0:DR, cols]),
                       reads=[tb2, t_cos], writes=[t_qro1])
                    op(E, "vector", lambda v, bk2=bk2, cols=cols:
                       v.scalar_tensor_tensor(qro2[0:32, :], bk2[32:64, :], -1.0,
                                              trig0[DR:DR + 32, cols], ALU.mult, ALU.mult),
                       reads=[tb2, t_sin], writes=[t_qro2])
                    op(E, "vector", lambda v, bk2=bk2, cols=cols:
                       v.scalar_tensor_tensor(qro2[32:64, :], bk2[0:32, :], 1.0,
                                              trig0[DR + 32:2 * DR, cols], ALU.mult, ALU.mult),
                       reads=[tb2, t_sin], writes=[t_qro2])
                    op(E, "vector", lambda v: v.tensor_add(qro1[:], qro1[:],
                                                           qro2[:]),
                       reads=[t_qro2], writes=[t_qro1])
                    op(E, "vector", lambda v: v.scalar_tensor_tensor(
                        q2[0:DR, :], qro1[:], SCALE, rb[0:DR, :],
                        ALU.mult, ALU.mult),
                       reads=[t_qro1, t_rb], writes=[t_q2])
                    sb, tsb = sqb[0], t_sqb[0]
                    op(E, "scalar", lambda s_, cols=cols, sb=sb: s_.activation(
                        sb[:], q1[:], AF.Square, bias=0.0, scale=1.0),
                       reads=[t_q1], writes=[tsb])
                    mmg(E, [(lambda t, sb=sb: t.matmul(
                        pb[6][0:1, :], P["ones_col"][:], sb[:],
                        start=True, stop=False, skip_group_check=True))],
                        reads=[tsb, P["t_oc"]], writes=[tpb[6]])
                    sb2, tsb2 = sqb[1], t_sqb[1]
                    op(E, "scalar", lambda s_, cols=cols, sb2=sb2: s_.activation(
                        sb2[0:DR, :], q2[0:DR, :], AF.Square, bias=0.0,
                        scale=1.0),
                       reads=[t_q2], writes=[tsb2])
                    mmg(E, [(lambda t, sb2=sb2: t.matmul(
                        pb[6][0:1, :], P["ones_col"][0:DR, :], sb2[0:DR, :],
                        start=False, stop=True, skip_group_check=True))],
                        reads=[tsb2, P["t_oc"]], writes=[tpb[6]])
                    op(E, "scalar", lambda s_, cols=cols: s_.activation(
                        qnt[:], pb[6][0:1, :], AF.Sqrt, bias=0.0,
                        scale=1.0),
                       reads=[tpb[6]], writes=[t_qnt])

                    halves = ([(0, g0 + 512)] if g0 < 2048
                              else [(0, 2048), (2048, g0 + 512)])
                    qcols = slice(g0, g0 + 512)
                    for ih, (h0, h1) in enumerate(halves):
                        two = len(halves) == 2
                        mrow_t = m_new if (two and ih == 1) else m_old
                        mrow_trk = t_mn if (two and ih == 1) else t_mo
                        op(E, "vector",
                           lambda v, qcols=qcols, ih=ih, mrow_t=mrow_t:
                           v.tensor_scalar(mrow_t[:], qnt[:],
                                           maxk[0:1, ih:ih + 1], 0.05,
                                           ALU.mult, ALU.add),
                           reads=[t_qnt, t_maxk], writes=[mrow_trk])
                        if two and ih == 1:
                            op(E, "vector", lambda v: v.tensor_max(
                                m_new[:], m_new[:], m_old[:]),
                               reads=[t_mo], writes=[t_mn])
                            op(E, "vector", lambda v: v.tensor_sub(
                                arow[:], m_old[:], m_new[:]),
                               reads=[t_mo, t_mn], writes=[t_ar])
                            op(E, "scalar", lambda s_: s_.activation(
                                arow[:], arow[:], AF.Exp, bias=0.0, scale=1.0),
                               reads=[t_ar], writes=[t_ar])
                        op(E, "vector", lambda v, qcols=qcols, mrow_t=mrow_t:
                           v.tensor_scalar_mul(q2[DR:DR + 1, :], mrow_t[:],
                                               -1.0),
                           reads=[mrow_trk], writes=[t_q2])
                        nkb = (h1 - h0) // 128
                        for ik in range(nkb):
                            kb = h0 + ik * 128
                            kcols = slice(kb, kb + 128)
                            stb, t_stb = pb[4 + (ik % 2)], tpb[4 + (ik % 2)]
                            mmg(E, [
                                (lambda t, kcols=kcols, qcols=qcols, stb=stb:
                                 t.matmul(stb[:], knope[:, kcols], q1[:],
                                          start=True, stop=False)),
                                (lambda t, kcols=kcols, qcols=qcols, stb=stb:
                                 t.matmul(stb[:], kpe[:, kcols], q2[:],
                                          start=False, stop=True)),
                            ], reads=[t_knope, t_kpe, t_q1, t_q2], writes=[t_stb])
                            ptb, t_ptb = pt[ik % 3], t_pt[ik % 3]
                            op(E, "scalar", lambda s_, stb=stb, ptb=ptb:
                               s_.activation(ptb[:], stb[:], AF.Exp, bias=0.0,
                                             scale=1.0),
                               reads=[t_stb], writes=[t_ptb])
                            qlo = 0
                            if kb >= g0:
                                qlo = ((kb - g0) // 128) * 128
                                op(E, "vector", lambda v, ptb=ptb, qlo=qlo:
                                   v.tensor_mul(ptb[:, qlo:qlo + 128],
                                                ptb[:, qlo:qlo + 128],
                                                P["ltri"][:]),
                                   reads=[P["t_ltri"]], writes=[t_ptb])
                            kb_i = kb // 128
                            mmg(E, [(lambda t, ptb=ptb, kb_i=kb_i, qlo=qlo,
                                     ik=ik, nkb=nkb: t.matmul(
                                         pb[2][:, qlo:512], vaug[:, kb_i, 0:DV],
                                         ptb[:, qlo:512], start=(ik == 0),
                                         stop=(ik == nkb - 1),
                                         skip_group_check=True))],
                                reads=[t_ptb, t_vaug], writes=[tpb[2]])
                            mmg(E, [(lambda t, ptb=ptb, qlo=qlo, ik=ik, nkb=nkb:
                                     t.matmul(pb[3][0:1, qlo:512],
                                              P["ones_col"][:],
                                              ptb[:, qlo:512], start=(ik == 0),
                                              stop=(ik == nkb - 1),
                                              skip_group_check=True))],
                                reads=[t_ptb, P["t_oc"]], writes=[tpb[3]])
                        if two and ih == 0:
                            op(E, "vector", lambda v: v.tensor_copy(
                                oacc[:], pb[2][:, :]),
                               reads=[tpb[2]], writes=[t_oacc])
                            op(E, "vector", lambda v: v.tensor_copy(
                                lacc[:], pb[3][0:1, :]),
                               reads=[tpb[3]], writes=[t_lacc])
                            continue
                        if two:
                            mmg(E, [(lambda t: t.matmul(
                                pb[7][:, :], P["ones_r32"][:], arow[:],
                                start=True, stop=True))],
                                reads=[t_ar, P["t_or"]], writes=[tpb[7]])
                            op(E, "vector", lambda v: v.tensor_copy(
                                rb[:], pb[7][:, :]), reads=[tpb[7]],
                               writes=[t_rb])
                            op(E, "vector", lambda v: v.tensor_mul(
                                oacc[:], oacc[:], rb[:]),
                               reads=[t_rb], writes=[t_oacc])
                            op(E, "vector", lambda v: v.tensor_add(
                                oacc[:], oacc[:], pb[2][:, :]),
                               reads=[tpb[2]], writes=[t_oacc])
                            op(E, "vector", lambda v: v.tensor_mul(
                                lacc[:], lacc[:], arow[:]),
                               reads=[t_ar], writes=[t_lacc])
                            op(E, "vector", lambda v: v.tensor_add(
                                lacc[:], lacc[:], pb[3][0:1, :]),
                               reads=[tpb[3]], writes=[t_lacc])
                            op(E, "vector", lambda v: v.reciprocal(
                                rlrow[:], lacc[:]), reads=[t_lacc],
                               writes=[t_rl])
                        else:
                            op(E, "vector", lambda v: v.reciprocal(
                                rlrow[:], pb[3][0:1, :]),
                               reads=[tpb[3]], writes=[t_rl])
                        mmg(E, [(lambda t: t.matmul(
                            pb[7][:, :], P["ones_r32"][:], rlrow[:],
                            start=True, stop=True))],
                            reads=[t_rl, P["t_or"]], writes=[tpb[7]])
                        op(E, "vector", lambda v: v.tensor_copy(
                            rlb[:], pb[7][:, :]), reads=[tpb[7]], writes=[t_rlb])
                        if two:
                            op(E, "vector", lambda v: v.tensor_mul(
                                attn_t[:], oacc[:], rlb[:]),
                               reads=[t_oacc, t_rlb], writes=[t_at])
                        else:
                            op(E, "vector", lambda v: v.tensor_mul(
                                attn_t[:], pb[2][:, :], rlb[:]),
                               reads=[tpb[2], t_rlb], writes=[t_at])
                        op(E, "sync", lambda sy, hl=hl, qcols=qcols: sy.dma_start(
                            out=io["attnin"].ap()[hl * 128:(hl + 1) * 128, qcols],
                            in_=attn_t[:]),
                           reads=[t_at], writes=[tk["attnin"]])
                if hl % 2 == 1 and A_DO_AG:
                    k = hl // 2
                    op(E, "gpsimd", lambda g, k=k: g.collective_compute(
                        "AllGather", ALU.bypass,
                        replica_groups=[[0, 1], [2, 3], [4, 5], [6, 7]],
                        ins=[io["attnin"].ap()[k * 256:(k + 1) * 256, :].opt()],
                        outs=[io["attnout"].ap()[k].opt()]),
                       reads=[tk["attnin"]], writes=[tk["attnout"]], inc=1,
                       q="cc")

    E.join()
    if A_STAGES < 3:
        return
    with ExitStack() as a3:
        e3 = a3.enter_context
        attn_all = e3(nc.sbuf_tensor("attn_all", [128, H, TOWN], BF16))
        wo_sb = e3(nc.sbuf_tensor("wo_sb", [128, NC_I, D], BF16))
        x0t = e3(nc.sbuf_tensor("x0t", [128, D], F32))
        x1t = e3(nc.sbuf_tensor("x1t", [128, D], F32))
        x1o = e3(nc.sbuf_tensor("x1o", [128, D], F32))
        osb = e3(nc.sbuf_tensor("osb", [1, 1], I32))
        lns = mk_ln(nc, a3, "A")
        t_aall = Trk(); t_wo = Trk(); t_x0 = Trk(); t_x1 = Trk(); t_x1o = Trk()
        t_osb = Trk()

        op(E, "gpsimd", lambda g: g.dma_start(
            out=wo_sb[:], in_=io["woT"].ap().rearrange("(c p) f -> p c f", p=128)),
           writes=[t_wo])
        op(E, "gpsimd", lambda g: g.dma_start(out=osb[:], in_=io["ownoff"][:, :]),
           writes=[t_osb])
        rbase = a3.enter_context(nc.gpsimd.register("rbase"))
        rdyn = a3.enter_context(nc.gpsimd.register("rdyn"))
        E.raw_noinc("gpsimd", lambda g: g.reg_load(rbase, osb[0:1, 0:1]),
                    deps=_deps([t_osb], [], []))
        for hg in range(H):
            rp, k, loc = hg // 8, (hg % 8) // 2, hg % 2
            base = (k * 512 + rp * 256 + loc * 128) * S
            if A_DYN_OFF:
                E.raw_noinc("gpsimd",
                            lambda g, base=base: g.reg_add(rdyn, rbase, base))
                op(E, "gpsimd", lambda g, hg=hg: g.dma_start(
                    out=attn_all[:, hg, :],
                    in_=bass.AP(io["attnout"], rdyn, [[S, 128], [1, TOWN]])),
                   reads=[tk["attnout"]], writes=[t_aall])
            else:
                src = (io["attnout"] if A_DO_AG else io["attnin"])
                hl = hg % 8
                rows = ((k * 512 + rp * 256 + loc * 128) if A_DO_AG
                        else hl * 128)
                op(E, "gpsimd", lambda g, hg=hg, src=src, rows=rows:
                   g.dma_start(
                       out=attn_all[:, hg, :],
                       in_=bass.AP(src, rows * S, [[S, 128], [1, TOWN]])),
                   reads=[tk["attnout" if A_DO_AG else "attnin"]],
                   writes=[t_aall])
        for tc in range(TOWN // 128):
            trows = slice(tc * 128, (tc + 1) * 128)
            op(E, "sync", lambda sy, trows=trows: sy.dma_start(
                out=x0t[:], in_=io["xo_tm"].ap()[trows, :]),
               reads=[], writes=[t_x0])
            for ft in range(4):
                fcols = slice(ft * 512, (ft + 1) * 512)
                bk, tb = pb[ft % 2], tpb[ft % 2]
                mmg(E, [
                    (lambda t, hgi=hgi, tc=tc, ft=ft, bk=bk:
                     t.matmul(bk[:], attn_all[:, hgi, tc * 128:(tc + 1) * 128],
                              wo_sb[:, hgi, ft * 512:(ft + 1) * 512],
                              start=(hgi == 0), stop=(hgi == H - 1)))
                    for hgi in range(H)],
                    reads=[t_aall, t_wo], writes=[tb])
                op(E, "vector", lambda v, bk=bk, fcols=fcols: v.tensor_add(
                    x1t[:, fcols], bk[:], x0t[:, fcols]),
                   reads=[tb, t_x0], writes=[t_x1])
            ln_apply(E, lns, x1t[:], t_x1, x1o[:], t_x1o, P)
            op(E, "sync", lambda sy, trows=trows: sy.dma_start(
                out=io["x1s"].ap()[trows, :], in_=x1o[:]),
               reads=[t_x1o], writes=[tk["x1s"]])


def phase_B(E, nc, io, tk, P, ap):
    ecm = ap.enter_context
    pb = [ecm(nc.psum_tensor(f"psB{i}", [128, 512], F32)) for i in range(6)]
    tpb = [Trk(True) for _ in range(6)]
    pbt = ecm(nc.psum_tensor("psBt", [128, 128], BF16))
    t_pbt = Trk(True)

    x1fmh = ecm(nc.sbuf_tensor("x1fmh", [128, NC_I, 1024], BF16))
    y_acc = ecm(nc.sbuf_tensor("y_acc", [128, 8, D], F32))
    u_bf = ecm(nc.sbuf_tensor("u_bf", [128, NJ, 1024], BF16))
    w2_sb = ecm(nc.sbuf_tensor("w2_sb", [128, NJ, D], BF16))
    w1_sl = [ecm(nc.sbuf_tensor(f"w1_sl{i}", [128, NC_I, 128], BF16))
             for i in range(2)]
    w3_sl = [ecm(nc.sbuf_tensor(f"w3_sl{i}", [128, NC_I, 128], BF16))
             for i in range(2)]
    gw_sb = ecm(nc.sbuf_tensor("gw_sb", [128, NC_I, E_EXP], BF16))
    wgt = ecm(nc.sbuf_tensor("wgt", [128, 16, E_EXP], F32))
    silu_f = ecm(nc.sbuf_tensor("silu_f", [128, 512], F32))
    x1t = ecm(nc.sbuf_tensor("x1tB", [128, D], F32))
    x1bf = ecm(nc.sbuf_tensor("x1bfB", [128, D], BF16))
    x2t = ecm(nc.sbuf_tensor("x2tB", [128, D], F32))
    g8 = ecm(nc.sbuf_tensor("g8", [128, E_EXP], F32))
    gcol = [ecm(nc.sbuf_tensor(f"gcol{i}", [128, 1], F32)) for i in range(4)]
    lns = mk_ln(nc, ap, "B")
    t_x1fmh = Trk(); t_yacc = Trk(); t_u = Trk(); t_w2 = Trk()
    t_w1s = [Trk(), Trk()]; t_w3s = [Trk(), Trk()]
    t_gw = Trk(); t_wgt = Trk(); t_silu = Trk()
    t_x1t = Trk(); t_x1bf = Trk(); t_x2t = Trk()
    t_g8 = Trk(); t_gc = [Trk() for _ in range(4)]

    op(E, "gpsimd", lambda g: g.dma_start(
        out=gw_sb[:], in_=io["gwT"].ap().rearrange("(c p) f -> p c f", p=128)),
       writes=[t_gw])

    for half in range(2):
        hoff = half * 1024
        for tci in range(8):
            tc = half * 8 + tci
            trows = slice(tc * 128, (tc + 1) * 128)
            op(E, "sync", lambda sy, trows=trows: sy.dma_start(
                out=x1t[:], in_=io["x1s"].ap()[trows, :]),
               reads=[tk["x1s"]], writes=[t_x1t])
            op(E, "scalar", lambda s_: s_.activation(
                x1bf[:], x1t[:], AF.Copy, bias=0.0, scale=1.0),
               reads=[t_x1t], writes=[t_x1bf])
            for ic in range(NC_I):
                mmg(E, [(lambda t, ic=ic: t.transpose(
                    pbt[:], x1bf[:, ic * 128:(ic + 1) * 128], P["ident"][:]))],
                    reads=[t_x1bf, P["t_ident"]], writes=[t_pbt])
                op(E, "vector", lambda v, ic=ic, tci=tci: v.tensor_copy(
                    x1fmh[:, ic, tci * 128:(tci + 1) * 128], pbt[:]),
                   reads=[t_pbt], writes=[t_x1fmh])
            mmg(E, [
                (lambda t, ic=ic, tci=tci:
                 t.matmul(pb[4][:, 0:E_EXP], x1fmh[:, ic,
                                                   tci * 128:(tci + 1) * 128],
                          gw_sb[:, ic, :], start=(ic == 0),
                          stop=(ic == NC_I - 1)))
                for ic in range(NC_I)],
                reads=[t_x1fmh, t_gw], writes=[tpb[4]])
            op(E, "vector", lambda v: v.tensor_copy(g8[:], pb[4][:, 0:E_EXP]),
               reads=[tpb[4]], writes=[t_g8])
            op(E, "vector", lambda v: v.reduce_max(gcol[0][:], g8[:], axis=AXX),
               reads=[t_g8], writes=[t_gc[0]])
            op(E, "vector", lambda v: v.tensor_scalar_mul(gcol[0][:], gcol[0][:],
                                                          -1.0),
               reads=[t_gc[0]], writes=[t_gc[0]])
            op(E, "scalar", lambda s_: s_.activation(
                g8[:], g8[:], AF.Exp, bias=gcol[0][:], scale=1.0,
                accum_out=gcol[1][:]),
               reads=[t_g8, t_gc[0]], writes=[t_g8, t_gc[1]])
            op(E, "vector", lambda v: v.reciprocal(gcol[1][:], gcol[1][:]),
               reads=[t_gc[1]], writes=[t_gc[1]])
            op(E, "vector", lambda v: v.tensor_scalar(
                g8[:], g8[:], gcol[1][:], None, ALU.mult),
               reads=[t_gc[1]], writes=[t_g8])
            op(E, "vector", lambda v: v.reduce_max(gcol[0][:], g8[:], axis=AXX),
               reads=[t_g8], writes=[t_gc[0]])
            op(E, "vector", lambda v: v.tensor_scalar(
                silu_f[:, 0:E_EXP], g8[:], gcol[0][:], None, ALU.is_ge),
               reads=[t_g8, t_gc[0]], writes=[t_silu])
            op(E, "vector", lambda v: v.scalar_tensor_tensor(
                silu_f[:, 0:E_EXP], silu_f[:, 0:E_EXP], -1e30, g8[:],
                ALU.mult, ALU.add),
               reads=[t_g8], writes=[t_silu])
            op(E, "vector", lambda v: v.reduce_max(gcol[1][:], silu_f[:, 0:E_EXP],
                                                   axis=AXX),
               reads=[t_silu], writes=[t_gc[1]])
            op(E, "vector", lambda v: v.tensor_scalar(
                silu_f[:, 0:E_EXP], g8[:], gcol[1][:], None, ALU.is_ge),
               reads=[t_g8, t_gc[1]], writes=[t_silu])
            op(E, "vector", lambda v: v.tensor_add(gcol[2][:], gcol[0][:],
                                                   gcol[1][:]),
               reads=[t_gc[0], t_gc[1]], writes=[t_gc[2]])
            op(E, "vector", lambda v: v.reciprocal(gcol[2][:], gcol[2][:]),
               reads=[t_gc[2]], writes=[t_gc[2]])
            op(E, "vector", lambda v, tc=tc: v.scalar_tensor_tensor(
                wgt[:, tc, :], g8[:], gcol[2][:], silu_f[:, 0:E_EXP],
                ALU.mult, ALU.mult),
               reads=[t_g8, t_gc[2], t_silu], writes=[t_wgt])
        for ep in range(10):
            is_sh = ep >= 8
            jlo = 0 if not is_sh else (ep - 8) * NJ
            njc = NJ
            if is_sh:
                src2 = io["sw2"].ap()[jlo * 128:(jlo + NJ) * 128, :]
            else:
                src2 = io["w2"].ap()[ep]
            for jc in range(njc):
                if jc == 1:
                    op(E, "gpsimd", lambda g, src2=src2: g.dma_start(
                        out=w2_sb[:],
                        in_=src2.rearrange("(c p) f -> p c f", p=128)),
                       writes=[t_w2])
                jg = jlo + jc
                if is_sh:
                    src1 = io["sw1"].ap()[:, jg * 128:(jg + 1) * 128]
                    src3 = io["sw3"].ap()[:, jg * 128:(jg + 1) * 128]
                else:
                    src1 = io["w1"].ap()[ep][:, jg * 128:(jg + 1) * 128]
                    src3 = io["w3"].ap()[ep][:, jg * 128:(jg + 1) * 128]
                wb = jc % 2
                op(E, "gpsimd", lambda g, src1=src1, wb=wb: g.dma_start(
                    out=w1_sl[wb][:],
                    in_=src1.rearrange("(c p) f -> p c f", p=128)),
                   writes=[t_w1s[wb]])
                op(E, "gpsimd", lambda g, src3=src3, wb=wb: g.dma_start(
                    out=w3_sl[wb][:],
                    in_=src3.rearrange("(c p) f -> p c f", p=128)),
                   writes=[t_w3s[wb]])
                for tt in range(2):
                    tcols = slice(tt * 512, (tt + 1) * 512)
                    bh, tbh = pb[0 + (tt % 2)], tpb[0 + (tt % 2)]
                    bg, tbg = pb[2 + (tt % 2)], tpb[2 + (tt % 2)]
                    mmg(E, [
                        (lambda t, ic=ic, bh=bh, tcols=tcols, wb=wb:
                         t.matmul(bh[:], w1_sl[wb][:, ic, :],
                                  x1fmh[:, ic, tcols],
                                  start=(ic == 0), stop=(ic == NC_I - 1)))
                        for ic in range(NC_I)],
                        reads=[t_w1s[wb], t_x1fmh], writes=[tbh])
                    mmg(E, [
                        (lambda t, ic=ic, bg=bg, tcols=tcols, wb=wb:
                         t.matmul(bg[:], w3_sl[wb][:, ic, :],
                                  x1fmh[:, ic, tcols],
                                  start=(ic == 0), stop=(ic == NC_I - 1)))
                        for ic in range(NC_I)],
                        reads=[t_w3s[wb], t_x1fmh], writes=[tbg])
                    op(E, "scalar", lambda s_, bh=bh: s_.activation(
                        silu_f[:], bh[:], AF.Silu, bias=0.0, scale=1.0),
                       reads=[tbh], writes=[t_silu])
                    op(E, "vector", lambda v, bg=bg, jc=jc, tcols=tcols:
                       v.tensor_mul(u_bf[:, jc, tcols], silu_f[:], bg[:]),
                       reads=[t_silu, tbg], writes=[t_u])
            for ot in range(4):
                ocols = slice(ot * 512, (ot + 1) * 512)
                for tci in range(8):
                    tc = half * 8 + tci
                    bk, tb = pb[2 + (tci % 4)], tpb[2 + (tci % 4)]
                    mmg(E, [
                        (lambda t, jc=jc, tci=tci, bk=bk, ocols=ocols:
                         t.matmul(bk[:], u_bf[:, jc, tci * 128:(tci + 1) * 128],
                                  w2_sb[:, jc, ocols], start=(jc == 0),
                                  stop=(jc == NJ - 1)))
                        for jc in range(NJ)],
                        reads=[t_u, t_w2], writes=[tb])
                    if ep == 0:
                        op(E, "vector",
                           lambda v, bk=bk, tci=tci, tc=tc, ocols=ocols:
                           v.tensor_scalar(y_acc[:, tci, ocols], bk[:],
                                           wgt[:, tc, 0:1], None, ALU.mult),
                           reads=[tb, t_wgt], writes=[t_yacc])
                    elif not is_sh:
                        op(E, "vector",
                           lambda v, bk=bk, tci=tci, tc=tc, ocols=ocols, ep=ep:
                           v.scalar_tensor_tensor(
                               y_acc[:, tci, ocols], bk[:], wgt[:, tc, ep:ep + 1],
                               y_acc[:, tci, ocols], ALU.mult, ALU.add),
                           reads=[tb, t_wgt], writes=[t_yacc])
                    else:
                        op(E, "vector",
                           lambda v, bk=bk, tci=tci, ocols=ocols:
                           v.scalar_tensor_tensor(
                               y_acc[:, tci, ocols], bk[:], 1.0,
                               y_acc[:, tci, ocols], ALU.mult, ALU.add),
                           reads=[tb], writes=[t_yacc])
        for tci in range(8):
            tc = half * 8 + tci
            trows = slice(tc * 128, (tc + 1) * 128)
            op(E, "sync", lambda sy, trows=trows: sy.dma_start(
                out=x1t[:], in_=io["x1s"].ap()[trows, :]),
               reads=[tk["x1s"]], writes=[t_x1t])
            op(E, "vector", lambda v, tci=tci: v.tensor_add(
                x2t[:], x1t[:], y_acc[:, tci, :]),
               reads=[t_x1t, t_yacc], writes=[t_x2t])
            ln_apply(E, lns, x2t[:], t_x2t, x1t[:], t_x1t, P)
            op(E, "sync", lambda sy, trows=trows: sy.dma_start(
                out=io["x2s"].ap()[trows, :], in_=x1t[:]),
               reads=[t_x1t], writes=[tk["x2s"]])


def phase_C(E, nc, io, tk, P, ap):
    ecm = ap.enter_context
    pb = [ecm(nc.psum_tensor(f"psC{i}", [128, 512], F32)) for i in range(6)]
    tpb = [Trk(True) for _ in range(6)]
    pbm = ecm(nc.psum_tensor("psCm", [128, HD + 1], F32))
    t_pbm = Trk(True)
    pbt = ecm(nc.psum_tensor("psCt", [128, 128], BF16))
    t_pbt = Trk(True)

    x2fm = ecm(nc.sbuf_tensor("x2fm", [128, NC_I, TOWN], BF16))
    kffm = ecm(nc.sbuf_tensor("kffm", [128, KVH, TOWN], BF16))
    vaug1 = ecm(nc.sbuf_tensor("vaug1", [128, 16, KVH, HD + 1], BF16))
    states = ecm(nc.sbuf_tensor("states", [128, KVH, HD + 1], F32))
    statebf = ecm(nc.sbuf_tensor("statebf", [128, KVH, HD + 1], BF16))
    inits = ecm(nc.sbuf_tensor("inits", [128, 2, KVH, HD + 1], F32))
    summ_sb = ecm(nc.sbuf_tensor("summ_sb", [128, 2, KVH, SUMC], F32))
    smask_sb = ecm(nc.sbuf_tensor("smask_sb", [128, 2, 16], F32))
    ktm_t = ecm(nc.sbuf_tensor("ktm_t", [128, 128], BF16))
    t_x2fm = Trk(); t_kffm = Trk(); t_vaug = Trk()
    t_st = Trk(); t_stbf = Trk(); t_inits = Trk()
    t_summ = Trk(); t_smask = Trk(); t_ktm = Trk()

    op(E, "sync", lambda sy: sy.dma_start(out=smask_sb[:],
                                          in_=io["smask"][:, :, :]),
       writes=[t_smask])
    op(E, "vector", lambda v: v.memset(vaug1.ap()[:, :, :, HD:HD + 1], 1.0),
       writes=[t_vaug])
    op(E, "vector", lambda v: v.memset(summ_sb[:], 0.0), writes=[t_summ])

    def ktm_make(tc, kvh):
        mmg(E, [(lambda t, tc=tc, kvh=kvh: t.transpose(
            pbt[:], kffm[:, kvh, tc * 128:(tc + 1) * 128], P["ident"][:]))],
            reads=[t_kffm, P["t_ident"]], writes=[t_pbt])
        op(E, "vector", lambda v: v.tensor_copy(ktm_t[:], pbt[:]),
           reads=[t_pbt], writes=[t_ktm])

    _rope_ctr = [0]

    def mk_rope(scope):
        _rope_ctr[0] += 1
        i = _rope_ctr[0]
        r = {nm: scope.enter_context(nc.sbuf_tensor(f"{nm}_{i}", [HD, 512], F32))
             for nm in ("cosx", "sinx", "ro1", "ro2")}
        r["elu"] = r["ro2"]
        r.update({"t_cos": Trk(), "t_sin": Trk(), "t_ro1": Trk(),
                  "t_ro2": Trk()})
        r["t_elu"] = r["t_ro2"]
        return r

    def rope_elu(r, bank, tb, out_ap, out_trk):
        hh = HD // 2
        op(E, "vector", lambda v: v.tensor_mul(r["ro1"][:], bank[0:HD, :],
                                               r["cosx"][:]),
           reads=[tb, r["t_cos"]], writes=[r["t_ro1"]])
        op(E, "vector", lambda v: v.scalar_tensor_tensor(
            r["ro2"][0:hh, :], bank[hh:HD, :], -1.0, r["sinx"][0:hh, :],
            ALU.mult, ALU.mult), reads=[tb, r["t_sin"]], writes=[r["t_ro2"]])
        op(E, "vector", lambda v: v.scalar_tensor_tensor(
            r["ro2"][hh:HD, :], bank[0:hh, :], 1.0, r["sinx"][hh:HD, :],
            ALU.mult, ALU.mult), reads=[tb, r["t_sin"]], writes=[r["t_ro2"]])
        op(E, "vector", lambda v: v.tensor_add(r["ro1"][:], r["ro1"][:],
                                               r["ro2"][:]),
           reads=[r["t_ro2"]], writes=[r["t_ro1"]])
        op(E, "vector", lambda v: v.tensor_scalar_min(r["elu"][:], r["ro1"][:],
                                                      0.0),
           reads=[r["t_ro1"]], writes=[r["t_elu"]])
        op(E, "scalar", lambda s_: s_.activation(r["elu"][:], r["elu"][:],
                                                 AF.Exp, bias=0.0, scale=1.0),
           reads=[r["t_elu"]], writes=[r["t_elu"]])
        op(E, "vector", lambda v: v.scalar_tensor_tensor(
            out_ap, r["ro1"][:], 0.0, r["elu"][:], ALU.max, ALU.add),
           reads=[r["t_ro1"], r["t_elu"]], writes=[out_trk])

    with ExitStack() as c12:
        ee = c12.enter_context
        x2t = ee(nc.sbuf_tensor("x2tC", [128, D], F32))
        x2bf = ee(nc.sbuf_tensor("x2bfC", [128, D], BF16))
        wk_sb = ee(nc.sbuf_tensor("wk_sb", [128, NC_I, KVH * HD], BF16))
        wv_sb = ee(nc.sbuf_tensor("wv_sb", [128, NC_I, KVH * HD], BF16))
        rr = mk_rope(c12)
        t_x2t = Trk(); t_x2bf = Trk(); t_wk = Trk(); t_wv = Trk()

        op(E, "gpsimd", lambda g: g.dma_start(
            out=wk_sb[:],
            in_=io["wkT1"].ap().rearrange("(c p) f -> p c f", p=128)),
           writes=[t_wk])
        op(E, "gpsimd", lambda g: g.dma_start(
            out=wv_sb[:],
            in_=io["wvT1"].ap().rearrange("(c p) f -> p c f", p=128)),
           writes=[t_wv])
        for tc in range(16):
            trows = slice(tc * 128, (tc + 1) * 128)
            op(E, "sync", lambda sy, trows=trows: sy.dma_start(
                out=x2t[:], in_=io["x2s"].ap()[trows, :]),
               reads=[tk["x2s"]], writes=[t_x2t])
            op(E, "scalar", lambda s_: s_.activation(x2bf[:], x2t[:], AF.Copy,
                                                     bias=0.0, scale=1.0),
               reads=[t_x2t], writes=[t_x2bf])
            for ic in range(NC_I):
                mmg(E, [(lambda t, ic=ic: t.transpose(
                    pbt[:], x2bf[:, ic * 128:(ic + 1) * 128], P["ident"][:]))],
                    reads=[t_x2bf, P["t_ident"]], writes=[t_pbt])
                op(E, "vector", lambda v, ic=ic, tc=tc: v.tensor_copy(
                    x2fm[:, ic, tc * 128:(tc + 1) * 128], pbt[:]),
                   reads=[t_pbt], writes=[t_x2fm])
        for tt in range(4):
            tcols = slice(tt * 512, (tt + 1) * 512)
            op(E, "sync", lambda sy, tcols=tcols, rr=rr: sy.dma_start(
                out=rr["cosx"][:], in_=io["cos1"].ap()[:, tcols]),
               writes=[rr["t_cos"]])
            op(E, "sync", lambda sy, tcols=tcols, rr=rr: sy.dma_start(
                out=rr["sinx"][:], in_=io["sin1"].ap()[:, tcols]),
               writes=[rr["t_sin"]])
            for kvh in range(KVH):
                bk, tb = pb[kvh % 2], tpb[kvh % 2]
                mmg(E, [
                    (lambda t, ic=ic, kvh=kvh, bk=bk, tcols=tcols:
                     t.matmul(bk[0:HD, :],
                              wk_sb[:, ic, kvh * HD:(kvh + 1) * HD],
                              x2fm[:, ic, tcols], start=(ic == 0),
                              stop=(ic == NC_I - 1)))
                    for ic in range(NC_I)],
                    reads=[t_wk, t_x2fm], writes=[tb])
                rope_elu(rr, bk, tb, kffm.ap()[:, kvh, tcols], t_kffm)
            for tci in range(4):
                tc = tt * 4 + tci
                bk, tb = pb[2 + (tci % 2)], tpb[2 + (tci % 2)]
                mmg(E, [
                    (lambda t, ic=ic, tc=tc, bk=bk:
                     t.matmul(bk[:], x2fm[:, ic, tc * 128:(tc + 1) * 128],
                              wv_sb[:, ic, :], start=(ic == 0),
                              stop=(ic == NC_I - 1)))
                    for ic in range(NC_I)],
                    reads=[t_wv, t_x2fm], writes=[tb])
                for kvh in range(KVH):
                    op(E, "vector", lambda v, bk=bk, tc=tc, kvh=kvh:
                       v.tensor_copy(vaug1[:, tc, kvh, 0:HD],
                                     bk[:, kvh * HD:(kvh + 1) * HD]),
                       reads=[tb], writes=[t_vaug])

    E.join()
    for osc in range(2):
        for kvh in range(KVH):
            for c8 in range(8):
                tc = osc * 8 + c8
                ktm_make(tc, kvh)
                mmg(E, [(lambda t, tc=tc, kvh=kvh, c8=c8:
                         t.matmul(pbm[:], ktm_t[:], vaug1[:, tc, kvh, :],
                                  start=(c8 == 0), stop=(c8 == 7),
                                  skip_group_check=True))],
                    reads=[t_ktm, t_vaug], writes=[t_pbm])
            op(E, "vector", lambda v, osc=osc, kvh=kvh: v.tensor_copy(
                summ_sb[:, osc, kvh, 0:HD + 1], pbm[:]),
               reads=[t_pbm], writes=[t_summ])
            r0 = (osc * KVH + kvh) * 128
            op(E, "sync", lambda sy, r0=r0, osc=osc, kvh=kvh: sy.dma_start(
                out=io["summin"].ap()[r0:r0 + 128, :],
                in_=summ_sb[:, osc, kvh, :]),
               reads=[t_summ], writes=[tk["summin"]])
    op(E, "gpsimd", lambda g: g.collective_compute(
        "AllGather", ALU.bypass, replica_groups=[list(range(8))],
        ins=[io["summin"].ap().opt()], outs=[io["summall"].ap().opt()]),
       reads=[tk["summin"]], writes=[tk["summall"]], inc=1, q="cc")
    with ExitStack() as c3:
        sall = c3.enter_context(nc.sbuf_tensor("sall", [128, 64, SUMC], F32))
        t_sall = Trk()
        op(E, "sync", lambda sy: sy.dma_start(
            out=sall[:], in_=io["summall"].ap().rearrange("(g p) c -> p g c",
                                                          p=128)),
           reads=[tk["summall"]], writes=[t_sall])
        for osc in range(2):
            for kvh in range(KVH):
                for rk in range(16):
                    g64 = (rk // 2) * 8 + (rk % 2) * 4 + kvh
                    if rk == 0:
                        op(E, "vector", lambda v, osc=osc, kvh=kvh, g64=g64,
                           rk=rk: v.tensor_scalar(
                               inits[:, osc, kvh, :], sall[:, g64, 0:HD + 1],
                               smask_sb[:, osc, rk:rk + 1], None, ALU.mult),
                           reads=[t_sall, t_smask], writes=[t_inits])
                    else:
                        op(E, "vector", lambda v, osc=osc, kvh=kvh, g64=g64,
                           rk=rk: v.scalar_tensor_tensor(
                               inits[:, osc, kvh, :], sall[:, g64, 0:HD + 1],
                               smask_sb[:, osc, rk:rk + 1],
                               inits[:, osc, kvh, :], ALU.mult, ALU.add),
                           reads=[t_sall, t_smask], writes=[t_inits])

    E.join()
    with ExitStack() as c4:
        ee = c4.enter_context
        qf = ee(nc.sbuf_tensor("qf", [128, H, 512], BF16))
        attn1g = ee(nc.sbuf_tensor("attn1g", [128, H, 512], BF16))
        wq_sl = [ee(nc.sbuf_tensor(f"wq_sl{i}", [128, NC_I, 128], BF16))
                 for i in range(2)]
        wo_sl = ee(nc.sbuf_tensor("wo_sl", [128, H, 512], BF16))
        abuf = [ee(nc.sbuf_tensor(f"abuf{i}", [128, 128], BF16))
                for i in range(2)]
        at_t = ee(nc.sbuf_tensor("at_t", [128, 128], BF16))
        dcol = ee(nc.sbuf_tensor("dcol", [128, 1], F32))
        x2t4 = ee(nc.sbuf_tensor("x2tC4", [128, D], F32))
        x3t = [ee(nc.sbuf_tensor(f"x3t{i}", [128, D], F32)) for i in range(2)]
        rr = mk_rope(c4)
        lns = mk_ln(nc, c4, "C")
        t_qf = Trk(); t_a1g = Trk(); t_wq = [Trk(), Trk()]; t_wo = Trk()
        t_ab = [Trk(), Trk()]; t_att = Trk(); t_dcol = Trk()
        t_x2t4 = Trk(); t_x3t = [Trk(), Trk()]

        for g in range(4):
            gcols = slice(g * 512, (g + 1) * 512)
            op(E, "sync", lambda sy, gcols=gcols, rr=rr: sy.dma_start(
                out=rr["cosx"][:], in_=io["cos1"].ap()[:, gcols]),
               writes=[rr["t_cos"]])
            op(E, "sync", lambda sy, gcols=gcols, rr=rr: sy.dma_start(
                out=rr["sinx"][:], in_=io["sin1"].ap()[:, gcols]),
               writes=[rr["t_sin"]])
            for h in range(H):
                hb = h % 2
                op(E, "gpsimd", lambda g_, h=h, hb=hb: g_.dma_start(
                    out=wq_sl[hb][:],
                    in_=io["wqT1"].ap()[:, h * HD:(h + 1) * HD]
                        .rearrange("(c p) f -> p c f", p=128)),
                   writes=[t_wq[hb]])
                bk, tb = pb[4], tpb[4]
                mmg(E, [
                    (lambda t, ic=ic, bk=bk, gcols=gcols, hb=hb:
                     t.matmul(bk[:], wq_sl[hb][:, ic, :], x2fm[:, ic, gcols],
                              start=(ic == 0), stop=(ic == NC_I - 1)))
                    for ic in range(NC_I)],
                    reads=[t_wq[hb], t_x2fm], writes=[tb])
                rope_elu(rr, bk, tb, qf.ap()[:, h, :], t_qf)
            for tci in range(4):
                tc = g * 4 + tci
                icols = slice(tci * 128, (tci + 1) * 128)
                if tc % 8 == 0:
                    osc = tc // 8
                    for kvh in range(KVH):
                        op(E, "vector", lambda v, osc=osc, kvh=kvh:
                           v.tensor_copy(states[:, kvh, :],
                                         inits[:, osc, kvh, :]),
                           reads=[t_inits], writes=[t_st])
                    op(E, "vector", lambda v: v.tensor_copy(statebf[:],
                                                            states[:]),
                       reads=[t_st], writes=[t_stbf])
                for h in range(H):
                    kvh = h // (H // KVH)
                    bk, tb = pb[5], tpb[5]
                    mmg(E, [(lambda t, h=h, kvh=kvh, tc=tc, icols=icols, bk=bk:
                             t.matmul(bk[:, 0:128],
                                      kffm[:, kvh, tc * 128:(tc + 1) * 128],
                                      qf[:, h, icols], start=True, stop=True))],
                        reads=[t_kffm, t_qf], writes=[tb])
                    ab, t_abx = abuf[h % 2], t_ab[h % 2]
                    op(E, "vector", lambda v, bk=bk, ab=ab: v.tensor_mul(
                        ab[:], bk[:, 0:128], P["ltri"][:]),
                       reads=[tb, P["t_ltri"]], writes=[t_abx])
                    mmg(E, [
                        (lambda t, ab=ab, tc=tc, kvh=kvh:
                         t.matmul(pbm[:], ab[:], vaug1[:, tc, kvh, :],
                                  start=True, stop=False)),
                        (lambda t, h=h, kvh=kvh, icols=icols:
                         t.matmul(pbm[:], qf[:, h, icols], statebf[:, kvh, :],
                                  start=False, stop=True)),
                    ], reads=[t_abx, t_vaug, t_qf, t_stbf], writes=[t_pbm])
                    op(E, "vector", lambda v: v.tensor_scalar(
                        dcol[:], pbm[:, HD:HD + 1], L1_EPS, None, ALU.add),
                       reads=[t_pbm], writes=[t_dcol])
                    op(E, "vector", lambda v: v.reciprocal(dcol[:], dcol[:]),
                       reads=[t_dcol], writes=[t_dcol])
                    op(E, "vector", lambda v: v.tensor_scalar(
                        at_t[:], pbm[:, 0:HD], dcol[:], None, ALU.mult),
                       reads=[t_pbm, t_dcol], writes=[t_att])
                    mmg(E, [(lambda t: t.transpose(pbt[:], at_t[:],
                                                   P["ident"][:]))],
                        reads=[t_att, P["t_ident"]], writes=[t_pbt])
                    op(E, "vector", lambda v, h=h, icols=icols: v.tensor_copy(
                        attn1g[:, h, icols], pbt[:]),
                       reads=[t_pbt], writes=[t_a1g])
                for kvh in range(KVH):
                    ktm_make(tc, kvh)
                    mmg(E, [(lambda t, tc=tc, kvh=kvh: t.matmul(
                        pbm[:], ktm_t[:], vaug1[:, tc, kvh, :],
                        start=True, stop=True))],
                        reads=[t_ktm, t_vaug], writes=[t_pbm])
                    op(E, "vector", lambda v, kvh=kvh: v.tensor_add(
                        states[:, kvh, :], states[:, kvh, :], pbm[:]),
                       reads=[t_pbm], writes=[t_st])
                op(E, "vector", lambda v: v.tensor_copy(statebf[:], states[:]),
                   reads=[t_st], writes=[t_stbf])
            for pairi in range(2):
                for ft in range(4):
                    fcols = slice(ft * 512, (ft + 1) * 512)
                    op(E, "gpsimd", lambda g_, ft=ft: g_.dma_start(
                        out=wo_sl[:],
                        in_=io["woT1"].ap()[:, ft * 512:(ft + 1) * 512]
                            .rearrange("(c p) f -> p c f", p=128)),
                       writes=[t_wo])
                    for tin in range(2):
                        tci = pairi * 2 + tin
                        bk, tb = pb[tin], tpb[tin]
                        mmg(E, [
                            (lambda t, h=h, tci=tci, bk=bk:
                             t.matmul(bk[:],
                                      attn1g[:, h, tci * 128:(tci + 1) * 128],
                                      wo_sl[:, h, :], start=(h == 0),
                                      stop=(h == H - 1)))
                            for h in range(H)],
                            reads=[t_a1g, t_wo], writes=[tb])
                        op(E, "vector", lambda v, bk=bk, tin=tin, fcols=fcols:
                           v.tensor_copy(x3t[tin][:, fcols], bk[:]),
                           reads=[tb], writes=[t_x3t[tin]])
                for tin in range(2):
                    tci = pairi * 2 + tin
                    tc = g * 4 + tci
                    trows = slice(tc * 128, (tc + 1) * 128)
                    op(E, "sync", lambda sy, trows=trows: sy.dma_start(
                        out=x2t4[:], in_=io["x2s"].ap()[trows, :]),
                       reads=[tk["x2s"]], writes=[t_x2t4])
                    op(E, "vector", lambda v, tin=tin: v.tensor_add(
                        x3t[tin][:], x3t[tin][:], x2t4[:]),
                       reads=[t_x2t4], writes=[t_x3t[tin]])
                    ln_apply(E, lns, x3t[tin][:], t_x3t[tin], x2t4[:], t_x2t4, P)
                    op(E, "sync", lambda sy, trows=trows: sy.dma_start(
                        out=io["x3s"].ap()[trows, :], in_=x2t4[:]),
                       reads=[t_x2t4], writes=[tk["x3s"]])


def phase_D(E, nc, io, tk, P, ap):
    ecm = ap.enter_context
    pb = [ecm(nc.psum_tensor(f"psD{i}", [128, 512], F32)) for i in range(4)]
    tpb = [Trk(True) for _ in range(4)]
    pbt = ecm(nc.psum_tensor("psDt", [128, 128], BF16))
    t_pbt = Trk(True)

    w_sb = ecm(nc.sbuf_tensor("l1w_sb", [128, NC_I, D], BF16))
    scb = ecm(nc.sbuf_tensor("scb", [128, D], F32))
    bib = ecm(nc.sbuf_tensor("bib", [128, D], F32))
    srow = ecm(nc.sbuf_tensor("srow", [1, D], F32))
    x3fm = ecm(nc.sbuf_tensor("x3fmD", [128, NC_I, 128], BF16))
    x3t = ecm(nc.sbuf_tensor("x3tD", [128, D], F32))
    x3bf = ecm(nc.sbuf_tensor("x3bfD", [128, D], BF16))
    yt = ecm(nc.sbuf_tensor("ytD", [128, D], F32))
    yo = ecm(nc.sbuf_tensor("yoD", [128, D], F32))
    lns = mk_ln(nc, ap, "D")
    t_w = Trk(); t_scb = Trk(); t_bib = Trk(); t_srow = Trk()
    t_x3fm = Trk(); t_x3t = Trk(); t_x3bf = Trk(); t_yt = Trk(); t_yo = Trk()

    op(E, "gpsimd", lambda g: g.dma_start(
        out=w_sb[:], in_=io["l1wT"].ap().rearrange("(c p) f -> p c f", p=128)),
       writes=[t_w])
    for nm, tile, trk in (("l1scale", scb, t_scb), ("l1bias", bib, t_bib)):
        op(E, "sync", lambda sy, nm=nm: sy.dma_start(out=srow[:],
                                                     in_=io[nm][:, :]),
           writes=[t_srow])
        for ft in range(4):
            fcols = slice(ft * 512, (ft + 1) * 512)
            mmg(E, [(lambda t, fcols=fcols: t.matmul(
                pb[3][:, :], P["ones_r32"][:], srow[0:1, fcols],
                start=True, stop=True))],
                reads=[t_srow, P["t_or"]], writes=[tpb[3]])
            op(E, "vector", lambda v, tile=tile, fcols=fcols: v.tensor_copy(
                tile[:, fcols], pb[3][:, :]),
               reads=[tpb[3]], writes=[trk])

    for tc in range(16):
        trows = slice(tc * 128, (tc + 1) * 128)
        op(E, "sync", lambda sy, trows=trows: sy.dma_start(
            out=x3t[:], in_=io["x3s"].ap()[trows, :]),
           reads=[tk["x3s"]], writes=[t_x3t])
        op(E, "scalar", lambda s_: s_.activation(x3bf[:], x3t[:], AF.Copy,
                                                 bias=0.0, scale=1.0),
           reads=[t_x3t], writes=[t_x3bf])
        for ic in range(NC_I):
            mmg(E, [(lambda t, ic=ic: t.transpose(
                pbt[:], x3bf[:, ic * 128:(ic + 1) * 128], P["ident"][:]))],
                reads=[t_x3bf, P["t_ident"]], writes=[t_pbt])
            op(E, "vector", lambda v, ic=ic: v.tensor_copy(
                x3fm[:, ic, :], pbt[:]),
               reads=[t_pbt], writes=[t_x3fm])
        for ft in range(4):
            fcols = slice(ft * 512, (ft + 1) * 512)
            bk, tb = pb[ft % 2], tpb[ft % 2]
            mmg(E, [
                (lambda t, ic=ic, ft=ft, bk=bk:
                 t.matmul(bk[:], x3fm[:, ic, :],
                          w_sb[:, ic, ft * 512:(ft + 1) * 512],
                          start=(ic == 0), stop=(ic == NC_I - 1)))
                for ic in range(NC_I)],
                reads=[t_x3fm, t_w], writes=[tb])
            op(E, "vector", lambda v, bk=bk, fcols=fcols: v.tensor_mul(
                yt[:, fcols], bk[:], scb[:, fcols]),
               reads=[tb, t_scb], writes=[t_yt])
            op(E, "vector", lambda v, fcols=fcols: v.tensor_add(
                yt[:, fcols], yt[:, fcols], bib[:, fcols]),
               reads=[t_bib], writes=[t_yt])
        op(E, "vector", lambda v: v.tensor_add(yt[:], yt[:], x3t[:]),
           reads=[t_x3t], writes=[t_yt])
        ln_apply(E, lns, yt[:], t_yt, yo[:], t_yo, P)
        op(E, "sync", lambda sy, trows=trows: sy.dma_start(
            out=io["y"].ap()[trows, :], in_=yo[:]),
           reads=[t_yo], writes=[tk["y"]])


def _rope_tables_fm(dim, positions):
    inv = 1.0 / (10000.0 ** (np.arange(0, dim, 2, dtype=np.float64) / dim))
    ang = positions[None, :].astype(np.float64) * np.concatenate([inv, inv])[:, None]
    return (np.cos(ang).astype(np.float32), np.sin(ang).astype(np.float32))


def _prep_inputs(inputs):
    x0 = np.ascontiguousarray(inputs["L0_attn_hidden_states"], dtype=np.float32)
    shared = {}
    shared["wqaT"] = np.ascontiguousarray(inputs["L0_attn_w_qa"].T)
    shared["wkvaT"] = np.ascontiguousarray(inputs["L0_attn_w_kva"].T)
    shared["woT"] = np.ascontiguousarray(inputs["L0_attn_w_o"].T)
    shared["qaln"] = np.ascontiguousarray(
        inputs["L0_attn_q_a_ln_w"].reshape(NC_Q, 128).T)
    shared["kvln"] = np.ascontiguousarray(
        inputs["L0_attn_kv_a_ln_w"].reshape(NC_KV, 128).T)
    cos0, sin0 = _rope_tables_fm(DR, np.arange(S))
    shared["cos0"] = cos0
    shared["sin0"] = sin0
    shared["gwT"] = np.ascontiguousarray(inputs["L0_ffn_gate_w"].T)
    shared["w1"] = np.ascontiguousarray(inputs["L0_ffn_w1"])
    shared["w3"] = np.ascontiguousarray(inputs["L0_ffn_w3"])
    shared["w2"] = np.ascontiguousarray(inputs["L0_ffn_w2"])
    shared["sw1"] = np.ascontiguousarray(inputs["L0_ffn_sw1"])
    shared["sw3"] = np.ascontiguousarray(inputs["L0_ffn_sw3"])
    shared["sw2"] = np.ascontiguousarray(inputs["L0_ffn_sw2"])
    shared["wqT1"] = np.ascontiguousarray(inputs["L1_attn_Wq"].T)
    shared["wkT1"] = np.ascontiguousarray(inputs["L1_attn_Wk"].T)
    shared["wvT1"] = np.ascontiguousarray(inputs["L1_attn_Wv"].T)
    shared["woT1"] = np.ascontiguousarray(inputs["L1_attn_Wo"].T)
    shared["l1wT"] = np.ascontiguousarray(inputs["L1_ffn_weight"].T)
    shared["l1scale"] = np.ascontiguousarray(
        inputs["L1_ffn_scale_weight"].reshape(1, D))
    shared["l1bias"] = np.ascontiguousarray(inputs["L1_ffn_bias"].reshape(1, D))
    shared["ltri"] = np.triu(np.ones((128, 128), dtype=np.float32))
    shared["ident"] = np.eye(128, dtype=np.float32)
    wqbT = np.ascontiguousarray(inputs["L0_attn_w_qb"].T)
    wkvbT = np.ascontiguousarray(inputs["L0_attn_w_kvb"].T)

    in_maps = []
    for c in range(8):
        b, j = c // 2, c % 2
        s0 = TOWN * j
        m = dict(shared)
        m["xo_tm"] = np.ascontiguousarray(x0[b, s0:s0 + TOWN, :])
        m["xa_fm"] = np.ascontiguousarray(x0[b].T)
        m["wqbT_own"] = np.ascontiguousarray(
            wqbT[:, 8 * j * DQ:(8 * j + 8) * DQ])
        m["wkvbT_own"] = np.ascontiguousarray(
            wkvbT[:, 8 * j * 256:(8 * j + 8) * 256])
        m["ownoff"] = np.array([[s0]], dtype=np.int32)
        cos1, sin1 = _rope_tables_fm(HD, np.arange(s0, s0 + TOWN))
        m["cos1"] = cos1
        m["sin1"] = sin1
        sm = np.zeros((128, 2, 16), dtype=np.float32)
        for oi, osc in enumerate((2 * j, 2 * j + 1)):
            for rk in range(16):
                r, k = rk // 2, rk % 2
                rb_, jr = r // 2, r % 2
                gsc = 2 * jr + k
                if rb_ == b and gsc < osc:
                    sm[:, oi, rk] = 1.0
        m["smask"] = sm
        in_maps.append(m)
    return in_maps


_PHASE_INPUTS = {
    "A": ["xo_tm", "xa_fm", "wqaT", "wqbT_own", "wkvaT", "wkvbT_own", "woT",
          "qaln", "kvln", "cos0", "sin0", "ownoff"],
    "B": ["gwT", "w1", "w3", "w2", "sw1", "sw3", "sw2"],
    "C": ["wkT1", "wvT1", "wqT1", "woT1", "cos1", "sin1", "smask"],
    "D": ["l1wT", "l1scale", "l1bias"],
}
_PHASE_STREAM_IN = {"B": ["x1s"], "C": ["x2s"], "D": ["x3s"]}
_PHASE_STREAM_OUT = {"A": "x1s", "B": "x2s", "C": "x3s", "D": "y"}

_prog_cache = {}


def kernel(**inputs):
    in_maps = _prep_inputs(inputs)
    stream = [dict() for _ in range(8)]
    out = np.empty((B, S, D), dtype=np.float32)
    for phases in LAUNCH_PLAN:
        key = tuple(phases)
        if key not in _prog_cache:
            _prog_cache[key] = build_program(list(phases))
        nc = _prog_cache[key]
        names = ["ltri", "ident"]
        for ph in phases:
            names += _PHASE_INPUTS[ph]
        first = phases[0]
        sins = _PHASE_STREAM_IN.get(first, [])
        maps = []
        for c in range(8):
            mm = {nm: in_maps[c][nm] for nm in names}
            for snm in sins:
                mm[snm] = stream[c][snm]
            maps.append(mm)
        res = run_bass_kernel_spmd(nc, maps, core_ids=list(range(8)))
        last = phases[-1]
        onm = _PHASE_STREAM_OUT[last]
        for c in range(8):
            arr = res.results[c][onm]
            if onm == "y":
                b, j = c // 2, c % 2
                out[b, TOWN * j:TOWN * (j + 1), :] = arr
            else:
                stream[c][onm] = arr
    return out
